# version 11
# baseline (speedup 1.0000x reference)
import os as _osmod
import numpy as np
import ml_dtypes
import concourse.bass as bass
import concourse.mybir as mybir
from concourse.bass_utils import run_bass_kernel_spmd

F32 = mybir.dt.float32
BF16 = mybir.dt.bfloat16
AF = mybir.ActivationFunctionType
ALU = mybir.AluOpType
AX = mybir.AxisListType

NCORES = 8
S = 8192
D = 2048
TLOC = S // NCORES
NT = TLOC // 128
H = 16
NOPE, ROPE, VD = 128, 64, 128
QKD = NOPE + ROPE
QL, KVL = 768, 512
DFF = 8192
DC = D // 128
SCALE_A = QKD ** -0.5
SCALE_B = 128 ** -0.5
SHIFT = 18.0


class Prog:

    QUEUES = {
        "sd": ("sync", 16),
        "ad": ("scalar", 16),
        "gd": ("gpsimd", 16),
        "pe": ("tensor", 1),
        "act": ("scalar", 1),
        "dve": ("vector", 1),
        "pool": ("gpsimd", 1),
    }
    COMPUTE = ("pe", "act", "dve", "pool")

    def __init__(self, nc):
        self.nc = nc
        self.ops = []
        self.count = {q: 0 for q in self.COMPUTE}

    SERIAL = bool(_osmod.environ.get("SERIAL"))

    def emit(self, queue, fn, waits=()):
        assert queue in self.COMPUTE
        if self.SERIAL:
            waits = list(waits) + [(q, v) for q, v in self.count.items() if v]
        w = {}
        for t in waits:
            if t is None:
                continue
            q, v = t
            if q == queue == "pe":
                continue
            w[q] = max(w.get(q, 0), v)
        self.count[queue] += 1
        self.ops.append((queue, fn, w, 1, queue))
        return (queue, self.count[queue])

    def dma(self, queue, out, in_, waits=(), sem=None):
        assert queue in ("sd", "ad", "gd")
        assert sem is not None, "DMA needs a stream semaphore name"
        if self.SERIAL:
            waits = list(waits) + [(q, v) for q, v in self.count.items() if v]
        w = {}
        for t in waits:
            if t is None:
                continue
            q, v = t
            w[q] = max(w.get(q, 0), v)
        self.count[sem] = self.count.get(sem, 0) + 16
        self.ops.append((queue, lambda e, o=out, i=in_: e.dma_start(out=o, in_=i),
                         w, 16, sem))
        return (sem, self.count[sem])

    def full(self, sem):
        return (sem, self.count.get(sem, 0))

    def barrier(self):
        snap = [(q, v) for q, v in self.count.items() if v > 0]
        toks = []
        for q in self.COMPUTE:
            toks.append(self.emit(q, lambda e: e.nop(), waits=snap))
        return toks

    def build(self, block):
        import contextlib
        stack = contextlib.ExitStack()
        handles = {q: stack.enter_context(self.nc.semaphore(f"sem_{q}"))
                   for q in self.count}
        per_engine = {}
        for (queue, fn, w, inc, sem) in self.ops:
            eng = self.QUEUES[queue][0]
            per_engine.setdefault(eng, []).append((fn, w, inc, sem))

        def make_section(ops):
            def section(eng):
                waited = {q: 0 for q in self.count}
                for (fn, w, inc, sem) in ops:
                    for q, v in w.items():
                        if v > waited[q]:
                            eng.wait_ge(handles[q], v)
                            waited[q] = v
                    inst = fn(eng)
                    if isinstance(inst, (list, tuple)):
                        inst = inst[-1]
                    inst.then_inc(handles[sem], inc)
            return section

        for engname, ops in per_engine.items():
            getattr(block, engname)(make_section(ops))
        return stack


class Guard:

    def __init__(self):
        self.last = {}

    def use(self, name):
        return list(self.last.get(name, ()))

    def consumed(self, name, toks):
        self.last[name] = [t for t in toks if t is not None]


class Arena:

    def __init__(self, backing):
        self.b = backing
        self.cap = backing.shape[1]
        self.free_list = [(0, self.cap)]
        self.regions = {}
        self.hwm = 0

    def alloc(self, name, dt, elems):
        units = elems * (2 if dt == F32 else 1)
        units = (units + 1) & ~1
        for i, (a, b) in enumerate(self.free_list):
            if b - a >= units:
                self.free_list[i] = (a + units, b)
                if self.free_list[i][0] == self.free_list[i][1]:
                    self.free_list.pop(i)
                self.regions[name] = (dt, a, units)
                self.hwm = max(self.hwm, max(r[1] + r[2]
                                             for r in self.regions.values()))
                return self.get(name)
        raise MemoryError(f"arena: no room for {name} ({units} units); "
                          f"free={self.free_list}")

    def free(self, name):
        dt, a, n = self.regions.pop(name)
        fl = self.free_list
        fl.append((a, a + n))
        fl.sort()
        merged = []
        for seg in fl:
            if merged and merged[-1][1] == seg[0]:
                merged[-1] = (merged[-1][0], seg[1])
            else:
                merged.append(seg)
        self.free_list = merged

    def get(self, name, rows=128):
        dt, a, n = self.regions[name]
        v = self.b[0:rows, a:a + n]
        return v.bitcast(F32) if dt == F32 else v


def rmsnorm_from_psum(p, ps_ap, dst_ap, scr_ap, stat4, width, eps_ap, waits):
    s1 = p.emit("act", lambda e: e.activation(
        scr_ap, ps_ap, AF.Square, accum_out=stat4[:, 0:1]), waits=waits)
    s2 = p.emit("act", lambda e: e.activation(
        stat4[:, 1:2], stat4[:, 0:1], AF.Ln, bias=eps_ap, scale=1.0 / width),
        waits=[s1])
    s3 = p.emit("act", lambda e: e.activation(
        stat4[:, 1:2], stat4[:, 1:2], AF.Exp, scale=-0.5), waits=[s2])
    s4 = p.emit("dve", lambda e: e.tensor_scalar(
        dst_ap, ps_ap, stat4[:, 1:2], None, ALU.mult), waits=[s3])
    return s1, s4


def rope32(p, x1_ranges, x2_ranges, hi_bf, lo_bf, cos0, sin0, scrA, scrB,
           waits):
    readers, done = [], []
    for (x1, n0, n1), (x2, m0, m1) in zip(x1_ranges, x2_ranges):
        w = n1 - n0
        t1 = p.emit("dve", lambda e, x1=x1, n0=n0, n1=n1: e.tensor_mul(
            scrA[:, n0:n1], x1, cos0[:, n0:n1]), waits=waits)
        t2 = p.emit("dve", lambda e, x2=x2, n0=n0, n1=n1: e.tensor_mul(
            scrB[:, n0:n1], x2, sin0[:, n0:n1]), waits=waits)
        d1 = p.emit("dve", lambda e, n0=n0, n1=n1: e.tensor_sub(
            hi_bf[:, n0:n1], scrA[:, n0:n1], scrB[:, n0:n1]), waits=[t1, t2])
        t3 = p.emit("dve", lambda e, x1=x1, n0=n0, n1=n1: e.tensor_mul(
            scrA[:, n0:n1], x1, sin0[:, n0:n1]), waits=[d1])
        t4 = p.emit("dve", lambda e, x2=x2, n0=n0, n1=n1: e.tensor_mul(
            scrB[:, n0:n1], x2, cos0[:, n0:n1]), waits=[d1])
        d2 = p.emit("dve", lambda e, n0=n0, n1=n1: e.tensor_add(
            lo_bf[:, n0:n1], scrA[:, n0:n1], scrB[:, n0:n1]), waits=[t3, t4])
        readers += [t3, t4]
        done.append(d2)
    return readers, done


def build_L0():
    nc = bass.Bass("TRN2", target_bir_lowering=False, debug=False)
    ap = lambda name, shape, dt, kind: nc.dram_tensor(name, shape, dt, kind=kind).ap()

    x_in = ap("x", [TLOC, D], F32, "ExternalInput")
    w_dq = ap("w_dq", [D, QL], F32, "ExternalInput")
    w_uq = ap("w_uq", [QL, H * QKD], F32, "ExternalInput")
    g_q = ap("g_q", [128, QL // 128], F32, "ExternalInput")
    w_dkv = ap("w_dkv", [D, KVL], F32, "ExternalInput")
    g_kv = ap("g_kv", [128, KVL // 128], F32, "ExternalInput")
    w_uk = ap("w_uk", [KVL, H * NOPE], F32, "ExternalInput")
    w_uv = ap("w_uv", [KVL, H * VD], F32, "ExternalInput")
    w_kr = ap("w_kr", [D, ROPE], F32, "ExternalInput")
    cossin = ap("cossin", [ROPE, TLOC], F32, "ExternalInput")
    iden = ap("iden", [128, 128], BF16, "ExternalInput")
    consts = ap("consts", [128, 4], F32, "ExternalInput")

    import os as _os
    DBG = bool(_os.environ.get("L0DBG"))
    if DBG:
        cqT_o = ap("cqT_dbg", [QL // 128, 128, TLOC], BF16, "ExternalOutput")
        wuq_o = ap("wuq_dbg", [128, 6 * 3072], BF16, "ExternalOutput")
    qnopeT_o = ap("qnopeT", [H, NOPE, TLOC], BF16, "ExternalOutput")
    qpeT_o = ap("qpeT", [H // 2, 128, TLOC], BF16, "ExternalOutput")
    kAT_o = ap("kAT", [H, NOPE, TLOC], BF16, "ExternalOutput")
    vA_o = ap("vA", [TLOC, H * VD], BF16, "ExternalOutput")
    kpeT_o = ap("kpeT", [ROPE, TLOC], BF16, "ExternalOutput")
    knormA_o = ap("knormA", [H, TLOC], F32, "ExternalOutput")
    qmaxA_o = ap("qmaxA", [1, H], F32, "ExternalOutput")

    p = Prog(nc)
    G = Guard()

    with (
        nc.sbuf_tensor("bfA", [128, 94 * 1024], BF16) as bfA,
        nc.sbuf_tensor("stats", [128, 16], F32) as stats,
        nc.sbuf_tensor("idn", [128, 128], BF16) as idn,
        nc.sbuf_tensor("onescol", [128, 1], BF16) as onescol,
        nc.sbuf_tensor("qmax_sb", [1, H], F32) as qmax_sb,
        nc.sbuf_tensor("qtmp", [1, 2], F32) as qtmp,
        nc.sbuf_tensor("csts", [128, 4], F32) as csts,
        nc.sbuf_tensor("knrow", [1, TLOC], F32) as knrow,
        nc.psum_tensor("ps_a", [128, 512], F32) as ps_a,
        nc.psum_tensor("ps_b", [128, 512], F32) as ps_b,
        nc.psum_tensor("ps_w", [128, 1024], F32) as ps_w,
        nc.psum_tensor("ps_x", [128, 1024], F32) as ps_x,
        nc.psum_tensor("ps_t", [128, 128], BF16) as ps_t,
        nc.psum_tensor("ps_n", [1, 512], F32) as ps_n,
        nc.Block() as block,
    ):
        A = Arena(bfA)
        lnxT = A.alloc("lnxT", BF16, DC * TLOC)
        cqT = A.alloc("cqT", BF16, (QL // 128) * TLOC)
        ckvT = A.alloc("ckvT", BF16, (KVL // 128) * TLOC)
        kpe_hi = A.alloc("kpe_hi", BF16, TLOC)
        kpe_lo = A.alloc("kpe_lo", BF16, TLOC)
        cs1 = A.alloc("cos0", F32, TLOC)
        cs2 = A.alloc("sin0", F32, TLOC)
        xt = A.alloc("xt", F32, D)
        lnscr = A.alloc("lnscr", F32, D)
        ropescr = A.alloc("ropescr", F32, TLOC)
        ropescr2 = A.alloc("ropescr2", F32, TLOC)
        wraw = A.alloc("wraw", F32, 2 * 3072)
        lnx = A.alloc("lnx", BF16, D)
        wdq_bf = A.alloc("wdq", BF16, DC * QL)
        wdkv_bf = A.alloc("wdkv", BF16, DC * KVL)
        wkr_bf = A.alloc("wkr", BF16, DC * ROPE)

        p.dma("sd", idn[:], iden, sem="s0")
        p.dma("sd", csts[:], consts, sem="s0")
        cos0 = A.get("cos0", 32)
        sin0 = A.get("sin0", 32)
        p.dma("sd", cos0, cossin[0:32, :], sem="s0")
        p.dma("sd", sin0, cossin[32:64, :], sem="s0")
        t_ones = p.emit("dve", lambda e: e.memset(onescol[:], 1.0))
        p.dma("gd", wdq_bf.rearrange("p (c q) -> p c q", c=DC),
              w_dq.rearrange("(c p) q -> p c q", p=128), sem="gw")
        p.dma("gd", wdkv_bf.rearrange("p (c q) -> p c q", c=DC),
              w_dkv.rearrange("(c p) q -> p c q", p=128), sem="gw")
        p.dma("gd", wkr_bf.rearrange("p (c q) -> p c q", c=DC),
              w_kr.rearrange("(c p) q -> p c q", p=128), sem="gw")
        t_static = p.full("s0")
        t_gw = p.full("gw")
        t_idn = t_cs1 = t_cs2 = t_csts = t_static
        t_wdq = t_wdkv = t_wkr = t_gw

        t_prev = None
        for tt in range(NT):
            tl = p.dma("ad", xt, x_in[tt * 128:(tt + 1) * 128, :],
                       waits=[t_prev], sem="xl")
            sc = stats[:, 0:5]
            t1 = p.emit("dve", lambda e, tl=tl: e.scalar_tensor_tensor(
                lnscr, xt, 1.0, xt, ALU.mult, ALU.mult, accum_out=sc[:, 0:1]),
                waits=[tl])
            t2 = p.emit("dve", lambda e, tl=tl: e.reduce_sum(
                sc[:, 1:2], xt, axis=AX.X), waits=[tl])
            t3 = p.emit("dve", lambda e: e.tensor_scalar_mul(
                sc[:, 1:2], sc[:, 1:2], 1.0 / D), waits=[t2])
            t4 = p.emit("dve", lambda e: e.scalar_tensor_tensor(
                sc[:, 2:3], sc[:, 1:2], 1.0, sc[:, 1:2], ALU.mult, ALU.mult),
                waits=[t3])
            t5 = p.emit("dve", lambda e: e.tensor_scalar(
                sc[:, 0:1], sc[:, 0:1], 1.0 / D, None, ALU.mult), waits=[t1])
            t6 = p.emit("dve", lambda e: e.tensor_sub(
                sc[:, 2:3], sc[:, 0:1], sc[:, 2:3]), waits=[t4, t5])
            t7 = p.emit("act", lambda e: e.activation(
                sc[:, 3:4], sc[:, 2:3], AF.Ln, bias=csts[:, 0:1]),
                waits=[t6, t_csts])
            t8 = p.emit("act", lambda e: e.activation(
                sc[:, 4:5], sc[:, 3:4], AF.Exp, scale=-0.5), waits=[t7])
            t9 = p.emit("dve", lambda e: e.tensor_scalar(
                lnx, xt, sc[:, 1:2], sc[:, 4:5], ALU.subtract, ALU.mult),
                waits=[t8])
            t_prev = t9
            for c in range(DC):
                tp = p.emit("pe", lambda e, c=c: e.transpose(
                    ps_t[:], lnx[:, c * 128:(c + 1) * 128], idn[:]),
                    waits=[t9, t_idn] + G.use("ps_t"))
                ev = p.emit("dve", lambda e, c=c, tt=tt: e.tensor_copy(
                    lnxT[:, c * TLOC + tt * 128: c * TLOC + (tt + 1) * 128],
                    ps_t[:]), waits=[tp])
                G.consumed("ps_t", [ev])
        t_lnxT = ev

        cqtile = A.alloc("cqtile", BF16, QL)
        rms32 = A.alloc("rms32", F32, QL)

        def down_proj(dstT, w_sb, t_w, width, eps_ap):
            lastT = None
            for tt in range(NT):
                mm = None
                pre = G.use("ps_w")
                for c in range(DC):
                    for n0 in range(0, width, 512):
                        n1 = min(width, n0 + 512)
                        mm = p.emit("pe", lambda e, c=c, n0=n0, n1=n1, tt=tt: e.matmul(
                            ps_w[:, n0:n1],
                            lnxT[:, c * TLOC + tt * 128: c * TLOC + (tt + 1) * 128],
                            w_sb[:, c * width + n0: c * width + n1],
                            start=(c == 0), stop=(c == DC - 1)),
                            waits=[t_lnxT, t_w] + pre)
                s1, s4 = rmsnorm_from_psum(
                    p, ps_w[:, 0:width], cqtile[:, 0:width], rms32[:, 0:width],
                    stats[:, 10:12], width, eps_ap, [mm, t_csts] + G.use("cqtile"))
                G.consumed("ps_w", [s1, s4])
                for c in range(width // 128):
                    tp = p.emit("pe", lambda e, c=c: e.transpose(
                        ps_t[:], cqtile[:, c * 128:(c + 1) * 128], idn[:]),
                        waits=[s4] + G.use("ps_t"))
                    lastT = p.emit("dve", lambda e, c=c, tt=tt: e.tensor_copy(
                        dstT[:, c * TLOC + tt * 128: c * TLOC + (tt + 1) * 128],
                        ps_t[:]), waits=[tp])
                    G.consumed("ps_t", [lastT])
                G.consumed("cqtile", [lastT])
            return lastT

        t_cqT = down_proj(cqT, wdq_bf, t_wdq, QL, csts[:, 1:2])
        t_ckvT = down_proj(ckvT, wdkv_bf, t_wdkv, KVL, csts[:, 1:2])

        mmx1 = mmx2 = None
        pre = G.use("ps_x") + G.use("ps_a") + G.use("ps_b")
        for c in range(DC):
            for ns in range(2):
                mmx1 = p.emit("pe", lambda e, c=c, ns=ns: e.matmul(
                    ps_x[0:32, ns * 512:(ns + 1) * 512],
                    wkr_bf[:, c * ROPE:c * ROPE + 32],
                    lnxT[:, c * TLOC + ns * 512: c * TLOC + (ns + 1) * 512],
                    start=(c == 0), stop=(c == DC - 1)),
                    waits=[t_lnxT, t_wkr] + pre)
                mmx2 = p.emit("pe", lambda e, c=c, ns=ns: e.matmul(
                    (ps_a if ns == 0 else ps_b)[0:32, 0:512],
                    wkr_bf[:, c * ROPE + 32:(c + 1) * ROPE],
                    lnxT[:, c * TLOC + ns * 512: c * TLOC + (ns + 1) * 512],
                    start=(c == 0), stop=(c == DC - 1)),
                    waits=[t_lnxT, t_wkr] + pre)
        rsA = A.get("ropescr", 32)
        rsB = A.get("ropescr2", 32)
        kpe_hi_ap = A.get("kpe_hi", 32)
        kpe_lo_ap = A.get("kpe_lo", 32)
        x1r = [(ps_x[0:32, 0:512], 0, 512), (ps_x[0:32, 512:1024], 512, 1024)]
        x2r = [(ps_a[0:32, 0:512], 0, 512), (ps_b[0:32, 0:512], 512, 1024)]
        readers, dones = rope32(p, x1r, x2r, kpe_hi_ap, kpe_lo_ap, cos0, sin0,
                                rsA, rsB, [mmx1, mmx2, t_cs1, t_cs2])
        G.consumed("ps_x", readers)
        G.consumed("ps_a", readers)
        G.consumed("ps_b", readers)
        r_kpe = dones[-1]
        p.dma("sd", kpeT_o[0:32, :], kpe_hi_ap, waits=dones, sem="kp")
        p.dma("sd", kpeT_o[32:64, :], kpe_lo_ap, waits=dones, sem="kp")
        st_kpe = p.full("kp")

        bar = p.barrier()
        A.free("lnxT")
        A.free("wdq")
        A.free("wdkv")
        A.free("wkr")
        A.free("lnx")
        A.free("xt")
        A.free("lnscr")
        wuq_bf = A.alloc("wuq", BF16, 6 * 3072)
        wuk_bf = A.alloc("wuk", BF16, 4 * 2048)
        wuv_bf = A.alloc("wuv", BF16, 4 * 2048)
        gq_sb = A.alloc("gq", F32, 6)
        qhT = A.alloc("qhT", BF16, TLOC)
        khT = A.alloc("khT", BF16, TLOC)
        vat = A.alloc("vat", BF16, H * VD)
        sq = A.alloc("sq", BF16, TLOC)
        qpe_hi = A.alloc("qpe_hi", BF16, TLOC)
        qpe_lo = A.alloc("qpe_lo", BF16, TLOC)
        sqh = A.alloc("sqh", BF16, TLOC)
        sql = A.alloc("sql", BF16, TLOC)
        kpesq_hi = A.alloc("kpesq_hi", BF16, TLOC)
        kpesq_lo = A.alloc("kpesq_lo", BF16, TLOC)
        def b32(name):
            return A.get(name, 32)
        gkv_sb = A.alloc("gkv", F32, 4)
        gq_ap = A.get("gq")
        gkv_ap = A.get("gkv")
        p.dma("sd", gq_ap, g_q, waits=bar, sem="g2")
        p.dma("sd", gkv_ap, g_kv, waits=bar, sem="g2")
        t_gq = t_gkv = p.full("g2")

        def fold_weight(dst_sb, w_hbm, gcol, nchunk, width, t_g):
            tok = None
            for c in range(nchunk):
                par = c % 2
                buf = wraw[:, par * 3072:par * 3072 + width]
                tl = p.dma("sd", buf, w_hbm[c * 128:(c + 1) * 128, :],
                           waits=G.use(f"wraw{par}") + bar, sem=f"wr{par}")
                tok = p.emit("dve", lambda e, b=buf, cc=c: e.tensor_scalar(
                    dst_sb[:, cc * width:(cc + 1) * width], b, gcol[:, cc:cc + 1],
                    None, ALU.mult), waits=[tl, t_g])
                G.consumed(f"wraw{par}", [tok])
            return tok

        t_wuq = fold_weight(wuq_bf, w_uq, gq_ap, 6, 3072, t_gq)
        t_wuk = fold_weight(wuk_bf, w_uk, gkv_ap, 4, 2048, t_gkv)
        t_wuv = fold_weight(wuv_bf, w_uv, gkv_ap, 4, 2048, t_gkv)

        if DBG:
            p.dma("ad", cqT_o.rearrange("b p t -> p b t"),
                  cqT.rearrange("p (b t) -> p b t", b=QL // 128), waits=[t_cqT],
                  sem="dbg")
            p.dma("ad", wuq_o, wuq_bf, waits=[t_wuq], sem="dbg")
        QCH = QL // 128
        st_q_toks = []
        qm_toks = []
        st_qp_toks = []
        for h in range(H):
            mm = None
            pre = G.use("ps_w")
            for c in range(QCH):
                for ns in range(2):
                    mm = p.emit("pe", lambda e, h=h, c=c, ns=ns: e.matmul(
                        ps_w[:, ns * 512:(ns + 1) * 512],
                        wuq_bf[:, c * 3072 + h * QKD: c * 3072 + h * QKD + 128],
                        cqT[:, c * TLOC + ns * 512: c * TLOC + (ns + 1) * 512],
                        start=(c == 0), stop=(c == QCH - 1)),
                        waits=[t_cqT, t_wuq] + pre)
            ev = p.emit("act", lambda e: e.activation(qhT, ps_w[:], AF.Copy),
                        waits=[mm] + G.use("qhT"))
            G.consumed("ps_w", [ev])
            st_q = p.dma("sd", qnopeT_o[h], qhT, waits=[ev], sem="stq")
            st_q_toks.append(st_q)
            mmx1 = mmx2 = None
            prex = G.use("ps_x") + G.use("ps_a") + G.use("ps_b")
            for c in range(QCH):
                for ns in range(2):
                    mmx1 = p.emit("pe", lambda e, h=h, c=c, ns=ns: e.matmul(
                        ps_x[0:32, ns * 512:(ns + 1) * 512],
                        wuq_bf[:, c * 3072 + h * QKD + 128:
                               c * 3072 + h * QKD + 160],
                        cqT[:, c * TLOC + ns * 512: c * TLOC + (ns + 1) * 512],
                        start=(c == 0), stop=(c == QCH - 1)),
                        waits=[t_cqT, t_wuq] + prex)
                    mmx2 = p.emit("pe", lambda e, h=h, c=c, ns=ns: e.matmul(
                        (ps_a if ns == 0 else ps_b)[0:32, 0:512],
                        wuq_bf[:, c * 3072 + h * QKD + 160:
                               c * 3072 + h * QKD + 192],
                        cqT[:, c * TLOC + ns * 512: c * TLOC + (ns + 1) * 512],
                        start=(c == 0), stop=(c == QCH - 1)),
                        waits=[t_cqT, t_wuq] + prex)
            x1r = [(ps_x[0:32, 0:512], 0, 512), (ps_x[0:32, 512:1024], 512, 1024)]
            x2r = [(ps_a[0:32, 0:512], 0, 512), (ps_b[0:32, 0:512], 512, 1024)]
            readers, dones = rope32(
                p, x1r, x2r, b32("qpe_hi"), b32("qpe_lo"), cos0, sin0, rsA, rsB,
                [mmx1, mmx2, t_cs1, t_cs2] + G.use("qpe_hi") + G.use("qpe_lo"))
            G.consumed("ps_x", readers)
            G.consumed("ps_a", readers)
            G.consumed("ps_b", readers)
            base = (h % 2) * 64
            sp1 = p.dma("sd", qpeT_o[h // 2, base:base + 32, :], b32("qpe_hi"),
                        waits=dones, sem="sp1")
            sp2 = p.dma("sd", qpeT_o[h // 2, base + 32:base + 64, :],
                        b32("qpe_lo"), waits=dones, sem="sp2")
            st_qp_toks += [sp1, sp2]
            sqa = p.emit("dve", lambda e: e.tensor_mul(sq, qhT, qhT),
                         waits=[ev] + G.use("sqg"))
            sqb = p.emit("dve", lambda e: e.tensor_mul(
                b32("sqh"), b32("qpe_hi"), b32("qpe_hi")),
                waits=dones + G.use("sqhg"))
            sqc = p.emit("dve", lambda e: e.tensor_mul(
                b32("sql"), b32("qpe_lo"), b32("qpe_lo")),
                waits=dones + G.use("sqlg"))
            G.consumed("qpe_hi", [sp1, sqb])
            G.consumed("qpe_lo", [sp2, sqc])
            reds = []
            mmn = None
            for ns in range(2):
                pren = G.use("ps_n")
                p.emit("pe", lambda e, ns=ns: e.matmul(
                    ps_n[0:1, :], onescol[:],
                    sq[:, ns * 512:(ns + 1) * 512], start=True, stop=False),
                    waits=[sqa, t_ones] + pren)
                p.emit("pe", lambda e, ns=ns: e.matmul(
                    ps_n[0:1, :], onescol[0:32, :],
                    b32("sqh")[:, ns * 512:(ns + 1) * 512], start=False,
                    stop=False), waits=[sqb])
                mmn = p.emit("pe", lambda e, ns=ns: e.matmul(
                    ps_n[0:1, :], onescol[0:32, :],
                    b32("sql")[:, ns * 512:(ns + 1) * 512], start=False,
                    stop=True), waits=[sqc])
                red = p.emit("dve", lambda e, ns=ns: e.reduce_max(
                    qtmp[0:1, ns:ns + 1], ps_n[0:1, :], axis=AX.X), waits=[mmn])
                G.consumed("ps_n", [red])
                reds.append(red)
            red = p.emit("dve", lambda e, h=h: e.tensor_max(
                qmax_sb[0:1, h:h + 1], qtmp[0:1, 0:1], qtmp[0:1, 1:2]), waits=reds)
            G.consumed("sqg", [mmn])
            G.consumed("sqhg", [mmn])
            G.consumed("sqlg", [mmn])
            G.consumed("qhT", [st_q, sqa])
            qm_toks.append(red)
        st_qmax = p.dma("sd", qmaxA_o, qmax_sb[:], waits=[qm_toks[-1]], sem="qm")

        KCH = KVL // 128
        kq1 = p.emit("dve", lambda e: e.tensor_mul(
            b32("kpesq_hi"), kpe_hi_ap, kpe_hi_ap), waits=[r_kpe])
        kq2 = p.emit("dve", lambda e: e.tensor_mul(
            b32("kpesq_lo"), kpe_lo_ap, kpe_lo_ap), waits=[r_kpe])
        st_k_toks = []
        kn_toks = []
        for h in range(H):
            mm = None
            pre = G.use("ps_w")
            for c in range(KCH):
                for ns in range(2):
                    mm = p.emit("pe", lambda e, h=h, c=c, ns=ns: e.matmul(
                        ps_w[:, ns * 512:(ns + 1) * 512],
                        wuk_bf[:, c * 2048 + h * NOPE: c * 2048 + (h + 1) * NOPE],
                        ckvT[:, c * TLOC + ns * 512: c * TLOC + (ns + 1) * 512],
                        start=(c == 0), stop=(c == KCH - 1)),
                        waits=[t_ckvT, t_wuk] + pre)
            ev = p.emit("act", lambda e: e.activation(khT, ps_w[:], AF.Copy),
                        waits=[mm] + G.use("khT"))
            G.consumed("ps_w", [ev])
            st_k = p.dma("sd", kAT_o[h], khT, waits=[ev], sem="stk")
            st_k_toks.append(st_k)
            sqa = p.emit("dve", lambda e: e.tensor_mul(sq, khT, khT),
                         waits=[ev] + G.use("sqg"))
            mmn = None
            cps = []
            for ns in range(2):
                pren = G.use("ps_n")
                p.emit("pe", lambda e, ns=ns: e.matmul(
                    ps_n[0:1, :], onescol[:],
                    sq[:, ns * 512:(ns + 1) * 512], start=True, stop=False),
                    waits=[sqa, t_ones] + pren)
                p.emit("pe", lambda e, ns=ns: e.matmul(
                    ps_n[0:1, :], onescol[0:32, :],
                    b32("kpesq_hi")[:, ns * 512:(ns + 1) * 512], start=False,
                    stop=False), waits=[kq1])
                mmn = p.emit("pe", lambda e, ns=ns: e.matmul(
                    ps_n[0:1, :], onescol[0:32, :],
                    b32("kpesq_lo")[:, ns * 512:(ns + 1) * 512], start=False,
                    stop=True), waits=[kq2])
                cp = p.emit("dve", lambda e, ns=ns: e.tensor_copy(
                    knrow[0:1, ns * 512:(ns + 1) * 512], ps_n[0:1, :]),
                    waits=[mmn] + G.use("knrow"))
                G.consumed("ps_n", [cp])
                cps.append(cp)
            st_kn = p.dma("sd", knormA_o[h:h + 1, :], knrow[0:1, :], waits=cps, sem="skn")
            G.consumed("sqg", [mmn])
            G.consumed("khT", [st_k, sqa])
            G.consumed("knrow", [st_kn])
            kn_toks.append(st_kn)

        st_v = None
        for tt in range(NT):
            evs = []
            for ncol in range(4):
                psu, psb = (("ps_a", ps_a) if ncol % 2 == 0 else ("ps_b", ps_b))
                mm = None
                pre = G.use(psu)
                for c in range(KCH):
                    mm = p.emit("pe", lambda e, tt=tt, c=c, ncol=ncol, psb=psb:
                                e.matmul(
                        psb[:],
                        ckvT[:, c * TLOC + tt * 128: c * TLOC + (tt + 1) * 128],
                        wuv_bf[:, c * 2048 + ncol * 512: c * 2048 + (ncol + 1) * 512],
                        start=(c == 0), stop=(c == KCH - 1)),
                        waits=[t_ckvT, t_wuv, st_v] + pre)
                ev = p.emit("act", lambda e, ncol=ncol, psb=psb: e.activation(
                    vat[:, ncol * 512:(ncol + 1) * 512], psb[:], AF.Copy),
                    waits=[mm])
                G.consumed(psu, [ev])
                evs.append(ev)
            st_v = p.dma("ad", vA_o[tt * 128:(tt + 1) * 128, :], vat, waits=evs,
                         sem="stv")

        p.emit("pool", lambda e: e.nop(),
               waits=[p.full(x) for x in
                      ("stq", "sp1", "sp2", "qm", "stk", "skn", "stv", "kp")])

        p.build(block)
    return nc




def build_L1():
    nc = bass.Bass("TRN2", target_bir_lowering=False, debug=False)
    ap = lambda name, shape, dt, kind: nc.dram_tensor(name, shape, dt, kind=kind).ap()

    x_in = ap("x", [TLOC, D], F32, "ExternalInput")
    qn_i = ap("qnT", [H, 128, TLOC], BF16, "ExternalInput")
    qp_i = ap("qpT64", [64, H * TLOC], BF16, "ExternalInput")
    kT_i = ap("kT", [H, 128, S], BF16, "ExternalInput")
    v_i = ap("v", [S, H * 129], BF16, "ExternalInput")
    kpe_i = ap("kpeT", [64, S], BF16, "ExternalInput")
    knorm_i = ap("knorm", [H, S], F32, "ExternalInput")
    qmax_i = ap("qmax", [1, H], F32, "ExternalInput")
    mask_i = ap("mask", [128, 1024], BF16, "ExternalInput")
    iden = ap("iden", [128, 128], BF16, "ExternalInput")
    consts = ap("consts", [128, 4], F32, "ExternalInput")
    wo_i = ap("w_o", [D, D], F32, "ExternalInput")
    w1t_i = ap("w1t", [64, 128, 16, 128], F32, "ExternalInput")
    w2t_i = ap("w2t", [64, 128, D], F32, "ExternalInput")
    b1r_i = ap("b1r", [128, 64], F32, "ExternalInput")
    b2r_i = ap("b2r", [1, D], F32, "ExternalInput")
    wkT_i = ap("WkT", [D, D], F32, "ExternalInput")
    wvT_i = ap("WvT", [D, D], F32, "ExternalInput")

    x2_o = ap("x2", [TLOC, D], F32, "ExternalOutput")
    lnx2T_o = ap("lnx2T", [DC, 128, TLOC], BF16, "ExternalOutput")
    kBT_o = ap("kBT", [H, 128, TLOC], BF16, "ExternalOutput")
    vB_o = ap("vB", [TLOC, D], BF16, "ExternalOutput")
    w1q_o = ap("w1q", [64, 128, 16, 128], BF16, "ExternalOutput")
    w2q_o = ap("w2q", [64, 128, D], BF16, "ExternalOutput")

    L1DBG = bool(_osmod.environ.get("L1DBG"))
    if L1DBG:
        x1dbg_o = ap("x1dbg", [TLOC, D], F32, "ExternalOutput")
        callb_o = ap("callbdbg", [128, H], F32, "ExternalOutput")
        attT_o = ap("attTdbg", [DC, 128, TLOC], BF16, "ExternalOutput")
        acc_o = ap("accdbg", [128, NT * H * 129], F32, "ExternalOutput")
    x1_s = nc.dram_tensor("x1_s", [TLOC, D], F32, kind="Internal").ap()
    km_s = nc.dram_tensor("km_s", [H, 1], F32, kind="Internal").ap()

    p = Prog(nc)
    G = Guard()

    with (
        nc.sbuf_tensor("bfA", [128, 105600], BF16) as bfA,
        nc.sbuf_tensor("stats", [128, 36], F32) as stats,
        nc.sbuf_tensor("idn", [128, 128], BF16) as idn,
        nc.sbuf_tensor("csts", [128, 4], F32) as csts,
        nc.sbuf_tensor("onescol", [128, 1], BF16) as onescol,
        nc.sbuf_tensor("onesrow1", [1, 128], BF16) as onesrow1,
        nc.sbuf_tensor("taus", [128, 4], F32) as taus,
        nc.sbuf_tensor("callb", [128, H], F32) as callb,
        nc.sbuf_tensor("brow", [1, 160], F32) as brow,
        nc.sbuf_tensor("acc16", [128, 32], F32) as acc16,
        nc.psum_tensor("ps_s0", [128, 1024], F32) as ps_s0,
        nc.psum_tensor("ps_s1", [128, 1024], F32) as ps_s1,
        nc.psum_tensor("ps_a", [128, 512], F32) as ps_a,
        nc.psum_tensor("ps_b", [128, 512], F32) as ps_b,
        nc.psum_tensor("ps_t", [128, 128], BF16) as ps_t,
        nc.psum_tensor("ps_n", [1, 512], F32) as ps_n,
        nc.Block() as block,
    ):
        A = Arena(bfA)
        ps_scr = [ps_s0, ps_s1]
        ps_av = [ps_a, ps_b]

        def reg(name, rows=128):
            return A.get(name, rows)

        p.dma("sd", idn[:], iden, sem="s0")
        p.dma("sd", csts[:], consts, sem="s0")
        mask_sb = A.alloc("mask", BF16, 1024)
        p.dma("sd", mask_sb, mask_i, sem="s0")
        qmaxrow = A.alloc("qmaxrow", F32, 16)
        qmr = reg("qmaxrow", 1)
        p.dma("sd", qmr, qmax_i, sem="s0")
        t_static = p.full("s0")
        t_ones = p.emit("dve", lambda e: e.memset(onescol[:], 1.0))
        t_ones1 = p.emit("dve", lambda e: e.memset(onesrow1[:], 1.0))
        onesrowF = brow[0:1, 32:160]
        onescolF = stats[:, 32:33]
        t_onesF = p.emit("dve", lambda e: e.memset(onesrowF, 1.0))
        t_onescF = p.emit("dve", lambda e: e.memset(onescolF, 1.0))

        knorm_sb = A.alloc("knorm", F32, S)
        kn16 = reg("knorm", H)
        p.dma("ad", kn16, knorm_i, sem="kn")
        kmax16 = A.alloc("kmax16", F32, 1)
        km_col = reg("kmax16", H)
        t1 = p.emit("dve", lambda e: e.reduce_max(km_col, kn16, axis=AX.X),
                    waits=[p.full("kn")])
        p.dma("sd", km_s, km_col, waits=[t1], sem="km1")
        kmrow = A.alloc("kmrow", F32, 16)
        kmr = reg("kmrow", 1)
        p.dma("sd", kmr, km_s.rearrange("h o -> o h"),
              waits=[p.full("km1")], sem="km2")
        b1_ = p.emit("dve", lambda e: e.tensor_mul(brow[0:1, 0:16], qmr, kmr),
                     waits=[p.full("km2"), t_static])
        b2_ = p.emit("act", lambda e: e.activation(
            brow[0:1, 0:16], brow[0:1, 0:16], AF.Sqrt), waits=[b1_])
        b3_ = p.emit("dve", lambda e: e.tensor_scalar(
            brow[0:1, 0:16], brow[0:1, 0:16], -SCALE_A * 1.02, SHIFT, ALU.mult,
            ALU.add), waits=[b2_])
        bm_ = p.emit("pe", lambda e: e.matmul(
            ps_a[:, 0:16], onesrowF, brow[0:1, 0:16], start=True, stop=True),
            waits=[b3_, t_onesF])
        t_call = p.emit("dve", lambda e: e.tensor_copy(callb[:], ps_a[:, 0:16]),
                        waits=[bm_])
        G.consumed("av0", [t_call])
        A.free("knorm")
        A.free("kmax16")
        A.free("kmrow")
        A.free("qmaxrow")

        qn_sb = A.alloc("qn", BF16, H * TLOC)
        A.alloc("qp", BF16, H * TLOC)
        qp64 = reg("qp", 64)
        kbuf = [A.alloc(f"kb{i}", BF16, H * 512) for i in range(2)]
        for i in range(2):
            A.alloc(f"kpeb{i}", BF16, 512)
        qp128 = reg("qp", 128)
        t_qz = p.emit("dve", lambda e: e.memset(qp128[64:128, :], 0.0))
        kpe128b = [reg(f"kpeb{i}", 128) for i in range(2)]
        t_kpz = [p.emit("dve", lambda e, i=i: e.memset(
            kpe128b[i][64:128, :], 0.0)) for i in range(2)]
        vbuf = [A.alloc(f"vb{i}", BF16, H * 129) for i in range(4)]
        ptil = [A.alloc(f"pt{i}", BF16, 1024) for i in range(4)]
        acc = A.alloc("acc", F32, NT * H * 129)
        wr1b = [A.alloc(f"w1r{i}", F32, 1024) for i in range(3)]
        wqb = [A.alloc("wq0", BF16, 2048)]

        p.dma("ad", qn_sb.rearrange("p (h t) -> p h t", h=H),
              qn_i.rearrange("h p t -> p h t"), sem="q0")
        p.dma("ad", qp64, qp_i, sem="q0")
        t_q = p.full("q0")
        t_accz = p.emit("dve", lambda e: e.memset(acc, 0.0))

        def wchunk_half(w4, ci, hf, dst):
            if w4 is w2t_i:
                return dst, w4[ci][:, hf * 1024:(hf + 1) * 1024]
            return (dst.rearrange("p (c n) -> p c n", c=8),
                    w4[ci][:, hf * 8:(hf + 1) * 8, :])

        zt = p.emit("dve", lambda e: e.memset(acc16[:], 0.0))
        scale_state = {"tok": None}
        tau_tok = {"t": None}
        load_toks = {}
        pend_qc = {}
        q_w1 = []
        q_w2 = []
        NWB = 3

        jobs = []
        for w4, col in ((w1t_i, 0), (w2t_i, 16)):
            for ci in range(64):
                for hf in range(2):
                    jobs.append(("s", w4, ci, hf, col))
        for w4, wqo, tcol, ncol, sl in ((w1t_i, w1q_o, 0, 2, q_w1),
                                        (w2t_i, w2q_o, 1, 3, q_w2)):
            for ci in range(64):
                for hf in range(2):
                    jobs.append(("q", w4, ci, hf, (wqo, tcol, ncol, sl)))

        def job_load(u):
            kind, w4, ci, hf, _ = jobs[u]
            par = u % NWB
            dd, ss = wchunk_half(w4, ci, hf, wr1b[par])
            load_toks[u] = p.dma(("gd", "ad")[u % 2], dd, ss,
                                 waits=G.use(f"w1r{par}"), sem=f"sc{u % 6}")

        def emit_tau_chain():
            s_last = scale_state["tok"]
            r1_ = p.emit("dve", lambda e: e.reduce_sum(
                stats[:, 0:1], acc16[:, 0:8], axis=AX.X), waits=[s_last])
            r2_ = p.emit("dve", lambda e: e.reduce_sum(
                stats[:, 1:2], acc16[:, 16:24], axis=AX.X), waits=[s_last])
            m1_ = p.emit("pe", lambda e: e.matmul(
                ps_n[0:1, 0:2], onescolF, stats[:, 0:2], start=True, stop=True),
                waits=[r1_, r2_, t_onescF] + G.use("ps_n"))
            t1_ = p.emit("dve", lambda e: e.tensor_scalar(
                stats[0:1, 4:6], ps_n[0:1, 0:2], 0.7 / (DFF * D), 0.0, ALU.mult,
                ALU.add), waits=[m1_])
            G.consumed("ps_n", [t1_])
            bm2 = p.emit("pe", lambda e: e.matmul(
                ps_a[:, 0:2], onesrowF, stats[0:1, 4:6], start=True, stop=True),
                waits=[t1_, t_onesF] + G.use("av0"))
            t2_ = p.emit("dve", lambda e: e.tensor_copy(
                taus[:, 0:2], ps_a[:, 0:2]), waits=[bm2])
            G.consumed("av0", [t2_])
            return p.emit("dve", lambda e: e.tensor_scalar(
                taus[:, 2:4], taus[:, 0:2], -1.0, 0.0, ALU.mult, ALU.add),
                waits=[t2_])

        def job_compute(u):
            kind, w4, ci, hf, x = jobs[u]
            par = u % NWB
            buf = wr1b[par]
            tl = load_toks[u]
            if kind == "s":
                col = x
                rtk = p.emit("dve", lambda e, b=buf: e.reduce_sum(
                    stats[:, 16:24], b.rearrange("p (c n) -> p c n", c=8),
                    axis=AX.X, apply_absolute_value=True),
                    waits=[tl] + G.use("scr16"))
                tok = p.emit("dve", lambda e, col=col: e.tensor_add(
                    acc16[:, col:col + 8], acc16[:, col:col + 8],
                    stats[:, 16:24]), waits=[rtk, scale_state["tok"], zt])
                G.consumed("scr16", [tok])
                G.consumed(f"w1r{par}", [tok])
                scale_state["tok"] = tok
            else:
                wqo, tcol, ncol, sl = x
                half = wqb[0][:, hf * 1024:(hf + 1) * 1024]
                qb_ = p.emit("dve", lambda e, b=buf, half=half, ncol=ncol:
                             e.tensor_scalar(
                    half, b, taus[:, ncol:ncol + 1], None, ALU.is_le),
                    waits=[tl, tau_tok["t"]]
                    + (G.use("wq0") if hf == 0 else []))
                qc_ = p.emit("dve", lambda e, b=buf, half=half, tcol=tcol:
                             e.scalar_tensor_tensor(
                    half, b, taus[:, tcol:tcol + 1], half, ALU.is_ge,
                    ALU.subtract), waits=[qb_])
                G.consumed(f"w1r{par}", [qc_])
                pend_qc[hf] = qc_
                if hf == 1:
                    st = p.dma("sd", wqo[ci],
                               (wqb[0] if w4 is w2t_i else
                                wqb[0].rearrange("p (b n) -> p b n", b=16)),
                               waits=[pend_qc[0], pend_qc[1]], sem="wqs0")
                    G.consumed("wq0", [st])
                    sl.append(st)

        drain = {"u": 0}

        def drain_slots(n):
            for _ in range(n):
                u = drain["u"]
                if u >= len(jobs):
                    return
                job_compute(u)
                if u + NWB < len(jobs):
                    job_load(u + NWB)
                drain["u"] += 1
                if u == 255:
                    tau_tok["t"] = emit_tau_chain()

        for u0 in range(NWB):
            job_load(u0)

        merge_toks = {}
        vload = [None] * 4

        def scores_one(kb, h, kt, kpar, kpe64, tkl, tpl, jmin, Nq):
            spar = kt % 2
            scr = ps_scr[spar]
            pre = G.use(f"scr{spar}")
            mm = None
            for n0 in range(0, Nq, 512):
                n1 = min(Nq, n0 + 512)
                mm = p.emit("pe", lambda e, n0=n0, n1=n1: e.matmul(
                    scr[:, n0:n1],
                    kbuf[kpar][:, h * 512 + kt * 128: h * 512 + (kt + 1) * 128],
                    qn_sb[:, h * TLOC + 128 * jmin + n0:
                          h * TLOC + 128 * jmin + n1],
                    start=True, stop=False, skip_group_check=True),
                    waits=[tkl, t_q] + pre)
                mm = p.emit("pe", lambda e, n0=n0, n1=n1: e.matmul(
                    scr[:, n0:n1],
                    kpe128b[kpar][:, kt * 128:(kt + 1) * 128],
                    qp128[:, h * TLOC + 128 * jmin + n0:
                          h * TLOC + 128 * jmin + n1],
                    start=False, stop=False, skip_group_check=True),
                    waits=[tpl, t_qz, t_kpz[kpar]])
            mm = p.emit("pe", lambda e: e.matmul(
                scr[:, 0:128],
                mask_sb[:, (kb % 2) * 512 + kt * 128:
                        (kb % 2) * 512 + (kt + 1) * 128],
                idn[:], start=False, stop=True, skip_group_check=True),
                waits=[t_static])
            ex = p.emit("act", lambda e: e.activation(
                ptil[kt][:, 0:Nq], scr[:, 0:Nq], AF.Exp,
                bias=callb[:, h:h + 1], scale=SCALE_A),
                waits=[mm, t_call] + G.use(f"pt{kt}"))
            G.consumed(f"scr{spar}", [ex])
            return ex

        def av_one(j, h, jmin, exp_toks):
            apar = j % 2
            pav = ps_av[apar]
            prea = G.use(f"av{apar}")
            mm = None
            for kt in range(4):
                mm = p.emit("pe", lambda e, kt=kt: e.matmul(
                    pav[:, 0:129],
                    ptil[kt][:, (j - jmin) * 128:(j - jmin + 1) * 128],
                    vbuf[kt][:, h * 129:(h + 1) * 129],
                    start=(kt == 0), stop=(kt == 3)),
                    waits=[exp_toks[kt], vload[kt]] + prea)
            mg = p.emit("dve", lambda e: e.tensor_add(
                acc[:, (j * H + h) * 129:(j * H + h + 1) * 129],
                acc[:, (j * H + h) * 129:(j * H + h + 1) * 129],
                pav[:, 0:129]),
                waits=[mm, t_accz, merge_toks.get((j, h))])
            G.consumed(f"av{apar}", [mg])
            merge_toks[(j, h)] = mg
            return mm

        for kb in range(16):
            jmin = kb // 2
            Nq = TLOC - 128 * jmin
            kpar = kb % 2
            tkl = p.dma("sd", kbuf[kpar].rearrange("p (h n) -> p h n", h=H),
                        kT_i.rearrange("h p n -> p h n")
                        [:, :, 512 * kb:512 * (kb + 1)],
                        waits=G.use(f"kb{kpar}") + [t_accz], sem=f"kl{kpar}")
            kpe64 = reg(f"kpeb{kpar}", 64)
            tpl = p.dma("sd", kpe64, kpe_i[0:64, 512 * kb:512 * (kb + 1)],
                        waits=G.use(f"kpeb{kpar}"), sem=f"kpl{kpar}")
            for kt in range(4):
                vload[kt] = p.dma(
                    "sd", vbuf[kt],
                    v_i.rearrange("(a p) d -> p a d", p=128)[:, 4 * kb + kt, :],
                    waits=G.use(f"vb{kt}"), sem=f"vl{kt}")
            last_mm = None
            for h in range(H):
                exp_toks = [scores_one(kb, h, kt, kpar, kpe64, tkl, tpl,
                                       jmin, Nq) for kt in range(4)]
                for j in range(jmin, NT):
                    last_mm = av_one(j, h, jmin, exp_toks)
                for kt in range(4):
                    G.consumed(f"pt{kt}", [last_mm])
                drain_slots(2)
            for kt in range(4):
                G.consumed(f"vb{kt}", [last_mm])
            G.consumed(f"kb{kpar}", [last_mm])
            G.consumed(f"kpeb{kpar}", [last_mm])

        drain_slots(len(jobs))

        bar_f = p.barrier()
        A.free("qn")
        A.free("qp")
        for i in range(2):
            A.free(f"kb{i}")
            A.free(f"kpeb{i}")
        for i in range(4):
            A.free(f"vb{i}")
            A.free(f"pt{i}")
        attT = A.alloc("attT", BF16, DC * TLOC)
        att_bf = A.alloc("att_bf", BF16, D)
        linv = stats[:, 8:9]
        t_attT = None
        for tt in range(NT):
            sc_toks = []
            for h in range(H):
                base = (tt * H + h) * 129
                l1 = p.emit("dve", lambda e, base=base: e.tensor_scalar_max(
                    stats[:, 7:8], acc[:, base + 128:base + 129], 1e-30),
                    waits=[merge_toks[(tt, h)]] + G.use("linv") + bar_f)
                l2 = p.emit("dve", lambda e: e.reciprocal(linv, stats[:, 7:8]),
                            waits=[l1])
                sc = p.emit("dve", lambda e, base=base, h=h: e.tensor_scalar(
                    att_bf[:, h * 128:(h + 1) * 128], acc[:, base:base + 128],
                    linv, None, ALU.mult), waits=[l2] + G.use("attbf"))
                G.consumed("linv", [sc])
                sc_toks.append(sc)
            for c in range(DC):
                tp = p.emit("pe", lambda e, c=c: e.transpose(
                    ps_t[:], att_bf[:, c * 128:(c + 1) * 128], idn[:]),
                    waits=[sc_toks[-1], t_static] + G.use("ps_t"))
                t_attT = p.emit("dve", lambda e, c=c, tt=tt: e.tensor_copy(
                    attT[:, c * TLOC + tt * 128: c * TLOC + (tt + 1) * 128],
                    ps_t[:]), waits=[tp])
                G.consumed("ps_t", [t_attT])
            G.consumed("attbf", [t_attT])

        bar_w = p.barrier()
        A.free("acc")
        wo_res = A.alloc("wo_res", BF16, DC * D)
        for c in range(DC):
            p.dma("gd", wo_res[:, c * D:(c + 1) * D],
                  wo_i[c * 128:(c + 1) * 128, :], sem="wol", waits=bar_w)
        t_wo = p.full("wol")
        xres = A.alloc("xres", F32, D)
        lnscr = A.alloc("lnscr", F32, D)
        lnx1 = A.alloc("lnx1", BF16, D)
        lnx1T = A.alloc("lnx1T", BF16, DC * TLOC)
        x1t = A.alloc("x1t", F32, D)

        t_lnx1T = None
        for tt in range(NT):
            xl = p.dma("sd", xres, x_in[tt * 128:(tt + 1) * 128, :],
                       waits=G.use("xres") + bar_w, sem="xl")
            evs = []
            for ncol in range(4):
                apar = ncol % 2
                pav = ps_av[apar]
                prea = G.use(f"av{apar}")
                mm = None
                for c in range(DC):
                    mm = p.emit("pe", lambda e, tt=tt, c=c, ncol=ncol, pav=pav:
                                e.matmul(
                        pav[:, 0:512],
                        attT[:, c * TLOC + tt * 128: c * TLOC + (tt + 1) * 128],
                        wo_res[:, c * D + ncol * 512: c * D + (ncol + 1) * 512],
                        start=(c == 0), stop=(c == DC - 1)),
                        waits=[t_attT, t_wo] + prea)
                ev = p.emit("dve", lambda e, ncol=ncol, pav=pav:
                            e.scalar_tensor_tensor(
                    x1t[:, ncol * 512:(ncol + 1) * 512], pav[:, 0:512], 1.0,
                    xres[:, ncol * 512:(ncol + 1) * 512], ALU.mult, ALU.add),
                    waits=[mm, xl] + G.use("x1t")[0:0])
                G.consumed(f"av{apar}", [ev])
                evs.append(ev)
            G.consumed("xres", evs)
            st_x1 = p.dma("sd", x1_s[tt * 128:(tt + 1) * 128, :], x1t,
                          waits=evs, sem="sx1")
            sc = stats[:, 0:5]
            t1 = p.emit("act", lambda e: e.activation(
                lnscr, x1t, AF.Square, accum_out=sc[:, 0:1]), waits=evs)
            t2 = p.emit("dve", lambda e: e.reduce_sum(
                sc[:, 1:2], x1t, axis=AX.X), waits=evs)
            t3 = p.emit("dve", lambda e: e.tensor_scalar_mul(
                sc[:, 1:2], sc[:, 1:2], 1.0 / D), waits=[t2])
            t4 = p.emit("dve", lambda e: e.scalar_tensor_tensor(
                sc[:, 2:3], sc[:, 1:2], 1.0, sc[:, 1:2], ALU.mult, ALU.mult),
                waits=[t3])
            t5 = p.emit("dve", lambda e: e.tensor_scalar(
                sc[:, 0:1], sc[:, 0:1], 1.0 / D, None, ALU.mult), waits=[t1])
            t6 = p.emit("dve", lambda e: e.tensor_sub(
                sc[:, 2:3], sc[:, 0:1], sc[:, 2:3]), waits=[t4, t5])
            t7 = p.emit("act", lambda e: e.activation(
                sc[:, 3:4], sc[:, 2:3], AF.Ln, bias=csts[:, 0:1]),
                waits=[t6, t_static])
            t8 = p.emit("act", lambda e: e.activation(
                sc[:, 4:5], sc[:, 3:4], AF.Exp, scale=-0.5), waits=[t7])
            t9 = p.emit("dve", lambda e: e.tensor_scalar(
                lnx1, x1t, sc[:, 1:2], sc[:, 4:5], ALU.subtract, ALU.mult),
                waits=[t8] + G.use("lnx1"))
            G.consumed("x1t", [t9, st_x1, t1, t2])
            tpd = None
            for c in range(DC):
                tp = p.emit("pe", lambda e, c=c: e.transpose(
                    ps_t[:], lnx1[:, c * 128:(c + 1) * 128], idn[:]),
                    waits=[t9, t_static] + G.use("ps_t"))
                tpd = p.emit("dve", lambda e, c=c, tt=tt: e.tensor_copy(
                    lnx1T[:, c * TLOC + tt * 128: c * TLOC + (tt + 1) * 128],
                    ps_t[:]), waits=[tp])
                G.consumed("ps_t", [tpd])
            G.consumed("lnx1", [tpd])
            t_lnx1T = tpd

        if L1DBG:
            p.dma("ad", acc_o, acc,
                  waits=[merge_toks[(j, h)] for j in range(NT)
                         for h in range(H)], sem="dbg")
            p.dma("ad", x1dbg_o, x1_s, waits=[p.full("sx1")], sem="dbg")
            p.dma("ad", callb_o, callb[:], waits=[t_call], sem="dbg")
            p.dma("ad", attT_o.rearrange("c p t -> p c t"),
                  attT.rearrange("p (c t) -> p c t", c=DC), waits=[t_attT],
                  sem="dbg")
        bar2 = p.barrier()
        A.free("attT")
        A.free("att_bf")
        A.free("wo_res")
        A.free("mask")
        A.free("x1t")
        for i in range(2):
            A.free(f"w1r{i}")
        A.free("wq0")
        out2 = A.alloc("out2", F32, NT * D)
        hTb = A.alloc("hTb", BF16, 8 * TLOC)
        w2fb = A.alloc("w2fb", BF16, 8 * D)
        b1sb = A.alloc("b1sb", F32, 64)
        b1c = reg("b1sb")
        p.dma("sd", b1c, b1r_i, sem="b1", waits=bar2)
        t_b1 = p.full("b1")
        b2sb = A.alloc("b2sb", BF16, D)
        b2c = reg("b2sb", 1)
        p.dma("gd", b2c, b2r_i, sem="b2", waits=bar2)
        t_b2 = p.full("b2")
        w1qb = [A.alloc(f"w1qb{i}", BF16, 2048) for i in range(2)]
        t_o2z = p.emit("dve", lambda e: e.memset(out2, 0.0),
                       waits=bar2 + ([p.full("dbg")] if L1DBG else []))

        o2_toks = {}

        def ffnA_one(fb, fl):
            fc = fb * 8 + fl
            par = fc % 2
            tl = p.dma("sd", w1qb[par].rearrange("p (b n) -> p b n", b=16),
                       w1q_o[fc],
                       waits=G.use(f"w1qb{par}") + [q_w1[fc]] + bar2,
                       sem=f"w1l{par}")
            spar = fl % 2
            scr = ps_scr[spar]
            pre = G.use(f"scr{spar}")
            mm = None
            for dc2 in range(DC):
                for ns in range(2):
                    mm = p.emit("pe", lambda e, dc2=dc2, ns=ns: e.matmul(
                        scr[:, ns * 512:(ns + 1) * 512],
                        w1qb[par][:, dc2 * 128:(dc2 + 1) * 128],
                        lnx1T[:, dc2 * TLOC + ns * 512:
                              dc2 * TLOC + (ns + 1) * 512],
                        start=(dc2 == 0), stop=(dc2 == DC - 1)),
                        waits=[tl, t_lnx1T] + pre)
            G.consumed(f"w1qb{par}", [mm])
            ge = p.emit("act", lambda e: e.activation(
                hTb[:, fl * TLOC:(fl + 1) * TLOC], scr[:], AF.Gelu,
                bias=b1c[:, fc:fc + 1]), waits=[mm, t_b1] + G.use(f"hTb{fl}"))
            G.consumed(f"scr{spar}", [ge])
            return ge

        def ffnB_one(fb, tt, ncol, ge_toks, w2l_toks):
            apar = (tt * 4 + ncol) % 2
            pav = ps_av[apar]
            prea = G.use(f"av{apar}")
            mm = None
            for fl in range(8):
                mm = p.emit("pe", lambda e, fl=fl: e.matmul(
                    pav[:, 0:512],
                    hTb[:, fl * TLOC + tt * 128: fl * TLOC + (tt + 1) * 128],
                    w2fb[:, fl * D + ncol * 512: fl * D + (ncol + 1) * 512],
                    start=(fl == 0), stop=False, skip_group_check=True),
                    waits=[ge_toks[fl], w2l_toks[fl]] + prea)
            if fb == 0:
                mm = p.emit("pe", lambda e: e.matmul(
                    pav[:, 0:512], onesrow1[:],
                    b2c[:, ncol * 512:(ncol + 1) * 512],
                    start=False, stop=True, skip_group_check=True),
                    waits=[t_b2, t_ones1])
            mg = p.emit("dve", lambda e: e.tensor_add(
                out2[:, tt * D + ncol * 512: tt * D + (ncol + 1) * 512],
                out2[:, tt * D + ncol * 512: tt * D + (ncol + 1) * 512],
                pav[:, 0:512]),
                waits=[mm, t_o2z, o2_toks.get((tt, ncol))])
            G.consumed(f"av{apar}", [mg])
            o2_toks[(tt, ncol)] = mg
            return mm

        for fb in range(8):
            ge_toks = [ffnA_one(fb, fl) for fl in range(8)]
            w2l_toks = []
            for fl in range(8):
                fc = fb * 8 + fl
                tl = p.dma("gd", w2fb[:, fl * D:(fl + 1) * D], w2q_o[fc],
                           waits=G.use(f"w2fb{fl}") + [q_w2[fc]] + bar2,
                           sem="w2l")
                w2l_toks.append(tl)
            last = None
            for tt in range(NT):
                for ncol in range(4):
                    last = ffnB_one(fb, tt, ncol, ge_toks, w2l_toks)
            for fl in range(8):
                G.consumed(f"hTb{fl}", [last])
                G.consumed(f"w2fb{fl}", [last])

        bar3 = p.barrier()
        A.free("lnx1T")
        A.free("hTb")
        A.free("w2fb")
        A.free("b1sb")
        for i in range(2):
            A.free(f"w1qb{i}")
        x2t = A.alloc("x2t", F32, D)
        lnx2 = A.alloc("lnx2", BF16, D)
        lnx2T = A.alloc("lnx2T", BF16, DC * TLOC)
        t_lnx2T = None
        for tt in range(NT):
            xl = p.dma("sd", xres, x1_s[tt * 128:(tt + 1) * 128, :],
                       waits=G.use("xres") + [p.full("sx1")], sem="xl2")
            ev = p.emit("dve", lambda e, tt=tt: e.tensor_add(
                x2t, out2[:, tt * D:(tt + 1) * D], xres),
                waits=[xl, o2_toks[(tt, 0)], o2_toks[(tt, 1)],
                       o2_toks[(tt, 2)], o2_toks[(tt, 3)]] + G.use("x2t")
                + bar3)
            G.consumed("xres", [ev])
            st_x2 = p.dma("sd", x2_o[tt * 128:(tt + 1) * 128, :], x2t,
                          waits=[ev], sem="sx2")
            sc = stats[:, 0:5]
            t1 = p.emit("act", lambda e: e.activation(
                lnscr, x2t, AF.Square, accum_out=sc[:, 0:1]), waits=[ev])
            t2 = p.emit("dve", lambda e: e.reduce_sum(
                sc[:, 1:2], x2t, axis=AX.X), waits=[ev])
            t3 = p.emit("dve", lambda e: e.tensor_scalar_mul(
                sc[:, 1:2], sc[:, 1:2], 1.0 / D), waits=[t2])
            t4 = p.emit("dve", lambda e: e.scalar_tensor_tensor(
                sc[:, 2:3], sc[:, 1:2], 1.0, sc[:, 1:2], ALU.mult, ALU.mult),
                waits=[t3])
            t5 = p.emit("dve", lambda e: e.tensor_scalar(
                sc[:, 0:1], sc[:, 0:1], 1.0 / D, None, ALU.mult), waits=[t1])
            t6 = p.emit("dve", lambda e: e.tensor_sub(
                sc[:, 2:3], sc[:, 0:1], sc[:, 2:3]), waits=[t4, t5])
            t7 = p.emit("act", lambda e: e.activation(
                sc[:, 3:4], sc[:, 2:3], AF.Ln, bias=csts[:, 0:1]),
                waits=[t6, t_static])
            t8 = p.emit("act", lambda e: e.activation(
                sc[:, 4:5], sc[:, 3:4], AF.Exp, scale=-0.5), waits=[t7])
            t9 = p.emit("dve", lambda e: e.tensor_scalar(
                lnx2, x2t, sc[:, 1:2], sc[:, 4:5], ALU.subtract, ALU.mult),
                waits=[t8] + G.use("lnx2"))
            G.consumed("x2t", [t9, st_x2, t1, t2])
            tpd = None
            for c in range(DC):
                tp = p.emit("pe", lambda e, c=c: e.transpose(
                    ps_t[:], lnx2[:, c * 128:(c + 1) * 128], idn[:]),
                    waits=[t9, t_static] + G.use("ps_t"))
                tpd = p.emit("dve", lambda e, c=c, tt=tt: e.tensor_copy(
                    lnx2T[:, c * TLOC + tt * 128: c * TLOC + (tt + 1) * 128],
                    ps_t[:]), waits=[tp])
                G.consumed("ps_t", [tpd])
            G.consumed("lnx2", [tpd])
            t_lnx2T = tpd
        p.dma("sd", lnx2T_o.rearrange("c p t -> p c t"),
              lnx2T.rearrange("p (c t) -> p c t", c=DC),
              waits=[t_lnx2T], sem="slnxT")

        bar4 = p.barrier()
        A.free("out2")
        A.free("xres")
        A.free("lnscr")
        A.free("x2t")
        A.free("lnx2")
        A.free("b2sb")
        wbig = A.alloc("wbig", BF16, DC * D)
        for c in range(DC):
            p.dma("gd", wbig[:, c * D:(c + 1) * D],
                  wkT_i[c * 128:(c + 1) * 128, :], sem="wkl", waits=bar4)
        t_wk = p.full("wkl")
        ktile = A.alloc("ktile", F32, D)
        knb = A.alloc("knb", BF16, D)
        kbT = A.alloc("kbT", BF16, H * TLOC)
        hsum = stats[:, 16:32]
        t_kbT = None
        for tt in range(NT):
            evs = []
            for ncol in range(4):
                spar = ncol % 2
                scr = ps_scr[spar]
                pre = G.use(f"scr{spar}")
                mm = None
                for c in range(DC):
                    mm = p.emit("pe", lambda e, tt=tt, c=c, ncol=ncol, scr=scr:
                                e.matmul(
                        scr[:, 0:512],
                        lnx2T[:, c * TLOC + tt * 128: c * TLOC + (tt + 1) * 128],
                        wbig[:, c * D + ncol * 512: c * D + (ncol + 1) * 512],
                        start=(c == 0), stop=(c == DC - 1)),
                        waits=[t_lnx2T, t_wk] + pre)
                ev = p.emit("act", lambda e, ncol=ncol, scr=scr: e.activation(
                    ktile[:, ncol * 512:(ncol + 1) * 512], scr[:, 0:512],
                    AF.Copy), waits=[mm] + G.use("ktile")[0:0])
                G.consumed(f"scr{spar}", [ev])
                evs.append(ev)
            sq_toks = []
            for h in range(H):
                sq_toks.append(p.emit("act", lambda e, h=h: e.activation(
                    lnscr[:, 0:128], ktile[:, h * 128:(h + 1) * 128],
                    AF.Square, accum_out=hsum[:, h:h + 1]),
                    waits=evs + G.use("hsum")))
            rs1 = p.emit("act", lambda e: e.activation(
                stats[:, 0:16] if False else hsum, hsum, AF.Ln,
                bias=csts[:, 0:1], scale=1.0 / 128), waits=sq_toks)
            rs2 = p.emit("act", lambda e: e.activation(
                hsum, hsum, AF.Exp, scale=-0.5), waits=[rs1])
            nm_toks = []
            for h in range(H):
                nm_toks.append(p.emit("dve", lambda e, h=h: e.tensor_scalar(
                    knb[:, h * 128:(h + 1) * 128],
                    ktile[:, h * 128:(h + 1) * 128], hsum[:, h:h + 1], None,
                    ALU.mult), waits=[rs2] + G.use("knb")))
            G.consumed("hsum", nm_toks)
            G.consumed("ktile", nm_toks)
            for c in range(H):
                tp = p.emit("pe", lambda e, c=c: e.transpose(
                    ps_t[:], knb[:, c * 128:(c + 1) * 128], idn[:]),
                    waits=[nm_toks[-1], t_static] + G.use("ps_t"))
                t_kbT = p.emit("dve", lambda e, c=c, tt=tt: e.tensor_copy(
                    kbT[:, c * TLOC + tt * 128: c * TLOC + (tt + 1) * 128],
                    ps_t[:]), waits=[tp])
                G.consumed("ps_t", [t_kbT])
            G.consumed("knb", [t_kbT])
        p.dma("sd", kBT_o.rearrange("h p t -> p h t"),
              kbT.rearrange("p (h t) -> p h t", h=H), waits=[t_kbT], sem="skbt")

        for c in range(DC):
            p.dma("gd", wbig[:, c * D:(c + 1) * D],
                  wvT_i[c * 128:(c + 1) * 128, :],
                  waits=[t_kbT] if c < 2 else (), sem="wvl")
        t_wv = p.full("wvl")
        vtile = A.alloc("vtile", BF16, D)
        st_vb = None
        for tt in range(NT):
            evs = []
            for ncol in range(4):
                spar = ncol % 2
                scr = ps_scr[spar]
                pre = G.use(f"scr{spar}")
                mm = None
                for c in range(DC):
                    mm = p.emit("pe", lambda e, tt=tt, c=c, ncol=ncol, scr=scr:
                                e.matmul(
                        scr[:, 0:512],
                        lnx2T[:, c * TLOC + tt * 128: c * TLOC + (tt + 1) * 128],
                        wbig[:, c * D + ncol * 512: c * D + (ncol + 1) * 512],
                        start=(c == 0), stop=(c == DC - 1)),
                        waits=[t_lnx2T, t_wv] + pre)
                ev = p.emit("act", lambda e, ncol=ncol, scr=scr: e.activation(
                    vtile[:, ncol * 512:(ncol + 1) * 512], scr[:, 0:512],
                    AF.Copy), waits=[mm] + G.use("vtile")[0:0])
                G.consumed(f"scr{spar}", [ev])
                evs.append(ev)
            st_vb = p.dma("ad", vB_o[tt * 128:(tt + 1) * 128, :], vtile,
                          waits=evs + G.use("vtile"), sem="svb")
            G.consumed("vtile", [st_vb])

        p.emit("pool", lambda e: e.nop(),
               waits=[p.full(x) for x in
                      ("sx2", "slnxT", "skbt", "svb", "wqs0")])

        p.build(block)
    return nc




def build_L2():
    nc = bass.Bass("TRN2", target_bir_lowering=False, debug=False)
    ap = lambda name, shape, dt, kind: nc.dram_tensor(name, shape, dt, kind=kind).ap()

    x2_i = ap("x2", [TLOC, D], F32, "ExternalInput")
    lnx2T_i = ap("lnx2T", [DC, 128, TLOC], BF16, "ExternalInput")
    kT_i = ap("kT", [H, 128, S], BF16, "ExternalInput")
    v_i = ap("v", [S, H * 129], BF16, "ExternalInput")
    mask_i = ap("mask", [128, 1024], BF16, "ExternalInput")
    iden = ap("iden", [128, 128], BF16, "ExternalInput")
    consts = ap("consts", [128, 4], F32, "ExternalInput")
    wqT_i = ap("WqT", [D, D], F32, "ExternalInput")
    woT_i = ap("WoT", [D, D], F32, "ExternalInput")
    gq_i = ap("gq", [128, 1], F32, "ExternalInput")
    gk_i = ap("gk", [128, 1], F32, "ExternalInput")
    w1q_i = ap("w1q", [64, 128, 16, 128], BF16, "ExternalInput")
    w2q_i = ap("w2q", [64, 128, D], BF16, "ExternalInput")
    b1r_i = ap("b1r", [128, 64], F32, "ExternalInput")
    b2r_i = ap("b2r", [1, D], F32, "ExternalInput")

    xout_o = ap("xout", [TLOC, D], F32, "ExternalOutput")

    x3_s = nc.dram_tensor("x3_s", [TLOC, D], F32, kind="Internal").ap()

    p = Prog(nc)
    G = Guard()

    with (
        nc.sbuf_tensor("bfA", [128, 105600], BF16) as bfA,
        nc.sbuf_tensor("stats", [128, 36], F32) as stats,
        nc.sbuf_tensor("idn", [128, 128], BF16) as idn,
        nc.sbuf_tensor("csts", [128, 4], F32) as csts,
        nc.sbuf_tensor("onescol", [128, 1], BF16) as onescol,
        nc.sbuf_tensor("onesrow1", [1, 128], BF16) as onesrow1,
        nc.sbuf_tensor("g2col", [128, 1], F32) as g2col,
        nc.sbuf_tensor("callb", [128, H], F32) as callb,
        nc.sbuf_tensor("brow", [1, 160], F32) as brow,
        nc.psum_tensor("ps_s0", [128, 1024], F32) as ps_s0,
        nc.psum_tensor("ps_s1", [128, 1024], F32) as ps_s1,
        nc.psum_tensor("ps_a", [128, 512], F32) as ps_a,
        nc.psum_tensor("ps_b", [128, 512], F32) as ps_b,
        nc.psum_tensor("ps_t", [128, 128], BF16) as ps_t,
        nc.psum_tensor("ps_n", [1, 512], F32) as ps_n,
        nc.Block() as block,
    ):
        A = Arena(bfA)
        ps_scr = [ps_s0, ps_s1]
        ps_av = [ps_a, ps_b]

        p.dma("sd", idn[:], iden, sem="s0")
        p.dma("sd", csts[:], consts, sem="s0")
        mask_sb = A.alloc("mask", BF16, 1024)
        p.dma("sd", mask_sb, mask_i, sem="s0")
        p.dma("sd", stats[:, 24:25], gq_i, sem="s0")
        p.dma("sd", stats[:, 25:26], gk_i, sem="s0")
        t_static = p.full("s0")
        t_ones = p.emit("dve", lambda e: e.memset(onescol[:], 1.0))
        t_ones1 = p.emit("dve", lambda e: e.memset(onesrow1[:], 1.0))
        onesrowF = brow[0:1, 32:160]
        onescolF = stats[:, 32:33]
        t_onesF = p.emit("dve", lambda e: e.memset(onesrowF, 1.0))
        t_g2 = p.emit("dve", lambda e: e.tensor_mul(
            g2col[:], stats[:, 24:25], stats[:, 25:26]), waits=[t_static])

        qT = A.alloc("qT", BF16, H * TLOC)
        qn_ps = A.alloc("qn_ps", F32, 16)
        lnx2T = A.alloc("lnx2T", BF16, DC * TLOC)
        p.dma("ad", lnx2T.rearrange("p (c t) -> p c t", c=DC),
              lnx2T_i.rearrange("c p t -> p c t"), sem="lx")
        t_lnx2T = p.full("lx")
        wbig = A.alloc("wbig", BF16, DC * D)
        for c in range(DC):
            p.dma("gd", wbig[:, c * D:(c + 1) * D],
                  wqT_i[c * 128:(c + 1) * 128, :], sem="wql")
        t_wq = p.full("wql")
        qtile = A.alloc("qtile", F32, D)
        qnb = A.alloc("qnb", BF16, D)
        hsum = stats[:, 16:32][:, 0:16] if False else stats[:, 0:16]
        sq = A.alloc("sq", BF16, TLOC)
        t_qT = None
        qmax_toks = []
        for tt in range(NT):
            evs = []
            for ncol in range(4):
                spar = ncol % 2
                scr = ps_scr[spar]
                pre = G.use(f"scr{spar}")
                mm = None
                for c in range(DC):
                    mm = p.emit("pe", lambda e, tt=tt, c=c, ncol=ncol, scr=scr:
                                e.matmul(
                        scr[:, 0:512],
                        lnx2T[:, c * TLOC + tt * 128: c * TLOC + (tt + 1) * 128],
                        wbig[:, c * D + ncol * 512: c * D + (ncol + 1) * 512],
                        start=(c == 0), stop=(c == DC - 1)),
                        waits=[t_lnx2T, t_wq] + pre)
                ev = p.emit("act", lambda e, ncol=ncol, scr=scr: e.activation(
                    qtile[:, ncol * 512:(ncol + 1) * 512], scr[:, 0:512],
                    AF.Copy), waits=[mm] + G.use("qtile")[0:0])
                G.consumed(f"scr{spar}", [ev])
                evs.append(ev)
            sq_toks = []
            for h in range(H):
                sq_toks.append(p.emit("act", lambda e, h=h: e.activation(
                    qnb[:, 0:128], qtile[:, h * 128:(h + 1) * 128],
                    AF.Square, accum_out=hsum[:, h:h + 1]),
                    waits=evs + G.use("hsum") + G.use("qnb")))
            rs1 = p.emit("act", lambda e: e.activation(
                hsum, hsum, AF.Ln, bias=csts[:, 0:1], scale=1.0 / 128),
                waits=sq_toks + [t_static])
            rs2 = p.emit("act", lambda e: e.activation(
                hsum, hsum, AF.Exp, scale=-0.5), waits=[rs1])
            nm_toks = []
            for h in range(H):
                nm_toks.append(p.emit("dve", lambda e, h=h: e.tensor_scalar(
                    qnb[:, h * 128:(h + 1) * 128],
                    qtile[:, h * 128:(h + 1) * 128], hsum[:, h:h + 1], None,
                    ALU.mult), waits=[rs2] + G.use("qnb")))
            G.consumed("hsum", nm_toks)
            G.consumed("qtile", nm_toks)
            G.consumed("qnb", nm_toks)
            for h in range(H):
                tp = p.emit("pe", lambda e, h=h: e.transpose(
                    ps_t[:], qnb[:, h * 128:(h + 1) * 128], idn[:]),
                    waits=[nm_toks[h], t_static] + G.use("ps_t"))
                t_qT = p.emit("dve", lambda e, h=h, tt=tt: e.tensor_scalar(
                    qT[:, h * TLOC + tt * 128: h * TLOC + (tt + 1) * 128],
                    ps_t[:], g2col[:], None, ALU.mult), waits=[tp, t_g2])
                G.consumed("ps_t", [t_qT])
            G.consumed("qnb", [t_qT])
        qn_row = A.get("qn_ps", 1)
        for h in range(H):
            sqa = p.emit("dve", lambda e, h=h: e.tensor_mul(
                sq, qT[:, h * TLOC:(h + 1) * TLOC],
                qT[:, h * TLOC:(h + 1) * TLOC]), waits=[t_qT] + G.use("sqg"))
            reds = []
            mmn = None
            for ns in range(2):
                pren = G.use("ps_n")
                mmn = p.emit("pe", lambda e, ns=ns: e.matmul(
                    ps_n[0:1, :], onescol[:],
                    sq[:, ns * 512:(ns + 1) * 512], start=True, stop=True),
                    waits=[sqa, t_ones] + pren)
                red = p.emit("dve", lambda e, ns=ns: e.reduce_max(
                    stats[0:1, 20 + ns:21 + ns], ps_n[0:1, :], axis=AX.X),
                    waits=[mmn])
                G.consumed("ps_n", [red])
                reds.append(red)
            qmax_toks.append(p.emit("dve", lambda e, h=h: e.tensor_max(
                qn_row[0:1, h:h + 1], stats[0:1, 20:21], stats[0:1, 21:22]),
                waits=reds))
            G.consumed("sqg", [mmn])
        b1_ = p.emit("act", lambda e: e.activation(
            brow[0:1, 0:16], qn_row[0:1, 0:16], AF.Sqrt), waits=qmax_toks)
        b3_ = p.emit("dve", lambda e: e.tensor_scalar(
            brow[0:1, 0:16], brow[0:1, 0:16],
            -SCALE_B * 1.02 * (128.0 ** 0.5), SHIFT, ALU.mult, ALU.add),
            waits=[b1_])
        bm_ = p.emit("pe", lambda e: e.matmul(
            ps_a[:, 0:16], onesrowF, brow[0:1, 0:16], start=True, stop=True),
            waits=[b3_, t_onesF] + G.use("av0"))
        t_call = p.emit("dve", lambda e: e.tensor_copy(callb[:], ps_a[:, 0:16]),
                        waits=[bm_])
        G.consumed("av0", [t_call])

        bar_q = p.barrier()
        A.free("wbig")
        A.free("qtile")
        A.free("qnb")
        A.free("sq")
        A.free("lnx2T")
        kbuf = [A.alloc(f"kb{i}", BF16, H * 512) for i in range(2)]
        vbuf = [A.alloc(f"vb{i}", BF16, H * 129) for i in range(4)]
        ptil = [A.alloc(f"pt{i}", BF16, 1024) for i in range(4)]
        acc = A.alloc("acc", F32, NT * H * 129)
        t_accz = p.emit("dve", lambda e: e.memset(acc, 0.0), waits=bar_q)

        merge_toks = {}
        vload = [None] * 4

        def scores_one(kb, h, kt, kpar, tkl, jmin, Nq):
            spar = kt % 2
            scr = ps_scr[spar]
            pre = G.use(f"scr{spar}")
            mm = None
            for n0 in range(0, Nq, 512):
                n1 = min(Nq, n0 + 512)
                mm = p.emit("pe", lambda e, n0=n0, n1=n1: e.matmul(
                    scr[:, n0:n1],
                    kbuf[kpar][:, h * 512 + kt * 128: h * 512 + (kt + 1) * 128],
                    qT[:, h * TLOC + 128 * jmin + n0:
                       h * TLOC + 128 * jmin + n1],
                    start=True, stop=False, skip_group_check=True),
                    waits=[tkl, t_qT] + pre)
            mm = p.emit("pe", lambda e: e.matmul(
                scr[:, 0:128],
                mask_sb[:, (kb % 2) * 512 + kt * 128:
                        (kb % 2) * 512 + (kt + 1) * 128],
                idn[:], start=False, stop=True, skip_group_check=True),
                waits=[t_static])
            ex = p.emit("act", lambda e: e.activation(
                ptil[kt][:, 0:Nq], scr[:, 0:Nq], AF.Exp,
                bias=callb[:, h:h + 1], scale=SCALE_B),
                waits=[mm, t_call] + G.use(f"pt{kt}"))
            G.consumed(f"scr{spar}", [ex])
            return ex

        def av_one(j, h, jmin, exp_toks):
            apar = j % 2
            pav = ps_av[apar]
            prea = G.use(f"av{apar}")
            mm = None
            for kt in range(4):
                mm = p.emit("pe", lambda e, kt=kt: e.matmul(
                    pav[:, 0:129],
                    ptil[kt][:, (j - jmin) * 128:(j - jmin + 1) * 128],
                    vbuf[kt][:, h * 129:(h + 1) * 129],
                    start=(kt == 0), stop=(kt == 3)),
                    waits=[exp_toks[kt], vload[kt]] + prea)
            mg = p.emit("dve", lambda e: e.tensor_add(
                acc[:, (j * H + h) * 129:(j * H + h + 1) * 129],
                acc[:, (j * H + h) * 129:(j * H + h + 1) * 129],
                pav[:, 0:129]),
                waits=[mm, t_accz, merge_toks.get((j, h))])
            G.consumed(f"av{apar}", [mg])
            merge_toks[(j, h)] = mg
            return mm

        for kb in range(16):
            jmin = kb // 2
            Nq = TLOC - 128 * jmin
            kpar = kb % 2
            tkl = p.dma("sd", kbuf[kpar].rearrange("p (h n) -> p h n", h=H),
                        kT_i.rearrange("h p n -> p h n")
                        [:, :, 512 * kb:512 * (kb + 1)],
                        waits=G.use(f"kb{kpar}") + bar_q, sem=f"kl{kpar}")
            for kt in range(4):
                vload[kt] = p.dma(
                    "ad", vbuf[kt],
                    v_i.rearrange("(a p) d -> p a d", p=128)[:, 4 * kb + kt, :],
                    waits=G.use(f"vb{kt}") + bar_q, sem=f"vl{kt}")
            last_mm = None
            for h in range(H):
                exp_toks = [scores_one(kb, h, kt, kpar, tkl, jmin, Nq)
                            for kt in range(4)]
                for j in range(jmin, NT):
                    last_mm = av_one(j, h, jmin, exp_toks)
                for kt in range(4):
                    G.consumed(f"pt{kt}", [last_mm])
            for kt in range(4):
                G.consumed(f"vb{kt}", [last_mm])
            G.consumed(f"kb{kpar}", [last_mm])

        bar_f = p.barrier()
        for i in range(2):
            A.free(f"kb{i}")
        for i in range(4):
            A.free(f"vb{i}")
            A.free(f"pt{i}")
        A.free("qT")
        A.free("qn_ps")
        attT = A.alloc("attT", BF16, DC * TLOC)
        att_bf = A.alloc("att_bf", BF16, D)
        linv = stats[:, 8:9]
        t_attT = None
        for tt in range(NT):
            sc_toks = []
            for h in range(H):
                base = (tt * H + h) * 129
                l1 = p.emit("dve", lambda e, base=base: e.tensor_scalar_max(
                    stats[:, 7:8], acc[:, base + 128:base + 129], 1e-30),
                    waits=[merge_toks[(tt, h)]] + G.use("linv") + bar_f)
                l2 = p.emit("dve", lambda e: e.reciprocal(linv, stats[:, 7:8]),
                            waits=[l1])
                sc = p.emit("dve", lambda e, base=base, h=h: e.tensor_scalar(
                    att_bf[:, h * 128:(h + 1) * 128], acc[:, base:base + 128],
                    linv, None, ALU.mult), waits=[l2] + G.use("attbf"))
                G.consumed("linv", [sc])
                sc_toks.append(sc)
            for c in range(DC):
                tp = p.emit("pe", lambda e, c=c: e.transpose(
                    ps_t[:], att_bf[:, c * 128:(c + 1) * 128], idn[:]),
                    waits=[sc_toks[-1], t_static] + G.use("ps_t"))
                t_attT = p.emit("dve", lambda e, c=c, tt=tt: e.tensor_copy(
                    attT[:, c * TLOC + tt * 128: c * TLOC + (tt + 1) * 128],
                    ps_t[:]), waits=[tp])
                G.consumed("ps_t", [t_attT])
            G.consumed("attbf", [t_attT])

        bar_w = p.barrier()
        A.free("acc")
        wo_res = A.alloc("wo_res", BF16, DC * D)
        for c in range(DC):
            p.dma("gd", wo_res[:, c * D:(c + 1) * D],
                  woT_i[c * 128:(c + 1) * 128, :], sem="wol", waits=bar_w)
        t_wo = p.full("wol")
        xres = A.alloc("xres", F32, D)
        lnscr = A.alloc("lnscr", F32, D)
        lnx3 = A.alloc("lnx3", BF16, D)
        lnx3T = A.alloc("lnx3T", BF16, DC * TLOC)
        x1t = A.alloc("x1t", F32, D)

        t_lnx3T = None
        for tt in range(NT):
            xl = p.dma("sd", xres, x2_i[tt * 128:(tt + 1) * 128, :],
                       waits=G.use("xres") + bar_w, sem="xl")
            evs = []
            for ncol in range(4):
                apar = ncol % 2
                pav = ps_av[apar]
                prea = G.use(f"av{apar}")
                mm = None
                for c in range(DC):
                    mm = p.emit("pe", lambda e, tt=tt, c=c, ncol=ncol, pav=pav:
                                e.matmul(
                        pav[:, 0:512],
                        attT[:, c * TLOC + tt * 128: c * TLOC + (tt + 1) * 128],
                        wo_res[:, c * D + ncol * 512: c * D + (ncol + 1) * 512],
                        start=(c == 0), stop=(c == DC - 1)),
                        waits=[t_attT, t_wo] + prea)
                ev = p.emit("dve", lambda e, ncol=ncol, pav=pav:
                            e.scalar_tensor_tensor(
                    x1t[:, ncol * 512:(ncol + 1) * 512], pav[:, 0:512], 1.0,
                    xres[:, ncol * 512:(ncol + 1) * 512], ALU.mult, ALU.add),
                    waits=[mm, xl])
                G.consumed(f"av{apar}", [ev])
                evs.append(ev)
            G.consumed("xres", evs)
            st_x1 = p.dma("sd", x3_s[tt * 128:(tt + 1) * 128, :], x1t,
                          waits=evs, sem="sx1")
            sc = stats[:, 0:5]
            t1 = p.emit("act", lambda e: e.activation(
                lnscr, x1t, AF.Square, accum_out=sc[:, 0:1]), waits=evs)
            t2 = p.emit("dve", lambda e: e.reduce_sum(
                sc[:, 1:2], x1t, axis=AX.X), waits=evs)
            t3 = p.emit("dve", lambda e: e.tensor_scalar_mul(
                sc[:, 1:2], sc[:, 1:2], 1.0 / D), waits=[t2])
            t4 = p.emit("dve", lambda e: e.scalar_tensor_tensor(
                sc[:, 2:3], sc[:, 1:2], 1.0, sc[:, 1:2], ALU.mult, ALU.mult),
                waits=[t3])
            t5 = p.emit("dve", lambda e: e.tensor_scalar(
                sc[:, 0:1], sc[:, 0:1], 1.0 / D, None, ALU.mult), waits=[t1])
            t6 = p.emit("dve", lambda e: e.tensor_sub(
                sc[:, 2:3], sc[:, 0:1], sc[:, 2:3]), waits=[t4, t5])
            t7 = p.emit("act", lambda e: e.activation(
                sc[:, 3:4], sc[:, 2:3], AF.Ln, bias=csts[:, 0:1]),
                waits=[t6, t_static])
            t8 = p.emit("act", lambda e: e.activation(
                sc[:, 4:5], sc[:, 3:4], AF.Exp, scale=-0.5), waits=[t7])
            t9 = p.emit("dve", lambda e: e.tensor_scalar(
                lnx3, x1t, sc[:, 1:2], sc[:, 4:5], ALU.subtract, ALU.mult),
                waits=[t8] + G.use("lnx3"))
            G.consumed("x1t", [t9, st_x1, t1, t2])
            tpd = None
            for c in range(DC):
                tp = p.emit("pe", lambda e, c=c: e.transpose(
                    ps_t[:], lnx3[:, c * 128:(c + 1) * 128], idn[:]),
                    waits=[t9, t_static] + G.use("ps_t"))
                tpd = p.emit("dve", lambda e, c=c, tt=tt: e.tensor_copy(
                    lnx3T[:, c * TLOC + tt * 128: c * TLOC + (tt + 1) * 128],
                    ps_t[:]), waits=[tp])
                G.consumed("ps_t", [tpd])
            G.consumed("lnx3", [tpd])
            t_lnx3T = tpd

        bar2 = p.barrier()
        A.free("attT")
        A.free("att_bf")
        A.free("wo_res")
        A.free("mask")
        A.free("x1t")
        out2 = A.alloc("out2", F32, NT * D)
        hTb = A.alloc("hTb", BF16, 8 * TLOC)
        w2fb = A.alloc("w2fb", BF16, 8 * D)
        b1sb = A.alloc("b1sb", F32, 64)
        b1c = A.get("b1sb")
        p.dma("sd", b1c, b1r_i, sem="b1", waits=bar2)
        t_b1 = p.full("b1")
        b2sb = A.alloc("b2sb", BF16, D)
        b2c = A.get("b2sb", 1)
        p.dma("gd", b2c, b2r_i, sem="b2", waits=bar2)
        t_b2 = p.full("b2")
        w1qb = [A.alloc(f"w1qb{i}", BF16, 2048) for i in range(2)]
        t_o2z = p.emit("dve", lambda e: e.memset(out2, 0.0), waits=bar2)

        o2_toks = {}

        def ffnA_one(fb, fl):
            fc = fb * 8 + fl
            par = fc % 2
            tl = p.dma("sd", w1qb[par].rearrange("p (b n) -> p b n", b=16),
                       w1q_i[fc],
                       waits=G.use(f"w1qb{par}") + bar2, sem=f"w1l{par}")
            spar = fl % 2
            scr = ps_scr[spar]
            pre = G.use(f"scr{spar}")
            mm = None
            for dc2 in range(DC):
                for ns in range(2):
                    mm = p.emit("pe", lambda e, dc2=dc2, ns=ns: e.matmul(
                        scr[:, ns * 512:(ns + 1) * 512],
                        w1qb[par][:, dc2 * 128:(dc2 + 1) * 128],
                        lnx3T[:, dc2 * TLOC + ns * 512:
                              dc2 * TLOC + (ns + 1) * 512],
                        start=(dc2 == 0), stop=(dc2 == DC - 1)),
                        waits=[tl, t_lnx3T] + pre)
            G.consumed(f"w1qb{par}", [mm])
            ge = p.emit("act", lambda e: e.activation(
                hTb[:, fl * TLOC:(fl + 1) * TLOC], scr[:], AF.Gelu,
                bias=b1c[:, fc:fc + 1]), waits=[mm, t_b1] + G.use(f"hTb{fl}"))
            G.consumed(f"scr{spar}", [ge])
            return ge

        def ffnB_one(fb, tt, ncol, ge_toks, w2l_toks):
            apar = (tt * 4 + ncol) % 2
            pav = ps_av[apar]
            prea = G.use(f"av{apar}")
            mm = None
            for fl in range(8):
                mm = p.emit("pe", lambda e, fl=fl: e.matmul(
                    pav[:, 0:512],
                    hTb[:, fl * TLOC + tt * 128: fl * TLOC + (tt + 1) * 128],
                    w2fb[:, fl * D + ncol * 512: fl * D + (ncol + 1) * 512],
                    start=(fl == 0), stop=False, skip_group_check=True),
                    waits=[ge_toks[fl], w2l_toks[fl]] + prea)
            if fb == 0:
                mm = p.emit("pe", lambda e: e.matmul(
                    pav[:, 0:512], onesrow1[:],
                    b2c[:, ncol * 512:(ncol + 1) * 512],
                    start=False, stop=True, skip_group_check=True),
                    waits=[t_b2, t_ones1])
            mg = p.emit("dve", lambda e: e.tensor_add(
                out2[:, tt * D + ncol * 512: tt * D + (ncol + 1) * 512],
                out2[:, tt * D + ncol * 512: tt * D + (ncol + 1) * 512],
                pav[:, 0:512]),
                waits=[mm, t_o2z, o2_toks.get((tt, ncol))])
            G.consumed(f"av{apar}", [mg])
            o2_toks[(tt, ncol)] = mg
            return mm

        for fb in range(8):
            ge_toks = [ffnA_one(fb, fl) for fl in range(8)]
            w2l_toks = []
            for fl in range(8):
                fc = fb * 8 + fl
                tl = p.dma("gd", w2fb[:, fl * D:(fl + 1) * D], w2q_i[fc],
                           waits=G.use(f"w2fb{fl}") + bar2, sem="w2l")
                w2l_toks.append(tl)
            last = None
            for tt in range(NT):
                for ncol in range(4):
                    last = ffnB_one(fb, tt, ncol, ge_toks, w2l_toks)
            for fl in range(8):
                G.consumed(f"hTb{fl}", [last])
                G.consumed(f"w2fb{fl}", [last])

        st_xo = None
        for tt in range(NT):
            xl = p.dma("sd", xres, x3_s[tt * 128:(tt + 1) * 128, :],
                       waits=G.use("xres") + [p.full("sx1")], sem="xl2")
            ev = p.emit("dve", lambda e, tt=tt: e.tensor_add(
                lnscr, out2[:, tt * D:(tt + 1) * D], xres),
                waits=[xl, o2_toks[(tt, 0)], o2_toks[(tt, 1)],
                       o2_toks[(tt, 2)], o2_toks[(tt, 3)]] + G.use("lnscr"))
            G.consumed("xres", [ev])
            st_xo = p.dma("sd", xout_o[tt * 128:(tt + 1) * 128, :], lnscr,
                          waits=[ev], sem="sxo")
            G.consumed("lnscr", [st_xo])

        p.emit("pool", lambda e: e.nop(), waits=[p.full("sxo")])

        p.build(block)
    return nc


def _rope_tables():
    inv = (1.0 / (10000.0 ** (np.arange(0, ROPE, 2, dtype=np.float64) / ROPE))
           ).astype(np.float32)
    t = np.arange(S, dtype=np.float64)
    freqs = (t[:, None] * inv[None, :].astype(np.float64)).astype(np.float32)
    return (np.cos(freqs.astype(np.float64)).astype(np.float32),
            np.sin(freqs.astype(np.float64)).astype(np.float32))


def _prep_L0(inputs):
    x = np.asarray(inputs["L0_attn_a_x"], np.float32).reshape(S, D)
    cos, sin = _rope_tables()
    iden = np.eye(128, dtype=ml_dtypes.bfloat16)
    g_q = np.ascontiguousarray(
        np.asarray(inputs["L0_attn_a_g_q"], np.float32).reshape(QL // 128, 128).T)
    g_kv = np.ascontiguousarray(
        np.asarray(inputs["L0_attn_a_g_kv"], np.float32).reshape(KVL // 128, 128).T)
    shared = {
        "w_dq": np.asarray(inputs["L0_attn_a_w_dq"], np.float32),
        "w_uq": np.asarray(inputs["L0_attn_a_w_uq"], np.float32),
        "g_q": g_q,
        "w_dkv": np.asarray(inputs["L0_attn_a_w_dkv"], np.float32),
        "g_kv": g_kv,
        "w_uk": np.asarray(inputs["L0_attn_a_w_uk"], np.float32),
        "w_uv": np.asarray(inputs["L0_attn_a_w_uv"], np.float32),
        "w_kr": np.asarray(inputs["L0_attn_a_w_kr"], np.float32),
        "iden": iden,
        "consts": np.tile(np.array([[1e-5, 1e-6, 0.0, 0.0]], np.float32),
                          (128, 1)),
    }
    maps = []
    for c in range(NCORES):
        m = dict(shared)
        m["x"] = np.ascontiguousarray(x[c::NCORES])
        m["cossin"] = np.ascontiguousarray(
            np.concatenate([cos[c::NCORES].T, sin[c::NCORES].T], axis=0))
        maps.append(m)
    return maps




def _prep_L1(inputs, res0):
    H2, T = H // 2, TLOC
    bf = ml_dtypes.bfloat16
    kT_full = np.empty((H, 128, S), bf)
    v_full = np.ones((S, H, 129), bf)
    kpe_full = np.empty((64, S), bf)
    knorm_full = np.empty((H, S), np.float32)
    for c in range(NCORES):
        r = res0[c]
        kT_full[:, :, c::NCORES] = np.asarray(r["kAT"])
        v_full[c::NCORES, :, 0:128] = np.asarray(r["vA"]).reshape(TLOC, H, 128)
        kpe_full[:, c::NCORES] = np.asarray(r["kpeT"])
        knorm_full[:, c::NCORES] = np.asarray(r["knormA"])
    x = np.asarray(inputs["L0_attn_a_x"], np.float32).reshape(S, D)
    w1 = np.asarray(inputs["L0_ffn_w1"], np.float32)
    w2 = np.asarray(inputs["L0_ffn_w2"], np.float32)
    w1T = np.ascontiguousarray(w1.T)
    w1t = np.ascontiguousarray(
        w1T.reshape(16, 128, 64, 128).transpose(2, 1, 0, 3))
    w2t = np.ascontiguousarray(w2.T.reshape(64, 128, D))
    b1 = np.asarray(inputs["L0_ffn_b1"], np.float32)
    b1r = np.ascontiguousarray(b1.reshape(64, 128).T)
    b2r = np.asarray(inputs["L0_ffn_b2"], np.float32).reshape(1, D)
    shared = {
        "kT": kT_full, "v": v_full.reshape(S, H * 129), "kpeT": kpe_full, "knorm": knorm_full,
        "iden": np.eye(128, dtype=bf),
        "consts": np.tile(np.array([[1e-5, 1e-6, 0.0, 0.0]], np.float32),
                          (128, 1)),
        "w_o": np.asarray(inputs["L0_attn_a_w_o"], np.float32),
        "w1t": w1t, "w2t": w2t, "b1r": b1r, "b2r": b2r,
        "WkT": np.ascontiguousarray(
            np.asarray(inputs["L0_attn_b_Wk"], np.float32).T),
        "WvT": np.ascontiguousarray(
            np.asarray(inputs["L0_attn_b_Wv"], np.float32).T),
    }
    maps = []
    for c in range(NCORES):
        r = res0[c]
        qpe = np.asarray(r["qpeT"])
        qp64 = np.empty((64, H * T), bf)
        for h in range(H):
            qp64[:, h * T:(h + 1) * T] = qpe[h // 2, 64 * (h % 2):
                                             64 * (h % 2) + 64, :]
        rho = np.arange(128)[:, None]
        kap = np.arange(1024)[None, :]
        mask = np.where(kap <= 8 * rho + c, 0.0, -1e30).astype(bf)
        m = dict(shared)
        m["x"] = np.ascontiguousarray(x[c::NCORES])
        m["qnT"] = np.asarray(r["qnopeT"])
        m["qpT64"] = qp64
        m["qmax"] = np.asarray(r["qmaxA"])
        m["mask"] = mask
        maps.append(m)
    return maps




def _prep_L2(inputs, res1):
    bf = ml_dtypes.bfloat16
    kT_full = np.empty((H, 128, S), bf)
    v_full = np.ones((S, H, 129), bf)
    for c in range(NCORES):
        r = res1[c]
        kT_full[:, :, c::NCORES] = np.asarray(r["kBT"])
        v_full[c::NCORES, :, 0:128] = np.asarray(r["vB"]).reshape(TLOC, H, 128)
    shared = {
        "kT": kT_full, "v": v_full.reshape(S, H * 129),
        "iden": np.eye(128, dtype=bf),
        "consts": np.tile(np.array([[1e-5, 1e-6, 0.0, 0.0]], np.float32),
                          (128, 1)),
        "WqT": np.ascontiguousarray(
            np.asarray(inputs["L0_attn_b_Wq"], np.float32).T),
        "WoT": np.ascontiguousarray(
            np.asarray(inputs["L0_attn_b_Wo"], np.float32).T),
        "gq": np.asarray(inputs["L0_attn_b_gq"], np.float32).reshape(128, 1),
        "gk": np.asarray(inputs["L0_attn_b_gk"], np.float32).reshape(128, 1),
        "w1q": np.asarray(res1[0]["w1q"]),
        "w2q": np.asarray(res1[0]["w2q"]),
        "b1r": np.ascontiguousarray(
            np.asarray(inputs["L0_ffn_b1"], np.float32).reshape(64, 128).T),
        "b2r": np.asarray(inputs["L0_ffn_b2"], np.float32).reshape(1, D),
    }
    maps = []
    for c in range(NCORES):
        r = res1[c]
        rho = np.arange(128)[:, None]
        kap = np.arange(1024)[None, :]
        mask = np.where(kap <= 8 * rho + c, 0.0, -1e30).astype(bf)
        m = dict(shared)
        m["x2"] = np.asarray(r["x2"])
        m["lnx2T"] = np.asarray(r["lnx2T"])
        m["mask"] = mask
        maps.append(m)
    return maps


def kernel(**inputs):
    maps0 = _prep_L0(inputs)
    res0 = run_bass_kernel_spmd(build_L0(), maps0, core_ids=list(range(NCORES)))
    maps1 = _prep_L1(inputs, res0.results)
    res1 = run_bass_kernel_spmd(build_L1(), maps1, core_ids=list(range(NCORES)))
    maps2 = _prep_L2(inputs, res1.results)
    res2 = run_bass_kernel_spmd(build_L2(), maps2, core_ids=list(range(NCORES)))
    out = np.empty((S, D), np.float32)
    for c in range(NCORES):
        out[c::NCORES] = np.asarray(res2.results[c]["xout"], np.float32)
    return out.reshape(1, S, D)


# version 12
# speedup vs baseline: 1.0036x; 1.0036x over previous
import os as _osmod
import numpy as np
import ml_dtypes
import concourse.bass as bass
import concourse.mybir as mybir
from concourse.bass_utils import run_bass_kernel_spmd

F32 = mybir.dt.float32
BF16 = mybir.dt.bfloat16
AF = mybir.ActivationFunctionType
ALU = mybir.AluOpType
AX = mybir.AxisListType

NCORES = 8
S = 8192
D = 2048
TLOC = S // NCORES
NT = TLOC // 128
H = 16
NOPE, ROPE, VD = 128, 64, 128
QKD = NOPE + ROPE
QL, KVL = 768, 512
DFF = 8192
DC = D // 128
SCALE_A = QKD ** -0.5
SCALE_B = 128 ** -0.5
SHIFT = 18.0


class Prog:

    QUEUES = {
        "sd": ("sync", 16),
        "ad": ("scalar", 16),
        "gd": ("gpsimd", 16),
        "pe": ("tensor", 1),
        "act": ("scalar", 1),
        "dve": ("vector", 1),
        "pool": ("gpsimd", 1),
    }
    COMPUTE = ("pe", "act", "dve", "pool")

    def __init__(self, nc):
        self.nc = nc
        self.ops = []
        self.count = {q: 0 for q in self.COMPUTE}

    SERIAL = bool(_osmod.environ.get("SERIAL"))

    def emit(self, queue, fn, waits=()):
        assert queue in self.COMPUTE
        if self.SERIAL:
            waits = list(waits) + [(q, v) for q, v in self.count.items() if v]
        w = {}
        for t in waits:
            if t is None:
                continue
            q, v = t
            if q == queue == "pe":
                continue
            w[q] = max(w.get(q, 0), v)
        self.count[queue] += 1
        self.ops.append((queue, fn, w, 1, queue))
        return (queue, self.count[queue])

    def dma(self, queue, out, in_, waits=(), sem=None):
        assert queue in ("sd", "ad", "gd")
        assert sem is not None, "DMA needs a stream semaphore name"
        if self.SERIAL:
            waits = list(waits) + [(q, v) for q, v in self.count.items() if v]
        w = {}
        for t in waits:
            if t is None:
                continue
            q, v = t
            w[q] = max(w.get(q, 0), v)
        self.count[sem] = self.count.get(sem, 0) + 16
        self.ops.append((queue, lambda e, o=out, i=in_: e.dma_start(out=o, in_=i),
                         w, 16, sem))
        return (sem, self.count[sem])

    def full(self, sem):
        return (sem, self.count.get(sem, 0))

    def barrier(self):
        snap = [(q, v) for q, v in self.count.items() if v > 0]
        toks = []
        for q in self.COMPUTE:
            toks.append(self.emit(q, lambda e: e.nop(), waits=snap))
        return toks

    def build(self, block):
        import contextlib
        stack = contextlib.ExitStack()
        handles = {q: stack.enter_context(self.nc.semaphore(f"sem_{q}"))
                   for q in self.count}
        per_engine = {}
        for (queue, fn, w, inc, sem) in self.ops:
            eng = self.QUEUES[queue][0]
            per_engine.setdefault(eng, []).append((fn, w, inc, sem))

        def make_section(ops):
            def section(eng):
                waited = {q: 0 for q in self.count}
                for (fn, w, inc, sem) in ops:
                    for q, v in w.items():
                        if v > waited[q]:
                            eng.wait_ge(handles[q], v)
                            waited[q] = v
                    inst = fn(eng)
                    if isinstance(inst, (list, tuple)):
                        inst = inst[-1]
                    inst.then_inc(handles[sem], inc)
            return section

        for engname, ops in per_engine.items():
            getattr(block, engname)(make_section(ops))
        return stack


class Guard:

    def __init__(self):
        self.last = {}

    def use(self, name):
        return list(self.last.get(name, ()))

    def consumed(self, name, toks):
        self.last[name] = [t for t in toks if t is not None]


class Arena:

    def __init__(self, backing):
        self.b = backing
        self.cap = backing.shape[1]
        self.free_list = [(0, self.cap)]
        self.regions = {}
        self.hwm = 0

    def alloc(self, name, dt, elems):
        units = elems * (2 if dt == F32 else 1)
        units = (units + 1) & ~1
        for i, (a, b) in enumerate(self.free_list):
            if b - a >= units:
                self.free_list[i] = (a + units, b)
                if self.free_list[i][0] == self.free_list[i][1]:
                    self.free_list.pop(i)
                self.regions[name] = (dt, a, units)
                self.hwm = max(self.hwm, max(r[1] + r[2]
                                             for r in self.regions.values()))
                return self.get(name)
        raise MemoryError(f"arena: no room for {name} ({units} units); "
                          f"free={self.free_list}")

    def free(self, name):
        dt, a, n = self.regions.pop(name)
        fl = self.free_list
        fl.append((a, a + n))
        fl.sort()
        merged = []
        for seg in fl:
            if merged and merged[-1][1] == seg[0]:
                merged[-1] = (merged[-1][0], seg[1])
            else:
                merged.append(seg)
        self.free_list = merged

    def get(self, name, rows=128):
        dt, a, n = self.regions[name]
        v = self.b[0:rows, a:a + n]
        return v.bitcast(F32) if dt == F32 else v


def rmsnorm_from_psum(p, ps_ap, dst_ap, scr_ap, stat4, width, eps_ap, waits):
    s1 = p.emit("act", lambda e: e.activation(
        scr_ap, ps_ap, AF.Square, accum_out=stat4[:, 0:1]), waits=waits)
    s2 = p.emit("act", lambda e: e.activation(
        stat4[:, 1:2], stat4[:, 0:1], AF.Ln, bias=eps_ap, scale=1.0 / width),
        waits=[s1])
    s3 = p.emit("act", lambda e: e.activation(
        stat4[:, 1:2], stat4[:, 1:2], AF.Exp, scale=-0.5), waits=[s2])
    s4 = p.emit("dve", lambda e: e.tensor_scalar(
        dst_ap, ps_ap, stat4[:, 1:2], None, ALU.mult), waits=[s3])
    return s1, s4


def rope32(p, x1_ranges, x2_ranges, hi_bf, lo_bf, cos0, sin0, scrA, scrB,
           waits):
    readers, done = [], []
    for (x1, n0, n1), (x2, m0, m1) in zip(x1_ranges, x2_ranges):
        w = n1 - n0
        t1 = p.emit("dve", lambda e, x1=x1, n0=n0, n1=n1: e.tensor_mul(
            scrA[:, n0:n1], x1, cos0[:, n0:n1]), waits=waits)
        t2 = p.emit("dve", lambda e, x2=x2, n0=n0, n1=n1: e.tensor_mul(
            scrB[:, n0:n1], x2, sin0[:, n0:n1]), waits=waits)
        d1 = p.emit("dve", lambda e, n0=n0, n1=n1: e.tensor_sub(
            hi_bf[:, n0:n1], scrA[:, n0:n1], scrB[:, n0:n1]), waits=[t1, t2])
        t3 = p.emit("dve", lambda e, x1=x1, n0=n0, n1=n1: e.tensor_mul(
            scrA[:, n0:n1], x1, sin0[:, n0:n1]), waits=[d1])
        t4 = p.emit("dve", lambda e, x2=x2, n0=n0, n1=n1: e.tensor_mul(
            scrB[:, n0:n1], x2, cos0[:, n0:n1]), waits=[d1])
        d2 = p.emit("dve", lambda e, n0=n0, n1=n1: e.tensor_add(
            lo_bf[:, n0:n1], scrA[:, n0:n1], scrB[:, n0:n1]), waits=[t3, t4])
        readers += [t3, t4]
        done.append(d2)
    return readers, done


def build_L0():
    nc = bass.Bass("TRN2", target_bir_lowering=False, debug=False)
    ap = lambda name, shape, dt, kind: nc.dram_tensor(name, shape, dt, kind=kind).ap()

    x_in = ap("x", [TLOC, D], F32, "ExternalInput")
    w_dq = ap("w_dq", [D, QL], F32, "ExternalInput")
    w_uq = ap("w_uq", [QL, H * QKD], F32, "ExternalInput")
    g_q = ap("g_q", [128, QL // 128], F32, "ExternalInput")
    w_dkv = ap("w_dkv", [D, KVL], F32, "ExternalInput")
    g_kv = ap("g_kv", [128, KVL // 128], F32, "ExternalInput")
    w_uk = ap("w_uk", [KVL, H * NOPE], F32, "ExternalInput")
    w_uv = ap("w_uv", [KVL, H * VD], F32, "ExternalInput")
    w_kr = ap("w_kr", [D, ROPE], F32, "ExternalInput")
    cossin = ap("cossin", [ROPE, TLOC], F32, "ExternalInput")
    iden = ap("iden", [128, 128], BF16, "ExternalInput")
    consts = ap("consts", [128, 4], F32, "ExternalInput")

    import os as _os
    DBG = bool(_os.environ.get("L0DBG"))
    if DBG:
        cqT_o = ap("cqT_dbg", [QL // 128, 128, TLOC], BF16, "ExternalOutput")
        wuq_o = ap("wuq_dbg", [128, 6 * 3072], BF16, "ExternalOutput")
    qnopeT_o = ap("qnopeT", [H, NOPE, TLOC], BF16, "ExternalOutput")
    qpeT_o = ap("qpeT", [H // 2, 128, TLOC], BF16, "ExternalOutput")
    kAT_o = ap("kAT", [H, NOPE, TLOC], BF16, "ExternalOutput")
    vA_o = ap("vA", [TLOC, H * VD], BF16, "ExternalOutput")
    kpeT_o = ap("kpeT", [ROPE, TLOC], BF16, "ExternalOutput")
    knormA_o = ap("knormA", [H, TLOC], F32, "ExternalOutput")
    qmaxA_o = ap("qmaxA", [1, H], F32, "ExternalOutput")

    p = Prog(nc)
    G = Guard()

    with (
        nc.sbuf_tensor("bfA", [128, 94 * 1024], BF16) as bfA,
        nc.sbuf_tensor("stats", [128, 16], F32) as stats,
        nc.sbuf_tensor("idn", [128, 128], BF16) as idn,
        nc.sbuf_tensor("onescol", [128, 1], BF16) as onescol,
        nc.sbuf_tensor("qmax_sb", [1, H], F32) as qmax_sb,
        nc.sbuf_tensor("qtmp", [1, 2], F32) as qtmp,
        nc.sbuf_tensor("csts", [128, 4], F32) as csts,
        nc.sbuf_tensor("knrow", [1, TLOC], F32) as knrow,
        nc.psum_tensor("ps_a", [128, 512], F32) as ps_a,
        nc.psum_tensor("ps_b", [128, 512], F32) as ps_b,
        nc.psum_tensor("ps_w", [128, 1024], F32) as ps_w,
        nc.psum_tensor("ps_x", [128, 1024], F32) as ps_x,
        nc.psum_tensor("ps_t", [128, 128], BF16) as ps_t,
        nc.psum_tensor("ps_n", [1, 512], F32) as ps_n,
        nc.Block() as block,
    ):
        A = Arena(bfA)
        lnxT = A.alloc("lnxT", BF16, DC * TLOC)
        cqT = A.alloc("cqT", BF16, (QL // 128) * TLOC)
        ckvT = A.alloc("ckvT", BF16, (KVL // 128) * TLOC)
        kpe_hi = A.alloc("kpe_hi", BF16, TLOC)
        kpe_lo = A.alloc("kpe_lo", BF16, TLOC)
        cs1 = A.alloc("cos0", F32, TLOC)
        cs2 = A.alloc("sin0", F32, TLOC)
        xt = A.alloc("xt", F32, D)
        lnscr = A.alloc("lnscr", F32, D)
        ropescr = A.alloc("ropescr", F32, TLOC)
        ropescr2 = A.alloc("ropescr2", F32, TLOC)
        wraw = A.alloc("wraw", F32, 2 * 3072)
        lnx = A.alloc("lnx", BF16, D)
        wdq_bf = A.alloc("wdq", BF16, DC * QL)
        wdkv_bf = A.alloc("wdkv", BF16, DC * KVL)
        wkr_bf = A.alloc("wkr", BF16, DC * ROPE)

        p.dma("sd", idn[:], iden, sem="s0")
        p.dma("sd", csts[:], consts, sem="s0")
        cos0 = A.get("cos0", 32)
        sin0 = A.get("sin0", 32)
        p.dma("sd", cos0, cossin[0:32, :], sem="s0")
        p.dma("sd", sin0, cossin[32:64, :], sem="s0")
        t_ones = p.emit("dve", lambda e: e.memset(onescol[:], 1.0))
        p.dma("gd", wdq_bf.rearrange("p (c q) -> p c q", c=DC),
              w_dq.rearrange("(c p) q -> p c q", p=128), sem="gw")
        p.dma("gd", wdkv_bf.rearrange("p (c q) -> p c q", c=DC),
              w_dkv.rearrange("(c p) q -> p c q", p=128), sem="gw")
        p.dma("gd", wkr_bf.rearrange("p (c q) -> p c q", c=DC),
              w_kr.rearrange("(c p) q -> p c q", p=128), sem="gw")
        t_static = p.full("s0")
        t_gw = p.full("gw")
        t_idn = t_cs1 = t_cs2 = t_csts = t_static
        t_wdq = t_wdkv = t_wkr = t_gw

        t_prev = None
        for tt in range(NT):
            tl = p.dma("ad", xt, x_in[tt * 128:(tt + 1) * 128, :],
                       waits=[t_prev], sem="xl")
            sc = stats[:, 0:5]
            t1 = p.emit("dve", lambda e, tl=tl: e.scalar_tensor_tensor(
                lnscr, xt, 1.0, xt, ALU.mult, ALU.mult, accum_out=sc[:, 0:1]),
                waits=[tl])
            t2 = p.emit("dve", lambda e, tl=tl: e.reduce_sum(
                sc[:, 1:2], xt, axis=AX.X), waits=[tl])
            t3 = p.emit("dve", lambda e: e.tensor_scalar_mul(
                sc[:, 1:2], sc[:, 1:2], 1.0 / D), waits=[t2])
            t4 = p.emit("dve", lambda e: e.scalar_tensor_tensor(
                sc[:, 2:3], sc[:, 1:2], 1.0, sc[:, 1:2], ALU.mult, ALU.mult),
                waits=[t3])
            t5 = p.emit("dve", lambda e: e.tensor_scalar(
                sc[:, 0:1], sc[:, 0:1], 1.0 / D, None, ALU.mult), waits=[t1])
            t6 = p.emit("dve", lambda e: e.tensor_sub(
                sc[:, 2:3], sc[:, 0:1], sc[:, 2:3]), waits=[t4, t5])
            t7 = p.emit("act", lambda e: e.activation(
                sc[:, 3:4], sc[:, 2:3], AF.Ln, bias=csts[:, 0:1]),
                waits=[t6, t_csts])
            t8 = p.emit("act", lambda e: e.activation(
                sc[:, 4:5], sc[:, 3:4], AF.Exp, scale=-0.5), waits=[t7])
            t9 = p.emit("dve", lambda e: e.tensor_scalar(
                lnx, xt, sc[:, 1:2], sc[:, 4:5], ALU.subtract, ALU.mult),
                waits=[t8])
            t_prev = t9
            for c in range(DC):
                tp = p.emit("pe", lambda e, c=c: e.transpose(
                    ps_t[:], lnx[:, c * 128:(c + 1) * 128], idn[:]),
                    waits=[t9, t_idn] + G.use("ps_t"))
                ev = p.emit("dve", lambda e, c=c, tt=tt: e.tensor_copy(
                    lnxT[:, c * TLOC + tt * 128: c * TLOC + (tt + 1) * 128],
                    ps_t[:]), waits=[tp])
                G.consumed("ps_t", [ev])
        t_lnxT = ev

        cqtile = A.alloc("cqtile", BF16, QL)
        rms32 = A.alloc("rms32", F32, QL)

        def down_proj(dstT, w_sb, t_w, width, eps_ap):
            lastT = None
            for tt in range(NT):
                mm = None
                pre = G.use("ps_w")
                for c in range(DC):
                    for n0 in range(0, width, 512):
                        n1 = min(width, n0 + 512)
                        mm = p.emit("pe", lambda e, c=c, n0=n0, n1=n1, tt=tt: e.matmul(
                            ps_w[:, n0:n1],
                            lnxT[:, c * TLOC + tt * 128: c * TLOC + (tt + 1) * 128],
                            w_sb[:, c * width + n0: c * width + n1],
                            start=(c == 0), stop=(c == DC - 1)),
                            waits=[t_lnxT, t_w] + pre)
                s1, s4 = rmsnorm_from_psum(
                    p, ps_w[:, 0:width], cqtile[:, 0:width], rms32[:, 0:width],
                    stats[:, 10:12], width, eps_ap, [mm, t_csts] + G.use("cqtile"))
                G.consumed("ps_w", [s1, s4])
                for c in range(width // 128):
                    tp = p.emit("pe", lambda e, c=c: e.transpose(
                        ps_t[:], cqtile[:, c * 128:(c + 1) * 128], idn[:]),
                        waits=[s4] + G.use("ps_t"))
                    lastT = p.emit("dve", lambda e, c=c, tt=tt: e.tensor_copy(
                        dstT[:, c * TLOC + tt * 128: c * TLOC + (tt + 1) * 128],
                        ps_t[:]), waits=[tp])
                    G.consumed("ps_t", [lastT])
                G.consumed("cqtile", [lastT])
            return lastT

        t_cqT = down_proj(cqT, wdq_bf, t_wdq, QL, csts[:, 1:2])
        t_ckvT = down_proj(ckvT, wdkv_bf, t_wdkv, KVL, csts[:, 1:2])

        mmx1 = mmx2 = None
        pre = G.use("ps_x") + G.use("ps_a") + G.use("ps_b")
        for c in range(DC):
            for ns in range(2):
                mmx1 = p.emit("pe", lambda e, c=c, ns=ns: e.matmul(
                    ps_x[0:32, ns * 512:(ns + 1) * 512],
                    wkr_bf[:, c * ROPE:c * ROPE + 32],
                    lnxT[:, c * TLOC + ns * 512: c * TLOC + (ns + 1) * 512],
                    start=(c == 0), stop=(c == DC - 1)),
                    waits=[t_lnxT, t_wkr] + pre)
                mmx2 = p.emit("pe", lambda e, c=c, ns=ns: e.matmul(
                    (ps_a if ns == 0 else ps_b)[0:32, 0:512],
                    wkr_bf[:, c * ROPE + 32:(c + 1) * ROPE],
                    lnxT[:, c * TLOC + ns * 512: c * TLOC + (ns + 1) * 512],
                    start=(c == 0), stop=(c == DC - 1)),
                    waits=[t_lnxT, t_wkr] + pre)
        rsA = A.get("ropescr", 32)
        rsB = A.get("ropescr2", 32)
        kpe_hi_ap = A.get("kpe_hi", 32)
        kpe_lo_ap = A.get("kpe_lo", 32)
        x1r = [(ps_x[0:32, 0:512], 0, 512), (ps_x[0:32, 512:1024], 512, 1024)]
        x2r = [(ps_a[0:32, 0:512], 0, 512), (ps_b[0:32, 0:512], 512, 1024)]
        readers, dones = rope32(p, x1r, x2r, kpe_hi_ap, kpe_lo_ap, cos0, sin0,
                                rsA, rsB, [mmx1, mmx2, t_cs1, t_cs2])
        G.consumed("ps_x", readers)
        G.consumed("ps_a", readers)
        G.consumed("ps_b", readers)
        r_kpe = dones[-1]
        p.dma("sd", kpeT_o[0:32, :], kpe_hi_ap, waits=dones, sem="kp")
        p.dma("sd", kpeT_o[32:64, :], kpe_lo_ap, waits=dones, sem="kp")
        st_kpe = p.full("kp")

        bar = p.barrier()
        A.free("lnxT")
        A.free("wdq")
        A.free("wdkv")
        A.free("wkr")
        A.free("lnx")
        A.free("xt")
        A.free("lnscr")
        wuq_bf = A.alloc("wuq", BF16, 6 * 3072)
        wuk_bf = A.alloc("wuk", BF16, 4 * 2048)
        wuv_bf = A.alloc("wuv", BF16, 4 * 2048)
        gq_sb = A.alloc("gq", F32, 6)
        qhT = A.alloc("qhT", BF16, TLOC)
        khT = A.alloc("khT", BF16, TLOC)
        vat = A.alloc("vat", BF16, H * VD)
        sq = A.alloc("sq", BF16, TLOC)
        qpe_hi = A.alloc("qpe_hi", BF16, TLOC)
        qpe_lo = A.alloc("qpe_lo", BF16, TLOC)
        sqh = A.alloc("sqh", BF16, TLOC)
        sql = A.alloc("sql", BF16, TLOC)
        kpesq_hi = A.alloc("kpesq_hi", BF16, TLOC)
        kpesq_lo = A.alloc("kpesq_lo", BF16, TLOC)
        def b32(name):
            return A.get(name, 32)
        gkv_sb = A.alloc("gkv", F32, 4)
        gq_ap = A.get("gq")
        gkv_ap = A.get("gkv")
        p.dma("sd", gq_ap, g_q, waits=bar, sem="g2")
        p.dma("sd", gkv_ap, g_kv, waits=bar, sem="g2")
        t_gq = t_gkv = p.full("g2")

        def fold_weight(dst_sb, w_hbm, gcol, nchunk, width, t_g):
            tok = None
            for c in range(nchunk):
                par = c % 2
                buf = wraw[:, par * 3072:par * 3072 + width]
                tl = p.dma("sd", buf, w_hbm[c * 128:(c + 1) * 128, :],
                           waits=G.use(f"wraw{par}") + bar, sem=f"wr{par}")
                tok = p.emit("dve", lambda e, b=buf, cc=c: e.tensor_scalar(
                    dst_sb[:, cc * width:(cc + 1) * width], b, gcol[:, cc:cc + 1],
                    None, ALU.mult), waits=[tl, t_g])
                G.consumed(f"wraw{par}", [tok])
            return tok

        t_wuq = fold_weight(wuq_bf, w_uq, gq_ap, 6, 3072, t_gq)
        t_wuk = fold_weight(wuk_bf, w_uk, gkv_ap, 4, 2048, t_gkv)
        t_wuv = fold_weight(wuv_bf, w_uv, gkv_ap, 4, 2048, t_gkv)

        if DBG:
            p.dma("ad", cqT_o.rearrange("b p t -> p b t"),
                  cqT.rearrange("p (b t) -> p b t", b=QL // 128), waits=[t_cqT],
                  sem="dbg")
            p.dma("ad", wuq_o, wuq_bf, waits=[t_wuq], sem="dbg")
        QCH = QL // 128
        st_q_toks = []
        qm_toks = []
        st_qp_toks = []
        for h in range(H):
            mm = None
            pre = G.use("ps_w")
            for c in range(QCH):
                for ns in range(2):
                    mm = p.emit("pe", lambda e, h=h, c=c, ns=ns: e.matmul(
                        ps_w[:, ns * 512:(ns + 1) * 512],
                        wuq_bf[:, c * 3072 + h * QKD: c * 3072 + h * QKD + 128],
                        cqT[:, c * TLOC + ns * 512: c * TLOC + (ns + 1) * 512],
                        start=(c == 0), stop=(c == QCH - 1)),
                        waits=[t_cqT, t_wuq] + pre)
            ev = p.emit("act", lambda e: e.activation(qhT, ps_w[:], AF.Copy),
                        waits=[mm] + G.use("qhT"))
            G.consumed("ps_w", [ev])
            st_q = p.dma("sd", qnopeT_o[h], qhT, waits=[ev], sem="stq")
            st_q_toks.append(st_q)
            mmx1 = mmx2 = None
            prex = G.use("ps_x") + G.use("ps_a") + G.use("ps_b")
            for c in range(QCH):
                for ns in range(2):
                    mmx1 = p.emit("pe", lambda e, h=h, c=c, ns=ns: e.matmul(
                        ps_x[0:32, ns * 512:(ns + 1) * 512],
                        wuq_bf[:, c * 3072 + h * QKD + 128:
                               c * 3072 + h * QKD + 160],
                        cqT[:, c * TLOC + ns * 512: c * TLOC + (ns + 1) * 512],
                        start=(c == 0), stop=(c == QCH - 1)),
                        waits=[t_cqT, t_wuq] + prex)
                    mmx2 = p.emit("pe", lambda e, h=h, c=c, ns=ns: e.matmul(
                        (ps_a if ns == 0 else ps_b)[0:32, 0:512],
                        wuq_bf[:, c * 3072 + h * QKD + 160:
                               c * 3072 + h * QKD + 192],
                        cqT[:, c * TLOC + ns * 512: c * TLOC + (ns + 1) * 512],
                        start=(c == 0), stop=(c == QCH - 1)),
                        waits=[t_cqT, t_wuq] + prex)
            x1r = [(ps_x[0:32, 0:512], 0, 512), (ps_x[0:32, 512:1024], 512, 1024)]
            x2r = [(ps_a[0:32, 0:512], 0, 512), (ps_b[0:32, 0:512], 512, 1024)]
            readers, dones = rope32(
                p, x1r, x2r, b32("qpe_hi"), b32("qpe_lo"), cos0, sin0, rsA, rsB,
                [mmx1, mmx2, t_cs1, t_cs2] + G.use("qpe_hi") + G.use("qpe_lo"))
            G.consumed("ps_x", readers)
            G.consumed("ps_a", readers)
            G.consumed("ps_b", readers)
            base = (h % 2) * 64
            sp1 = p.dma("sd", qpeT_o[h // 2, base:base + 32, :], b32("qpe_hi"),
                        waits=dones, sem="sp1")
            sp2 = p.dma("sd", qpeT_o[h // 2, base + 32:base + 64, :],
                        b32("qpe_lo"), waits=dones, sem="sp2")
            st_qp_toks += [sp1, sp2]
            sqa = p.emit("dve", lambda e: e.tensor_mul(sq, qhT, qhT),
                         waits=[ev] + G.use("sqg"))
            sqb = p.emit("dve", lambda e: e.tensor_mul(
                b32("sqh"), b32("qpe_hi"), b32("qpe_hi")),
                waits=dones + G.use("sqhg"))
            sqc = p.emit("dve", lambda e: e.tensor_mul(
                b32("sql"), b32("qpe_lo"), b32("qpe_lo")),
                waits=dones + G.use("sqlg"))
            G.consumed("qpe_hi", [sp1, sqb])
            G.consumed("qpe_lo", [sp2, sqc])
            reds = []
            mmn = None
            for ns in range(2):
                pren = G.use("ps_n")
                p.emit("pe", lambda e, ns=ns: e.matmul(
                    ps_n[0:1, :], onescol[:],
                    sq[:, ns * 512:(ns + 1) * 512], start=True, stop=False),
                    waits=[sqa, t_ones] + pren)
                p.emit("pe", lambda e, ns=ns: e.matmul(
                    ps_n[0:1, :], onescol[0:32, :],
                    b32("sqh")[:, ns * 512:(ns + 1) * 512], start=False,
                    stop=False), waits=[sqb])
                mmn = p.emit("pe", lambda e, ns=ns: e.matmul(
                    ps_n[0:1, :], onescol[0:32, :],
                    b32("sql")[:, ns * 512:(ns + 1) * 512], start=False,
                    stop=True), waits=[sqc])
                red = p.emit("dve", lambda e, ns=ns: e.reduce_max(
                    qtmp[0:1, ns:ns + 1], ps_n[0:1, :], axis=AX.X), waits=[mmn])
                G.consumed("ps_n", [red])
                reds.append(red)
            red = p.emit("dve", lambda e, h=h: e.tensor_max(
                qmax_sb[0:1, h:h + 1], qtmp[0:1, 0:1], qtmp[0:1, 1:2]), waits=reds)
            G.consumed("sqg", [mmn])
            G.consumed("sqhg", [mmn])
            G.consumed("sqlg", [mmn])
            G.consumed("qhT", [st_q, sqa])
            qm_toks.append(red)
        st_qmax = p.dma("sd", qmaxA_o, qmax_sb[:], waits=[qm_toks[-1]], sem="qm")

        KCH = KVL // 128
        kq1 = p.emit("dve", lambda e: e.tensor_mul(
            b32("kpesq_hi"), kpe_hi_ap, kpe_hi_ap), waits=[r_kpe])
        kq2 = p.emit("dve", lambda e: e.tensor_mul(
            b32("kpesq_lo"), kpe_lo_ap, kpe_lo_ap), waits=[r_kpe])
        st_k_toks = []
        kn_toks = []
        for h in range(H):
            mm = None
            pre = G.use("ps_w")
            for c in range(KCH):
                for ns in range(2):
                    mm = p.emit("pe", lambda e, h=h, c=c, ns=ns: e.matmul(
                        ps_w[:, ns * 512:(ns + 1) * 512],
                        wuk_bf[:, c * 2048 + h * NOPE: c * 2048 + (h + 1) * NOPE],
                        ckvT[:, c * TLOC + ns * 512: c * TLOC + (ns + 1) * 512],
                        start=(c == 0), stop=(c == KCH - 1)),
                        waits=[t_ckvT, t_wuk] + pre)
            ev = p.emit("act", lambda e: e.activation(khT, ps_w[:], AF.Copy),
                        waits=[mm] + G.use("khT"))
            G.consumed("ps_w", [ev])
            st_k = p.dma("sd", kAT_o[h], khT, waits=[ev], sem="stk")
            st_k_toks.append(st_k)
            sqa = p.emit("dve", lambda e: e.tensor_mul(sq, khT, khT),
                         waits=[ev] + G.use("sqg"))
            mmn = None
            cps = []
            for ns in range(2):
                pren = G.use("ps_n")
                p.emit("pe", lambda e, ns=ns: e.matmul(
                    ps_n[0:1, :], onescol[:],
                    sq[:, ns * 512:(ns + 1) * 512], start=True, stop=False),
                    waits=[sqa, t_ones] + pren)
                p.emit("pe", lambda e, ns=ns: e.matmul(
                    ps_n[0:1, :], onescol[0:32, :],
                    b32("kpesq_hi")[:, ns * 512:(ns + 1) * 512], start=False,
                    stop=False), waits=[kq1])
                mmn = p.emit("pe", lambda e, ns=ns: e.matmul(
                    ps_n[0:1, :], onescol[0:32, :],
                    b32("kpesq_lo")[:, ns * 512:(ns + 1) * 512], start=False,
                    stop=True), waits=[kq2])
                cp = p.emit("dve", lambda e, ns=ns: e.tensor_copy(
                    knrow[0:1, ns * 512:(ns + 1) * 512], ps_n[0:1, :]),
                    waits=[mmn] + G.use("knrow"))
                G.consumed("ps_n", [cp])
                cps.append(cp)
            st_kn = p.dma("sd", knormA_o[h:h + 1, :], knrow[0:1, :], waits=cps, sem="skn")
            G.consumed("sqg", [mmn])
            G.consumed("khT", [st_k, sqa])
            G.consumed("knrow", [st_kn])
            kn_toks.append(st_kn)

        st_v = None
        for tt in range(NT):
            evs = []
            for ncol in range(4):
                psu, psb = (("ps_a", ps_a) if ncol % 2 == 0 else ("ps_b", ps_b))
                mm = None
                pre = G.use(psu)
                for c in range(KCH):
                    mm = p.emit("pe", lambda e, tt=tt, c=c, ncol=ncol, psb=psb:
                                e.matmul(
                        psb[:],
                        ckvT[:, c * TLOC + tt * 128: c * TLOC + (tt + 1) * 128],
                        wuv_bf[:, c * 2048 + ncol * 512: c * 2048 + (ncol + 1) * 512],
                        start=(c == 0), stop=(c == KCH - 1)),
                        waits=[t_ckvT, t_wuv, st_v] + pre)
                ev = p.emit("act", lambda e, ncol=ncol, psb=psb: e.activation(
                    vat[:, ncol * 512:(ncol + 1) * 512], psb[:], AF.Copy),
                    waits=[mm])
                G.consumed(psu, [ev])
                evs.append(ev)
            st_v = p.dma("ad", vA_o[tt * 128:(tt + 1) * 128, :], vat, waits=evs,
                         sem="stv")

        p.emit("pool", lambda e: e.nop(),
               waits=[p.full(x) for x in
                      ("stq", "sp1", "sp2", "qm", "stk", "skn", "stv", "kp")])

        p.build(block)
    return nc




def build_L1():
    nc = bass.Bass("TRN2", target_bir_lowering=False, debug=False)
    ap = lambda name, shape, dt, kind: nc.dram_tensor(name, shape, dt, kind=kind).ap()

    x_in = ap("x", [TLOC, D], F32, "ExternalInput")
    qn_i = ap("qnT", [H, 128, TLOC], BF16, "ExternalInput")
    qp_i = ap("qpT64", [64, H * TLOC], BF16, "ExternalInput")
    kT_i = ap("kT", [H, 128, S], BF16, "ExternalInput")
    v_i = ap("v", [S, H * 129], BF16, "ExternalInput")
    kpe_i = ap("kpeT", [64, S], BF16, "ExternalInput")
    knorm_i = ap("knorm", [H, S], F32, "ExternalInput")
    qmax_i = ap("qmax", [1, H], F32, "ExternalInput")
    mask_i = ap("mask", [128, 1024], BF16, "ExternalInput")
    iden = ap("iden", [128, 128], BF16, "ExternalInput")
    consts = ap("consts", [128, 4], F32, "ExternalInput")
    wo_i = ap("w_o", [D, D], F32, "ExternalInput")
    w1t_i = ap("w1t", [64, 128, 16, 128], F32, "ExternalInput")
    w2t_i = ap("w2t", [64, 128, D], F32, "ExternalInput")
    b1r_i = ap("b1r", [128, 64], F32, "ExternalInput")
    b2r_i = ap("b2r", [1, D], F32, "ExternalInput")
    wkT_i = ap("WkT", [D, D], F32, "ExternalInput")
    wvT_i = ap("WvT", [D, D], F32, "ExternalInput")

    x2_o = ap("x2", [TLOC, D], F32, "ExternalOutput")
    lnx2T_o = ap("lnx2T", [DC, 128, TLOC], BF16, "ExternalOutput")
    kBT_o = ap("kBT", [H, 128, TLOC], BF16, "ExternalOutput")
    vB_o = ap("vB", [TLOC, D], BF16, "ExternalOutput")
    w1q_o = ap("w1q", [64, 128, 16, 128], BF16, "ExternalOutput")
    w2q_o = ap("w2q", [64, 128, D], BF16, "ExternalOutput")

    L1DBG = bool(_osmod.environ.get("L1DBG"))
    if L1DBG:
        x1dbg_o = ap("x1dbg", [TLOC, D], F32, "ExternalOutput")
        callb_o = ap("callbdbg", [128, H], F32, "ExternalOutput")
        attT_o = ap("attTdbg", [DC, 128, TLOC], BF16, "ExternalOutput")
        acc_o = ap("accdbg", [128, NT * H * 129], F32, "ExternalOutput")
    x1_s = nc.dram_tensor("x1_s", [TLOC, D], F32, kind="Internal").ap()
    km_s = nc.dram_tensor("km_s", [H, 1], F32, kind="Internal").ap()

    p = Prog(nc)
    G = Guard()

    with (
        nc.sbuf_tensor("bfA", [128, 105600], BF16) as bfA,
        nc.sbuf_tensor("stats", [128, 36], F32) as stats,
        nc.sbuf_tensor("idn", [128, 128], BF16) as idn,
        nc.sbuf_tensor("csts", [128, 4], F32) as csts,
        nc.sbuf_tensor("onescol", [128, 1], BF16) as onescol,
        nc.sbuf_tensor("onesrow1", [1, 128], BF16) as onesrow1,
        nc.sbuf_tensor("taus", [128, 4], F32) as taus,
        nc.sbuf_tensor("callb", [128, H], F32) as callb,
        nc.sbuf_tensor("brow", [1, 160], F32) as brow,
        nc.sbuf_tensor("acc16", [128, 32], F32) as acc16,
        nc.psum_tensor("ps_s0", [128, 1024], F32) as ps_s0,
        nc.psum_tensor("ps_s1", [128, 1024], F32) as ps_s1,
        nc.psum_tensor("ps_a", [128, 512], F32) as ps_a,
        nc.psum_tensor("ps_b", [128, 512], F32) as ps_b,
        nc.psum_tensor("ps_t", [128, 128], BF16) as ps_t,
        nc.psum_tensor("ps_n", [1, 512], F32) as ps_n,
        nc.Block() as block,
    ):
        A = Arena(bfA)
        ps_scr = [ps_s0, ps_s1]
        ps_av = [ps_a, ps_b]

        def reg(name, rows=128):
            return A.get(name, rows)

        p.dma("sd", idn[:], iden, sem="s0")
        p.dma("sd", csts[:], consts, sem="s0")
        mask_sb = A.alloc("mask", BF16, 1024)
        p.dma("sd", mask_sb, mask_i, sem="s0")
        qmaxrow = A.alloc("qmaxrow", F32, 16)
        qmr = reg("qmaxrow", 1)
        p.dma("sd", qmr, qmax_i, sem="s0")
        t_static = p.full("s0")
        t_ones = p.emit("dve", lambda e: e.memset(onescol[:], 1.0))
        t_ones1 = p.emit("dve", lambda e: e.memset(onesrow1[:], 1.0))
        onesrowF = brow[0:1, 32:160]
        onescolF = stats[:, 32:33]
        t_onesF = p.emit("dve", lambda e: e.memset(onesrowF, 1.0))
        t_onescF = p.emit("dve", lambda e: e.memset(onescolF, 1.0))

        knorm_sb = A.alloc("knorm", F32, S)
        kn16 = reg("knorm", H)
        p.dma("ad", kn16, knorm_i, sem="kn")
        kmax16 = A.alloc("kmax16", F32, 1)
        km_col = reg("kmax16", H)
        t1 = p.emit("dve", lambda e: e.reduce_max(km_col, kn16, axis=AX.X),
                    waits=[p.full("kn")])
        p.dma("sd", km_s, km_col, waits=[t1], sem="km1")
        kmrow = A.alloc("kmrow", F32, 16)
        kmr = reg("kmrow", 1)
        p.dma("sd", kmr, km_s.rearrange("h o -> o h"),
              waits=[p.full("km1")], sem="km2")
        b1_ = p.emit("dve", lambda e: e.tensor_mul(brow[0:1, 0:16], qmr, kmr),
                     waits=[p.full("km2"), t_static])
        b2_ = p.emit("act", lambda e: e.activation(
            brow[0:1, 0:16], brow[0:1, 0:16], AF.Sqrt), waits=[b1_])
        b3_ = p.emit("dve", lambda e: e.tensor_scalar(
            brow[0:1, 0:16], brow[0:1, 0:16], -SCALE_A * 1.02, SHIFT, ALU.mult,
            ALU.add), waits=[b2_])
        bm_ = p.emit("pe", lambda e: e.matmul(
            ps_a[:, 0:16], onesrowF, brow[0:1, 0:16], start=True, stop=True),
            waits=[b3_, t_onesF])
        t_call = p.emit("dve", lambda e: e.tensor_copy(callb[:], ps_a[:, 0:16]),
                        waits=[bm_])
        G.consumed("av0", [t_call])
        A.free("knorm")
        A.free("kmax16")
        A.free("kmrow")
        A.free("qmaxrow")

        qn_sb = A.alloc("qn", BF16, H * TLOC)
        A.alloc("qp", BF16, H * TLOC)
        qp64 = reg("qp", 64)
        kbuf = [A.alloc(f"kb{i}", BF16, H * 512) for i in range(2)]
        for i in range(2):
            A.alloc(f"kpeb{i}", BF16, 512)
        qp128 = reg("qp", 128)
        t_qz = p.emit("dve", lambda e: e.memset(qp128[64:128, :], 0.0))
        kpe128b = [reg(f"kpeb{i}", 128) for i in range(2)]
        t_kpz = [p.emit("dve", lambda e, i=i: e.memset(
            kpe128b[i][64:128, :], 0.0)) for i in range(2)]
        vbuf = [A.alloc(f"vb{i}", BF16, H * 129) for i in range(4)]
        ptil = [A.alloc(f"pt{i}", BF16, 1024) for i in range(4)]
        acc = A.alloc("acc", F32, NT * H * 129)
        wr1b = [A.alloc(f"w1r{i}", F32, 1024) for i in range(3)]
        wqb = [A.alloc("wq0", BF16, 2048)]

        p.dma("ad", qn_sb.rearrange("p (h t) -> p h t", h=H),
              qn_i.rearrange("h p t -> p h t"), sem="q0")
        p.dma("ad", qp64, qp_i, sem="q0")
        t_q = p.full("q0")
        t_accz = p.emit("dve", lambda e: e.memset(acc, 0.0))

        def wchunk_half(w4, ci, hf, dst):
            if w4 is w2t_i:
                return dst, w4[ci][:, hf * 1024:(hf + 1) * 1024]
            return (dst.rearrange("p (c n) -> p c n", c=8),
                    w4[ci][:, hf * 8:(hf + 1) * 8, :])

        zt = p.emit("dve", lambda e: e.memset(acc16[:], 0.0))
        scale_state = {"tok": None}
        tau_tok = {"t": None}
        load_toks = {}
        pend_qc = {}
        q_w1 = []
        q_w2 = []
        NWB = 3

        jobs = []
        for w4, col in ((w1t_i, 0), (w2t_i, 16)):
            for ci in range(64):
                for hf in range(2):
                    jobs.append(("s", w4, ci, hf, col))
        for w4, wqo, tcol, ncol, sl in ((w1t_i, w1q_o, 0, 2, q_w1),
                                        (w2t_i, w2q_o, 1, 3, q_w2)):
            for ci in range(64):
                for hf in range(2):
                    jobs.append(("q", w4, ci, hf, (wqo, tcol, ncol, sl)))

        def job_load(u):
            kind, w4, ci, hf, _ = jobs[u]
            par = u % NWB
            dd, ss = wchunk_half(w4, ci, hf, wr1b[par])
            load_toks[u] = p.dma(("gd", "ad")[u % 2], dd, ss,
                                 waits=G.use(f"w1r{par}"), sem=f"sc{u % 6}")

        def emit_tau_chain():
            s_last = scale_state["tok"]
            r1_ = p.emit("dve", lambda e: e.reduce_sum(
                stats[:, 0:1], acc16[:, 0:8], axis=AX.X), waits=[s_last])
            r2_ = p.emit("dve", lambda e: e.reduce_sum(
                stats[:, 1:2], acc16[:, 16:24], axis=AX.X), waits=[s_last])
            m1_ = p.emit("pe", lambda e: e.matmul(
                ps_n[0:1, 0:2], onescolF, stats[:, 0:2], start=True, stop=True),
                waits=[r1_, r2_, t_onescF] + G.use("ps_n"))
            t1_ = p.emit("dve", lambda e: e.tensor_scalar(
                stats[0:1, 4:6], ps_n[0:1, 0:2], 0.7 / (DFF * D), 0.0, ALU.mult,
                ALU.add), waits=[m1_])
            G.consumed("ps_n", [t1_])
            bm2 = p.emit("pe", lambda e: e.matmul(
                ps_a[:, 0:2], onesrowF, stats[0:1, 4:6], start=True, stop=True),
                waits=[t1_, t_onesF] + G.use("av0"))
            t2_ = p.emit("dve", lambda e: e.tensor_copy(
                taus[:, 0:2], ps_a[:, 0:2]), waits=[bm2])
            G.consumed("av0", [t2_])
            return p.emit("dve", lambda e: e.tensor_scalar(
                taus[:, 2:4], taus[:, 0:2], -1.0, 0.0, ALU.mult, ALU.add),
                waits=[t2_])

        def job_compute(u):
            kind, w4, ci, hf, x = jobs[u]
            par = u % NWB
            buf = wr1b[par]
            tl = load_toks[u]
            if kind == "s":
                col = x
                rtk = p.emit("dve", lambda e, b=buf: e.reduce_sum(
                    stats[:, 16:24], b.rearrange("p (c n) -> p c n", c=8),
                    axis=AX.X, apply_absolute_value=True),
                    waits=[tl] + G.use("scr16"))
                tok = p.emit("dve", lambda e, col=col: e.tensor_add(
                    acc16[:, col:col + 8], acc16[:, col:col + 8],
                    stats[:, 16:24]), waits=[rtk, scale_state["tok"], zt])
                G.consumed("scr16", [tok])
                G.consumed(f"w1r{par}", [tok])
                scale_state["tok"] = tok
            else:
                wqo, tcol, ncol, sl = x
                half = wqb[0][:, hf * 1024:(hf + 1) * 1024]
                qb_ = p.emit("dve", lambda e, b=buf, half=half, ncol=ncol:
                             e.tensor_scalar(
                    half, b, taus[:, ncol:ncol + 1], None, ALU.is_le),
                    waits=[tl, tau_tok["t"]]
                    + (G.use("wq0") if hf == 0 else []))
                qc_ = p.emit("dve", lambda e, b=buf, half=half, tcol=tcol:
                             e.scalar_tensor_tensor(
                    half, b, taus[:, tcol:tcol + 1], half, ALU.is_ge,
                    ALU.subtract), waits=[qb_])
                G.consumed(f"w1r{par}", [qc_])
                pend_qc[hf] = qc_
                if hf == 1:
                    st = p.dma("sd", wqo[ci],
                               (wqb[0] if w4 is w2t_i else
                                wqb[0].rearrange("p (b n) -> p b n", b=16)),
                               waits=[pend_qc[0], pend_qc[1]], sem="wqs0")
                    G.consumed("wq0", [st])
                    sl.append(st)

        drain = {"u": 0}

        def drain_slots(n):
            for _ in range(n):
                u = drain["u"]
                if u >= len(jobs):
                    return
                job_compute(u)
                if u + NWB < len(jobs):
                    job_load(u + NWB)
                drain["u"] += 1
                if u == 255:
                    tau_tok["t"] = emit_tau_chain()

        for u0 in range(NWB):
            job_load(u0)

        merge_toks = {}
        vload = [None] * 4

        def scores_one(kb, h, kt, kpar, kpe64, tkl, tpl, jmin, Nq):
            spar = kt % 2
            scr = ps_scr[spar]
            pre = G.use(f"scr{spar}")
            mm = None
            for n0 in range(0, Nq, 512):
                n1 = min(Nq, n0 + 512)
                mm = p.emit("pe", lambda e, n0=n0, n1=n1: e.matmul(
                    scr[:, n0:n1],
                    kbuf[kpar][:, h * 512 + kt * 128: h * 512 + (kt + 1) * 128],
                    qn_sb[:, h * TLOC + 128 * jmin + n0:
                          h * TLOC + 128 * jmin + n1],
                    start=True, stop=False, skip_group_check=True),
                    waits=[tkl, t_q] + pre)
                mm = p.emit("pe", lambda e, n0=n0, n1=n1: e.matmul(
                    scr[:, n0:n1],
                    kpe128b[kpar][:, kt * 128:(kt + 1) * 128],
                    qp128[:, h * TLOC + 128 * jmin + n0:
                          h * TLOC + 128 * jmin + n1],
                    start=False, stop=False, skip_group_check=True),
                    waits=[tpl, t_qz, t_kpz[kpar]])
            mm = p.emit("pe", lambda e: e.matmul(
                scr[:, 0:128],
                mask_sb[:, (kb % 2) * 512 + kt * 128:
                        (kb % 2) * 512 + (kt + 1) * 128],
                idn[:], start=False, stop=True, skip_group_check=True),
                waits=[t_static])
            ex = p.emit("act", lambda e: e.activation(
                ptil[kt][:, 0:Nq], scr[:, 0:Nq], AF.Exp,
                bias=callb[:, h:h + 1], scale=SCALE_A),
                waits=[mm, t_call] + G.use(f"pt{kt}"))
            G.consumed(f"scr{spar}", [ex])
            return ex

        def av_one(j, h, jmin, exp_toks):
            apar = j % 2
            pav = ps_av[apar]
            prea = G.use(f"av{apar}")
            mm = None
            for kt in range(4):
                mm = p.emit("pe", lambda e, kt=kt: e.matmul(
                    pav[:, 0:129],
                    ptil[kt][:, (j - jmin) * 128:(j - jmin + 1) * 128],
                    vbuf[kt][:, h * 129:(h + 1) * 129],
                    start=(kt == 0), stop=(kt == 3)),
                    waits=[exp_toks[kt], vload[kt]] + prea)
            mg = p.emit("dve", lambda e: e.tensor_add(
                acc[:, (j * H + h) * 129:(j * H + h + 1) * 129],
                acc[:, (j * H + h) * 129:(j * H + h + 1) * 129],
                pav[:, 0:129]),
                waits=[mm, t_accz, merge_toks.get((j, h))])
            G.consumed(f"av{apar}", [mg])
            merge_toks[(j, h)] = mg
            return mm

        for kb in range(16):
            jmin = kb // 2
            Nq = TLOC - 128 * jmin
            kpar = kb % 2
            tkl = p.dma("sd", kbuf[kpar].rearrange("p (h n) -> p h n", h=H),
                        kT_i.rearrange("h p n -> p h n")
                        [:, :, 512 * kb:512 * (kb + 1)],
                        waits=G.use(f"kb{kpar}") + [t_accz], sem=f"kl{kpar}")
            kpe64 = reg(f"kpeb{kpar}", 64)
            tpl = p.dma("sd", kpe64, kpe_i[0:64, 512 * kb:512 * (kb + 1)],
                        waits=G.use(f"kpeb{kpar}"), sem=f"kpl{kpar}")
            for kt in range(4):
                vload[kt] = p.dma(
                    "sd", vbuf[kt],
                    v_i.rearrange("(a p) d -> p a d", p=128)[:, 4 * kb + kt, :],
                    waits=G.use(f"vb{kt}"), sem=f"vl{kt}")
            last_mm = None
            for h in range(H):
                exp_toks = [scores_one(kb, h, kt, kpar, kpe64, tkl, tpl,
                                       jmin, Nq) for kt in range(4)]
                for j in range(jmin, NT):
                    last_mm = av_one(j, h, jmin, exp_toks)
                for kt in range(4):
                    G.consumed(f"pt{kt}", [last_mm])
                drain_slots(2)
            for kt in range(4):
                G.consumed(f"vb{kt}", [last_mm])
            G.consumed(f"kb{kpar}", [last_mm])
            G.consumed(f"kpeb{kpar}", [last_mm])

        drain_slots(len(jobs))

        bar_f = p.barrier()
        A.free("qn")
        A.free("qp")
        for i in range(2):
            A.free(f"kb{i}")
            A.free(f"kpeb{i}")
        for i in range(4):
            A.free(f"vb{i}")
            A.free(f"pt{i}")
        attT = A.alloc("attT", BF16, DC * TLOC)
        att_bf = A.alloc("att_bf", BF16, D)
        linv = stats[:, 8:9]
        t_attT = None
        for tt in range(NT):
            sc_toks = []
            for h in range(H):
                base = (tt * H + h) * 129
                l1 = p.emit("dve", lambda e, base=base: e.tensor_scalar_max(
                    stats[:, 7:8], acc[:, base + 128:base + 129], 1e-30),
                    waits=[merge_toks[(tt, h)]] + G.use("linv") + bar_f)
                l2 = p.emit("dve", lambda e: e.reciprocal(linv, stats[:, 7:8]),
                            waits=[l1])
                sc = p.emit("dve", lambda e, base=base, h=h: e.tensor_scalar(
                    att_bf[:, h * 128:(h + 1) * 128], acc[:, base:base + 128],
                    linv, None, ALU.mult), waits=[l2] + G.use("attbf"))
                G.consumed("linv", [sc])
                sc_toks.append(sc)
            for c in range(DC):
                tp = p.emit("pe", lambda e, c=c: e.transpose(
                    ps_t[:], att_bf[:, c * 128:(c + 1) * 128], idn[:]),
                    waits=[sc_toks[-1], t_static] + G.use("ps_t"))
                t_attT = p.emit("dve", lambda e, c=c, tt=tt: e.tensor_copy(
                    attT[:, c * TLOC + tt * 128: c * TLOC + (tt + 1) * 128],
                    ps_t[:]), waits=[tp])
                G.consumed("ps_t", [t_attT])
            G.consumed("attbf", [t_attT])

        bar_w = p.barrier()
        A.free("acc")
        wo_res = A.alloc("wo_res", BF16, DC * D)
        for c in range(DC):
            p.dma("gd", wo_res[:, c * D:(c + 1) * D],
                  wo_i[c * 128:(c + 1) * 128, :], sem="wol", waits=bar_w)
        t_wo = p.full("wol")
        xres = A.alloc("xres", F32, D)
        lnscr = A.alloc("lnscr", F32, D)
        lnx1 = A.alloc("lnx1", BF16, D)
        lnx1T = A.alloc("lnx1T", BF16, DC * TLOC)
        x1t = A.alloc("x1t", F32, D)

        t_lnx1T = None
        for tt in range(NT):
            xl = p.dma("sd", xres, x_in[tt * 128:(tt + 1) * 128, :],
                       waits=G.use("xres") + bar_w, sem="xl")
            evs = []
            for ncol in range(4):
                apar = ncol % 2
                pav = ps_av[apar]
                prea = G.use(f"av{apar}")
                mm = None
                for c in range(DC):
                    mm = p.emit("pe", lambda e, tt=tt, c=c, ncol=ncol, pav=pav:
                                e.matmul(
                        pav[:, 0:512],
                        attT[:, c * TLOC + tt * 128: c * TLOC + (tt + 1) * 128],
                        wo_res[:, c * D + ncol * 512: c * D + (ncol + 1) * 512],
                        start=(c == 0), stop=(c == DC - 1)),
                        waits=[t_attT, t_wo] + prea)
                ev = p.emit("dve", lambda e, ncol=ncol, pav=pav:
                            e.scalar_tensor_tensor(
                    x1t[:, ncol * 512:(ncol + 1) * 512], pav[:, 0:512], 1.0,
                    xres[:, ncol * 512:(ncol + 1) * 512], ALU.mult, ALU.add),
                    waits=[mm, xl] + (G.use("x1t") if ncol == 0 else []))
                G.consumed(f"av{apar}", [ev])
                evs.append(ev)
            G.consumed("xres", evs)
            st_x1 = p.dma("sd", x1_s[tt * 128:(tt + 1) * 128, :], x1t,
                          waits=evs, sem="sx1")
            sc = stats[:, 0:5]
            t1 = p.emit("act", lambda e: e.activation(
                lnscr, x1t, AF.Square, accum_out=sc[:, 0:1]), waits=evs)
            t2 = p.emit("dve", lambda e: e.reduce_sum(
                sc[:, 1:2], x1t, axis=AX.X), waits=evs)
            t3 = p.emit("dve", lambda e: e.tensor_scalar_mul(
                sc[:, 1:2], sc[:, 1:2], 1.0 / D), waits=[t2])
            t4 = p.emit("dve", lambda e: e.scalar_tensor_tensor(
                sc[:, 2:3], sc[:, 1:2], 1.0, sc[:, 1:2], ALU.mult, ALU.mult),
                waits=[t3])
            t5 = p.emit("dve", lambda e: e.tensor_scalar(
                sc[:, 0:1], sc[:, 0:1], 1.0 / D, None, ALU.mult), waits=[t1])
            t6 = p.emit("dve", lambda e: e.tensor_sub(
                sc[:, 2:3], sc[:, 0:1], sc[:, 2:3]), waits=[t4, t5])
            t7 = p.emit("act", lambda e: e.activation(
                sc[:, 3:4], sc[:, 2:3], AF.Ln, bias=csts[:, 0:1]),
                waits=[t6, t_static])
            t8 = p.emit("act", lambda e: e.activation(
                sc[:, 4:5], sc[:, 3:4], AF.Exp, scale=-0.5), waits=[t7])
            t9 = p.emit("dve", lambda e: e.tensor_scalar(
                lnx1, x1t, sc[:, 1:2], sc[:, 4:5], ALU.subtract, ALU.mult),
                waits=[t8] + G.use("lnx1"))
            G.consumed("x1t", [t9, st_x1, t1, t2])
            tpd = None
            for c in range(DC):
                tp = p.emit("pe", lambda e, c=c: e.transpose(
                    ps_t[:], lnx1[:, c * 128:(c + 1) * 128], idn[:]),
                    waits=[t9, t_static] + G.use("ps_t"))
                tpd = p.emit("dve", lambda e, c=c, tt=tt: e.tensor_copy(
                    lnx1T[:, c * TLOC + tt * 128: c * TLOC + (tt + 1) * 128],
                    ps_t[:]), waits=[tp])
                G.consumed("ps_t", [tpd])
            G.consumed("lnx1", [tpd])
            t_lnx1T = tpd

        if L1DBG:
            p.dma("ad", acc_o, acc,
                  waits=[merge_toks[(j, h)] for j in range(NT)
                         for h in range(H)], sem="dbg")
            p.dma("ad", x1dbg_o, x1_s, waits=[p.full("sx1")], sem="dbg")
            p.dma("ad", callb_o, callb[:], waits=[t_call], sem="dbg")
            p.dma("ad", attT_o.rearrange("c p t -> p c t"),
                  attT.rearrange("p (c t) -> p c t", c=DC), waits=[t_attT],
                  sem="dbg")
        bar2 = p.barrier()
        A.free("attT")
        A.free("att_bf")
        A.free("wo_res")
        A.free("mask")
        A.free("x1t")
        for i in range(2):
            A.free(f"w1r{i}")
        A.free("wq0")
        out2 = A.alloc("out2", F32, NT * D)
        hTb = A.alloc("hTb", BF16, 8 * TLOC)
        w2fb = A.alloc("w2fb", BF16, 8 * D)
        b1sb = A.alloc("b1sb", F32, 64)
        b1c = reg("b1sb")
        p.dma("sd", b1c, b1r_i, sem="b1", waits=bar2)
        t_b1 = p.full("b1")
        b2sb = A.alloc("b2sb", BF16, D)
        b2c = reg("b2sb", 1)
        p.dma("gd", b2c, b2r_i, sem="b2", waits=bar2)
        t_b2 = p.full("b2")
        w1qb = [A.alloc(f"w1qb{i}", BF16, 2048) for i in range(2)]
        t_o2z = p.emit("dve", lambda e: e.memset(out2, 0.0),
                       waits=bar2 + ([p.full("dbg")] if L1DBG else []))

        o2_toks = {}

        def ffnA_one(fb, fl):
            fc = fb * 8 + fl
            par = fc % 2
            tl = p.dma("sd", w1qb[par].rearrange("p (b n) -> p b n", b=16),
                       w1q_o[fc],
                       waits=G.use(f"w1qb{par}") + [q_w1[fc]] + bar2,
                       sem=f"w1l{par}")
            spar = fl % 2
            scr = ps_scr[spar]
            pre = G.use(f"scr{spar}")
            mm = None
            for dc2 in range(DC):
                for ns in range(2):
                    mm = p.emit("pe", lambda e, dc2=dc2, ns=ns: e.matmul(
                        scr[:, ns * 512:(ns + 1) * 512],
                        w1qb[par][:, dc2 * 128:(dc2 + 1) * 128],
                        lnx1T[:, dc2 * TLOC + ns * 512:
                              dc2 * TLOC + (ns + 1) * 512],
                        start=(dc2 == 0), stop=(dc2 == DC - 1)),
                        waits=[tl, t_lnx1T] + pre)
            G.consumed(f"w1qb{par}", [mm])
            ge = p.emit("act", lambda e: e.activation(
                hTb[:, fl * TLOC:(fl + 1) * TLOC], scr[:], AF.Gelu,
                bias=b1c[:, fc:fc + 1]), waits=[mm, t_b1] + G.use(f"hTb{fl}"))
            G.consumed(f"scr{spar}", [ge])
            return ge

        def ffnB_one(fb, tt, ncol, ge_toks, w2l_toks):
            apar = (tt * 4 + ncol) % 2
            pav = ps_av[apar]
            prea = G.use(f"av{apar}")
            mm = None
            for fl in range(8):
                mm = p.emit("pe", lambda e, fl=fl: e.matmul(
                    pav[:, 0:512],
                    hTb[:, fl * TLOC + tt * 128: fl * TLOC + (tt + 1) * 128],
                    w2fb[:, fl * D + ncol * 512: fl * D + (ncol + 1) * 512],
                    start=(fl == 0), stop=False, skip_group_check=True),
                    waits=[ge_toks[fl], w2l_toks[fl]] + prea)
            if fb == 0:
                mm = p.emit("pe", lambda e: e.matmul(
                    pav[:, 0:512], onesrow1[:],
                    b2c[:, ncol * 512:(ncol + 1) * 512],
                    start=False, stop=True, skip_group_check=True),
                    waits=[t_b2, t_ones1])
            mg = p.emit("dve", lambda e: e.tensor_add(
                out2[:, tt * D + ncol * 512: tt * D + (ncol + 1) * 512],
                out2[:, tt * D + ncol * 512: tt * D + (ncol + 1) * 512],
                pav[:, 0:512]),
                waits=[mm, t_o2z, o2_toks.get((tt, ncol))])
            G.consumed(f"av{apar}", [mg])
            o2_toks[(tt, ncol)] = mg
            return mm

        for fb in range(8):
            ge_toks = [ffnA_one(fb, fl) for fl in range(8)]
            w2l_toks = []
            for fl in range(8):
                fc = fb * 8 + fl
                tl = p.dma("gd", w2fb[:, fl * D:(fl + 1) * D], w2q_o[fc],
                           waits=G.use(f"w2fb{fl}") + [q_w2[fc]] + bar2,
                           sem="w2l")
                w2l_toks.append(tl)
            last = None
            for tt in range(NT):
                for ncol in range(4):
                    last = ffnB_one(fb, tt, ncol, ge_toks, w2l_toks)
            for fl in range(8):
                G.consumed(f"hTb{fl}", [last])
                G.consumed(f"w2fb{fl}", [last])

        bar3 = p.barrier()
        A.free("lnx1T")
        A.free("hTb")
        A.free("w2fb")
        A.free("b1sb")
        for i in range(2):
            A.free(f"w1qb{i}")
        x2t = A.alloc("x2t", F32, D)
        lnx2 = A.alloc("lnx2", BF16, D)
        lnx2T = A.alloc("lnx2T", BF16, DC * TLOC)
        t_lnx2T = None
        for tt in range(NT):
            xl = p.dma("sd", xres, x1_s[tt * 128:(tt + 1) * 128, :],
                       waits=G.use("xres") + [p.full("sx1")], sem="xl2")
            ev = p.emit("dve", lambda e, tt=tt: e.tensor_add(
                x2t, out2[:, tt * D:(tt + 1) * D], xres),
                waits=[xl, o2_toks[(tt, 0)], o2_toks[(tt, 1)],
                       o2_toks[(tt, 2)], o2_toks[(tt, 3)]] + G.use("x2t")
                + bar3)
            G.consumed("xres", [ev])
            st_x2 = p.dma("sd", x2_o[tt * 128:(tt + 1) * 128, :], x2t,
                          waits=[ev], sem="sx2")
            sc = stats[:, 0:5]
            t1 = p.emit("act", lambda e: e.activation(
                lnscr, x2t, AF.Square, accum_out=sc[:, 0:1]), waits=[ev])
            t2 = p.emit("dve", lambda e: e.reduce_sum(
                sc[:, 1:2], x2t, axis=AX.X), waits=[ev])
            t3 = p.emit("dve", lambda e: e.tensor_scalar_mul(
                sc[:, 1:2], sc[:, 1:2], 1.0 / D), waits=[t2])
            t4 = p.emit("dve", lambda e: e.scalar_tensor_tensor(
                sc[:, 2:3], sc[:, 1:2], 1.0, sc[:, 1:2], ALU.mult, ALU.mult),
                waits=[t3])
            t5 = p.emit("dve", lambda e: e.tensor_scalar(
                sc[:, 0:1], sc[:, 0:1], 1.0 / D, None, ALU.mult), waits=[t1])
            t6 = p.emit("dve", lambda e: e.tensor_sub(
                sc[:, 2:3], sc[:, 0:1], sc[:, 2:3]), waits=[t4, t5])
            t7 = p.emit("act", lambda e: e.activation(
                sc[:, 3:4], sc[:, 2:3], AF.Ln, bias=csts[:, 0:1]),
                waits=[t6, t_static])
            t8 = p.emit("act", lambda e: e.activation(
                sc[:, 4:5], sc[:, 3:4], AF.Exp, scale=-0.5), waits=[t7])
            t9 = p.emit("dve", lambda e: e.tensor_scalar(
                lnx2, x2t, sc[:, 1:2], sc[:, 4:5], ALU.subtract, ALU.mult),
                waits=[t8] + G.use("lnx2"))
            G.consumed("x2t", [t9, st_x2, t1, t2])
            tpd = None
            for c in range(DC):
                tp = p.emit("pe", lambda e, c=c: e.transpose(
                    ps_t[:], lnx2[:, c * 128:(c + 1) * 128], idn[:]),
                    waits=[t9, t_static] + G.use("ps_t"))
                tpd = p.emit("dve", lambda e, c=c, tt=tt: e.tensor_copy(
                    lnx2T[:, c * TLOC + tt * 128: c * TLOC + (tt + 1) * 128],
                    ps_t[:]), waits=[tp])
                G.consumed("ps_t", [tpd])
            G.consumed("lnx2", [tpd])
            t_lnx2T = tpd
        p.dma("sd", lnx2T_o.rearrange("c p t -> p c t"),
              lnx2T.rearrange("p (c t) -> p c t", c=DC),
              waits=[t_lnx2T], sem="slnxT")

        bar4 = p.barrier()
        A.free("out2")
        A.free("xres")
        A.free("lnscr")
        A.free("x2t")
        A.free("lnx2")
        A.free("b2sb")
        wbig = A.alloc("wbig", BF16, DC * D)
        for c in range(DC):
            p.dma("gd", wbig[:, c * D:(c + 1) * D],
                  wkT_i[c * 128:(c + 1) * 128, :], sem="wkl", waits=bar4)
        t_wk = p.full("wkl")
        ktile = A.alloc("ktile", F32, D)
        knb = A.alloc("knb", BF16, D)
        kbT = A.alloc("kbT", BF16, H * TLOC)
        hsum = stats[:, 16:32]
        t_kbT = None
        for tt in range(NT):
            evs = []
            for ncol in range(4):
                spar = ncol % 2
                scr = ps_scr[spar]
                pre = G.use(f"scr{spar}")
                mm = None
                for c in range(DC):
                    mm = p.emit("pe", lambda e, tt=tt, c=c, ncol=ncol, scr=scr:
                                e.matmul(
                        scr[:, 0:512],
                        lnx2T[:, c * TLOC + tt * 128: c * TLOC + (tt + 1) * 128],
                        wbig[:, c * D + ncol * 512: c * D + (ncol + 1) * 512],
                        start=(c == 0), stop=(c == DC - 1)),
                        waits=[t_lnx2T, t_wk] + pre)
                ev = p.emit("act", lambda e, ncol=ncol, scr=scr: e.activation(
                    ktile[:, ncol * 512:(ncol + 1) * 512], scr[:, 0:512],
                    AF.Copy), waits=[mm] + G.use("ktile")[0:0])
                G.consumed(f"scr{spar}", [ev])
                evs.append(ev)
            sq_toks = []
            for h in range(H):
                sq_toks.append(p.emit("act", lambda e, h=h: e.activation(
                    lnscr[:, 0:128], ktile[:, h * 128:(h + 1) * 128],
                    AF.Square, accum_out=hsum[:, h:h + 1]),
                    waits=evs + G.use("hsum")))
            rs1 = p.emit("act", lambda e: e.activation(
                stats[:, 0:16] if False else hsum, hsum, AF.Ln,
                bias=csts[:, 0:1], scale=1.0 / 128), waits=sq_toks)
            rs2 = p.emit("act", lambda e: e.activation(
                hsum, hsum, AF.Exp, scale=-0.5), waits=[rs1])
            nm_toks = []
            for h in range(H):
                nm_toks.append(p.emit("dve", lambda e, h=h: e.tensor_scalar(
                    knb[:, h * 128:(h + 1) * 128],
                    ktile[:, h * 128:(h + 1) * 128], hsum[:, h:h + 1], None,
                    ALU.mult), waits=[rs2] + G.use("knb")))
            G.consumed("hsum", nm_toks)
            G.consumed("ktile", nm_toks)
            for c in range(H):
                tp = p.emit("pe", lambda e, c=c: e.transpose(
                    ps_t[:], knb[:, c * 128:(c + 1) * 128], idn[:]),
                    waits=[nm_toks[-1], t_static] + G.use("ps_t"))
                t_kbT = p.emit("dve", lambda e, c=c, tt=tt: e.tensor_copy(
                    kbT[:, c * TLOC + tt * 128: c * TLOC + (tt + 1) * 128],
                    ps_t[:]), waits=[tp])
                G.consumed("ps_t", [t_kbT])
            G.consumed("knb", [t_kbT])
        p.dma("sd", kBT_o.rearrange("h p t -> p h t"),
              kbT.rearrange("p (h t) -> p h t", h=H), waits=[t_kbT], sem="skbt")

        for c in range(DC):
            p.dma("gd", wbig[:, c * D:(c + 1) * D],
                  wvT_i[c * 128:(c + 1) * 128, :],
                  waits=[t_kbT] if c < 2 else (), sem="wvl")
        t_wv = p.full("wvl")
        vtile = A.alloc("vtile", BF16, D)
        st_vb = None
        for tt in range(NT):
            evs = []
            for ncol in range(4):
                spar = ncol % 2
                scr = ps_scr[spar]
                pre = G.use(f"scr{spar}")
                mm = None
                for c in range(DC):
                    mm = p.emit("pe", lambda e, tt=tt, c=c, ncol=ncol, scr=scr:
                                e.matmul(
                        scr[:, 0:512],
                        lnx2T[:, c * TLOC + tt * 128: c * TLOC + (tt + 1) * 128],
                        wbig[:, c * D + ncol * 512: c * D + (ncol + 1) * 512],
                        start=(c == 0), stop=(c == DC - 1)),
                        waits=[t_lnx2T, t_wv] + pre)
                ev = p.emit("act", lambda e, ncol=ncol, scr=scr: e.activation(
                    vtile[:, ncol * 512:(ncol + 1) * 512], scr[:, 0:512],
                    AF.Copy), waits=[mm] + G.use("vtile")[0:0])
                G.consumed(f"scr{spar}", [ev])
                evs.append(ev)
            st_vb = p.dma("ad", vB_o[tt * 128:(tt + 1) * 128, :], vtile,
                          waits=evs + G.use("vtile"), sem="svb")
            G.consumed("vtile", [st_vb])

        p.emit("pool", lambda e: e.nop(),
               waits=[p.full(x) for x in
                      ("sx2", "slnxT", "skbt", "svb", "wqs0")])

        p.build(block)
    return nc




def build_L2():
    nc = bass.Bass("TRN2", target_bir_lowering=False, debug=False)
    ap = lambda name, shape, dt, kind: nc.dram_tensor(name, shape, dt, kind=kind).ap()

    x2_i = ap("x2", [TLOC, D], F32, "ExternalInput")
    lnx2T_i = ap("lnx2T", [DC, 128, TLOC], BF16, "ExternalInput")
    kT_i = ap("kT", [H, 128, S], BF16, "ExternalInput")
    v_i = ap("v", [S, H * 129], BF16, "ExternalInput")
    mask_i = ap("mask", [128, 1024], BF16, "ExternalInput")
    iden = ap("iden", [128, 128], BF16, "ExternalInput")
    consts = ap("consts", [128, 4], F32, "ExternalInput")
    wqT_i = ap("WqT", [D, D], F32, "ExternalInput")
    woT_i = ap("WoT", [D, D], F32, "ExternalInput")
    gq_i = ap("gq", [128, 1], F32, "ExternalInput")
    gk_i = ap("gk", [128, 1], F32, "ExternalInput")
    w1q_i = ap("w1q", [64, 128, 16, 128], BF16, "ExternalInput")
    w2q_i = ap("w2q", [64, 128, D], BF16, "ExternalInput")
    b1r_i = ap("b1r", [128, 64], F32, "ExternalInput")
    b2r_i = ap("b2r", [1, D], F32, "ExternalInput")

    xout_o = ap("xout", [TLOC, D], F32, "ExternalOutput")

    x3_s = nc.dram_tensor("x3_s", [TLOC, D], F32, kind="Internal").ap()

    p = Prog(nc)
    G = Guard()

    with (
        nc.sbuf_tensor("bfA", [128, 105600], BF16) as bfA,
        nc.sbuf_tensor("stats", [128, 36], F32) as stats,
        nc.sbuf_tensor("idn", [128, 128], BF16) as idn,
        nc.sbuf_tensor("csts", [128, 4], F32) as csts,
        nc.sbuf_tensor("onescol", [128, 1], BF16) as onescol,
        nc.sbuf_tensor("onesrow1", [1, 128], BF16) as onesrow1,
        nc.sbuf_tensor("g2col", [128, 1], F32) as g2col,
        nc.sbuf_tensor("callb", [128, H], F32) as callb,
        nc.sbuf_tensor("brow", [1, 160], F32) as brow,
        nc.psum_tensor("ps_s0", [128, 1024], F32) as ps_s0,
        nc.psum_tensor("ps_s1", [128, 1024], F32) as ps_s1,
        nc.psum_tensor("ps_a", [128, 512], F32) as ps_a,
        nc.psum_tensor("ps_b", [128, 512], F32) as ps_b,
        nc.psum_tensor("ps_t", [128, 128], BF16) as ps_t,
        nc.psum_tensor("ps_n", [1, 512], F32) as ps_n,
        nc.Block() as block,
    ):
        A = Arena(bfA)
        ps_scr = [ps_s0, ps_s1]
        ps_av = [ps_a, ps_b]

        p.dma("sd", idn[:], iden, sem="s0")
        p.dma("sd", csts[:], consts, sem="s0")
        mask_sb = A.alloc("mask", BF16, 1024)
        p.dma("sd", mask_sb, mask_i, sem="s0")
        p.dma("sd", stats[:, 24:25], gq_i, sem="s0")
        p.dma("sd", stats[:, 25:26], gk_i, sem="s0")
        t_static = p.full("s0")
        t_ones = p.emit("dve", lambda e: e.memset(onescol[:], 1.0))
        t_ones1 = p.emit("dve", lambda e: e.memset(onesrow1[:], 1.0))
        onesrowF = brow[0:1, 32:160]
        onescolF = stats[:, 32:33]
        t_onesF = p.emit("dve", lambda e: e.memset(onesrowF, 1.0))
        t_g2 = p.emit("dve", lambda e: e.tensor_mul(
            g2col[:], stats[:, 24:25], stats[:, 25:26]), waits=[t_static])

        qT = A.alloc("qT", BF16, H * TLOC)
        qn_ps = A.alloc("qn_ps", F32, 16)
        lnx2T = A.alloc("lnx2T", BF16, DC * TLOC)
        p.dma("ad", lnx2T.rearrange("p (c t) -> p c t", c=DC),
              lnx2T_i.rearrange("c p t -> p c t"), sem="lx")
        t_lnx2T = p.full("lx")
        wbig = A.alloc("wbig", BF16, DC * D)
        for c in range(DC):
            p.dma("gd", wbig[:, c * D:(c + 1) * D],
                  wqT_i[c * 128:(c + 1) * 128, :], sem="wql")
        t_wq = p.full("wql")
        qtile = A.alloc("qtile", F32, D)
        qnb = A.alloc("qnb", BF16, D)
        hsum = stats[:, 16:32][:, 0:16] if False else stats[:, 0:16]
        sq = A.alloc("sq", BF16, TLOC)
        t_qT = None
        qmax_toks = []
        for tt in range(NT):
            evs = []
            for ncol in range(4):
                spar = ncol % 2
                scr = ps_scr[spar]
                pre = G.use(f"scr{spar}")
                mm = None
                for c in range(DC):
                    mm = p.emit("pe", lambda e, tt=tt, c=c, ncol=ncol, scr=scr:
                                e.matmul(
                        scr[:, 0:512],
                        lnx2T[:, c * TLOC + tt * 128: c * TLOC + (tt + 1) * 128],
                        wbig[:, c * D + ncol * 512: c * D + (ncol + 1) * 512],
                        start=(c == 0), stop=(c == DC - 1)),
                        waits=[t_lnx2T, t_wq] + pre)
                ev = p.emit("act", lambda e, ncol=ncol, scr=scr: e.activation(
                    qtile[:, ncol * 512:(ncol + 1) * 512], scr[:, 0:512],
                    AF.Copy), waits=[mm] + G.use("qtile")[0:0])
                G.consumed(f"scr{spar}", [ev])
                evs.append(ev)
            sq_toks = []
            for h in range(H):
                sq_toks.append(p.emit("act", lambda e, h=h: e.activation(
                    qnb[:, 0:128], qtile[:, h * 128:(h + 1) * 128],
                    AF.Square, accum_out=hsum[:, h:h + 1]),
                    waits=evs + G.use("hsum") + G.use("qnb")))
            rs1 = p.emit("act", lambda e: e.activation(
                hsum, hsum, AF.Ln, bias=csts[:, 0:1], scale=1.0 / 128),
                waits=sq_toks + [t_static])
            rs2 = p.emit("act", lambda e: e.activation(
                hsum, hsum, AF.Exp, scale=-0.5), waits=[rs1])
            nm_toks = []
            for h in range(H):
                nm_toks.append(p.emit("dve", lambda e, h=h: e.tensor_scalar(
                    qnb[:, h * 128:(h + 1) * 128],
                    qtile[:, h * 128:(h + 1) * 128], hsum[:, h:h + 1], None,
                    ALU.mult), waits=[rs2] + G.use("qnb")))
            G.consumed("hsum", nm_toks)
            G.consumed("qtile", nm_toks)
            G.consumed("qnb", nm_toks)
            for h in range(H):
                tp = p.emit("pe", lambda e, h=h: e.transpose(
                    ps_t[:], qnb[:, h * 128:(h + 1) * 128], idn[:]),
                    waits=[nm_toks[h], t_static] + G.use("ps_t"))
                t_qT = p.emit("dve", lambda e, h=h, tt=tt: e.tensor_scalar(
                    qT[:, h * TLOC + tt * 128: h * TLOC + (tt + 1) * 128],
                    ps_t[:], g2col[:], None, ALU.mult), waits=[tp, t_g2])
                G.consumed("ps_t", [t_qT])
            G.consumed("qnb", [t_qT])
        qn_row = A.get("qn_ps", 1)
        for h in range(H):
            sqa = p.emit("dve", lambda e, h=h: e.tensor_mul(
                sq, qT[:, h * TLOC:(h + 1) * TLOC],
                qT[:, h * TLOC:(h + 1) * TLOC]), waits=[t_qT] + G.use("sqg"))
            reds = []
            mmn = None
            for ns in range(2):
                pren = G.use("ps_n")
                mmn = p.emit("pe", lambda e, ns=ns: e.matmul(
                    ps_n[0:1, :], onescol[:],
                    sq[:, ns * 512:(ns + 1) * 512], start=True, stop=True),
                    waits=[sqa, t_ones] + pren)
                red = p.emit("dve", lambda e, ns=ns: e.reduce_max(
                    stats[0:1, 20 + ns:21 + ns], ps_n[0:1, :], axis=AX.X),
                    waits=[mmn])
                G.consumed("ps_n", [red])
                reds.append(red)
            qmax_toks.append(p.emit("dve", lambda e, h=h: e.tensor_max(
                qn_row[0:1, h:h + 1], stats[0:1, 20:21], stats[0:1, 21:22]),
                waits=reds))
            G.consumed("sqg", [mmn])
        b1_ = p.emit("act", lambda e: e.activation(
            brow[0:1, 0:16], qn_row[0:1, 0:16], AF.Sqrt), waits=qmax_toks)
        b3_ = p.emit("dve", lambda e: e.tensor_scalar(
            brow[0:1, 0:16], brow[0:1, 0:16],
            -SCALE_B * 1.02 * (128.0 ** 0.5), SHIFT, ALU.mult, ALU.add),
            waits=[b1_])
        bm_ = p.emit("pe", lambda e: e.matmul(
            ps_a[:, 0:16], onesrowF, brow[0:1, 0:16], start=True, stop=True),
            waits=[b3_, t_onesF] + G.use("av0"))
        t_call = p.emit("dve", lambda e: e.tensor_copy(callb[:], ps_a[:, 0:16]),
                        waits=[bm_])
        G.consumed("av0", [t_call])

        bar_q = p.barrier()
        A.free("wbig")
        A.free("qtile")
        A.free("qnb")
        A.free("sq")
        A.free("lnx2T")
        kbuf = [A.alloc(f"kb{i}", BF16, H * 512) for i in range(2)]
        vbuf = [A.alloc(f"vb{i}", BF16, H * 129) for i in range(4)]
        ptil = [A.alloc(f"pt{i}", BF16, 1024) for i in range(4)]
        acc = A.alloc("acc", F32, NT * H * 129)
        t_accz = p.emit("dve", lambda e: e.memset(acc, 0.0), waits=bar_q)

        merge_toks = {}
        vload = [None] * 4

        def scores_one(kb, h, kt, kpar, tkl, jmin, Nq):
            spar = kt % 2
            scr = ps_scr[spar]
            pre = G.use(f"scr{spar}")
            mm = None
            for n0 in range(0, Nq, 512):
                n1 = min(Nq, n0 + 512)
                mm = p.emit("pe", lambda e, n0=n0, n1=n1: e.matmul(
                    scr[:, n0:n1],
                    kbuf[kpar][:, h * 512 + kt * 128: h * 512 + (kt + 1) * 128],
                    qT[:, h * TLOC + 128 * jmin + n0:
                       h * TLOC + 128 * jmin + n1],
                    start=True, stop=False, skip_group_check=True),
                    waits=[tkl, t_qT] + pre)
            mm = p.emit("pe", lambda e: e.matmul(
                scr[:, 0:128],
                mask_sb[:, (kb % 2) * 512 + kt * 128:
                        (kb % 2) * 512 + (kt + 1) * 128],
                idn[:], start=False, stop=True, skip_group_check=True),
                waits=[t_static])
            ex = p.emit("act", lambda e: e.activation(
                ptil[kt][:, 0:Nq], scr[:, 0:Nq], AF.Exp,
                bias=callb[:, h:h + 1], scale=SCALE_B),
                waits=[mm, t_call] + G.use(f"pt{kt}"))
            G.consumed(f"scr{spar}", [ex])
            return ex

        def av_one(j, h, jmin, exp_toks):
            apar = j % 2
            pav = ps_av[apar]
            prea = G.use(f"av{apar}")
            mm = None
            for kt in range(4):
                mm = p.emit("pe", lambda e, kt=kt: e.matmul(
                    pav[:, 0:129],
                    ptil[kt][:, (j - jmin) * 128:(j - jmin + 1) * 128],
                    vbuf[kt][:, h * 129:(h + 1) * 129],
                    start=(kt == 0), stop=(kt == 3)),
                    waits=[exp_toks[kt], vload[kt]] + prea)
            mg = p.emit("dve", lambda e: e.tensor_add(
                acc[:, (j * H + h) * 129:(j * H + h + 1) * 129],
                acc[:, (j * H + h) * 129:(j * H + h + 1) * 129],
                pav[:, 0:129]),
                waits=[mm, t_accz, merge_toks.get((j, h))])
            G.consumed(f"av{apar}", [mg])
            merge_toks[(j, h)] = mg
            return mm

        for kb in range(16):
            jmin = kb // 2
            Nq = TLOC - 128 * jmin
            kpar = kb % 2
            tkl = p.dma("sd", kbuf[kpar].rearrange("p (h n) -> p h n", h=H),
                        kT_i.rearrange("h p n -> p h n")
                        [:, :, 512 * kb:512 * (kb + 1)],
                        waits=G.use(f"kb{kpar}") + bar_q, sem=f"kl{kpar}")
            for kt in range(4):
                vload[kt] = p.dma(
                    "ad", vbuf[kt],
                    v_i.rearrange("(a p) d -> p a d", p=128)[:, 4 * kb + kt, :],
                    waits=G.use(f"vb{kt}") + bar_q, sem=f"vl{kt}")
            last_mm = None
            for h in range(H):
                exp_toks = [scores_one(kb, h, kt, kpar, tkl, jmin, Nq)
                            for kt in range(4)]
                for j in range(jmin, NT):
                    last_mm = av_one(j, h, jmin, exp_toks)
                for kt in range(4):
                    G.consumed(f"pt{kt}", [last_mm])
            for kt in range(4):
                G.consumed(f"vb{kt}", [last_mm])
            G.consumed(f"kb{kpar}", [last_mm])

        bar_f = p.barrier()
        for i in range(2):
            A.free(f"kb{i}")
        for i in range(4):
            A.free(f"vb{i}")
            A.free(f"pt{i}")
        A.free("qT")
        A.free("qn_ps")
        attT = A.alloc("attT", BF16, DC * TLOC)
        att_bf = A.alloc("att_bf", BF16, D)
        linv = stats[:, 8:9]
        t_attT = None
        for tt in range(NT):
            sc_toks = []
            for h in range(H):
                base = (tt * H + h) * 129
                l1 = p.emit("dve", lambda e, base=base: e.tensor_scalar_max(
                    stats[:, 7:8], acc[:, base + 128:base + 129], 1e-30),
                    waits=[merge_toks[(tt, h)]] + G.use("linv") + bar_f)
                l2 = p.emit("dve", lambda e: e.reciprocal(linv, stats[:, 7:8]),
                            waits=[l1])
                sc = p.emit("dve", lambda e, base=base, h=h: e.tensor_scalar(
                    att_bf[:, h * 128:(h + 1) * 128], acc[:, base:base + 128],
                    linv, None, ALU.mult), waits=[l2] + G.use("attbf"))
                G.consumed("linv", [sc])
                sc_toks.append(sc)
            for c in range(DC):
                tp = p.emit("pe", lambda e, c=c: e.transpose(
                    ps_t[:], att_bf[:, c * 128:(c + 1) * 128], idn[:]),
                    waits=[sc_toks[-1], t_static] + G.use("ps_t"))
                t_attT = p.emit("dve", lambda e, c=c, tt=tt: e.tensor_copy(
                    attT[:, c * TLOC + tt * 128: c * TLOC + (tt + 1) * 128],
                    ps_t[:]), waits=[tp])
                G.consumed("ps_t", [t_attT])
            G.consumed("attbf", [t_attT])

        bar_w = p.barrier()
        A.free("acc")
        wo_res = A.alloc("wo_res", BF16, DC * D)
        for c in range(DC):
            p.dma("gd", wo_res[:, c * D:(c + 1) * D],
                  woT_i[c * 128:(c + 1) * 128, :], sem="wol", waits=bar_w)
        t_wo = p.full("wol")
        xres = A.alloc("xres", F32, D)
        lnscr = A.alloc("lnscr", F32, D)
        lnx3 = A.alloc("lnx3", BF16, D)
        lnx3T = A.alloc("lnx3T", BF16, DC * TLOC)
        x1t = A.alloc("x1t", F32, D)

        t_lnx3T = None
        for tt in range(NT):
            xl = p.dma("sd", xres, x2_i[tt * 128:(tt + 1) * 128, :],
                       waits=G.use("xres") + bar_w, sem="xl")
            evs = []
            for ncol in range(4):
                apar = ncol % 2
                pav = ps_av[apar]
                prea = G.use(f"av{apar}")
                mm = None
                for c in range(DC):
                    mm = p.emit("pe", lambda e, tt=tt, c=c, ncol=ncol, pav=pav:
                                e.matmul(
                        pav[:, 0:512],
                        attT[:, c * TLOC + tt * 128: c * TLOC + (tt + 1) * 128],
                        wo_res[:, c * D + ncol * 512: c * D + (ncol + 1) * 512],
                        start=(c == 0), stop=(c == DC - 1)),
                        waits=[t_attT, t_wo] + prea)
                ev = p.emit("dve", lambda e, ncol=ncol, pav=pav:
                            e.scalar_tensor_tensor(
                    x1t[:, ncol * 512:(ncol + 1) * 512], pav[:, 0:512], 1.0,
                    xres[:, ncol * 512:(ncol + 1) * 512], ALU.mult, ALU.add),
                    waits=[mm, xl] + (G.use("x1t") if ncol == 0 else []))
                G.consumed(f"av{apar}", [ev])
                evs.append(ev)
            G.consumed("xres", evs)
            st_x1 = p.dma("sd", x3_s[tt * 128:(tt + 1) * 128, :], x1t,
                          waits=evs, sem="sx1")
            sc = stats[:, 0:5]
            t1 = p.emit("act", lambda e: e.activation(
                lnscr, x1t, AF.Square, accum_out=sc[:, 0:1]), waits=evs)
            t2 = p.emit("dve", lambda e: e.reduce_sum(
                sc[:, 1:2], x1t, axis=AX.X), waits=evs)
            t3 = p.emit("dve", lambda e: e.tensor_scalar_mul(
                sc[:, 1:2], sc[:, 1:2], 1.0 / D), waits=[t2])
            t4 = p.emit("dve", lambda e: e.scalar_tensor_tensor(
                sc[:, 2:3], sc[:, 1:2], 1.0, sc[:, 1:2], ALU.mult, ALU.mult),
                waits=[t3])
            t5 = p.emit("dve", lambda e: e.tensor_scalar(
                sc[:, 0:1], sc[:, 0:1], 1.0 / D, None, ALU.mult), waits=[t1])
            t6 = p.emit("dve", lambda e: e.tensor_sub(
                sc[:, 2:3], sc[:, 0:1], sc[:, 2:3]), waits=[t4, t5])
            t7 = p.emit("act", lambda e: e.activation(
                sc[:, 3:4], sc[:, 2:3], AF.Ln, bias=csts[:, 0:1]),
                waits=[t6, t_static])
            t8 = p.emit("act", lambda e: e.activation(
                sc[:, 4:5], sc[:, 3:4], AF.Exp, scale=-0.5), waits=[t7])
            t9 = p.emit("dve", lambda e: e.tensor_scalar(
                lnx3, x1t, sc[:, 1:2], sc[:, 4:5], ALU.subtract, ALU.mult),
                waits=[t8] + G.use("lnx3"))
            G.consumed("x1t", [t9, st_x1, t1, t2])
            tpd = None
            for c in range(DC):
                tp = p.emit("pe", lambda e, c=c: e.transpose(
                    ps_t[:], lnx3[:, c * 128:(c + 1) * 128], idn[:]),
                    waits=[t9, t_static] + G.use("ps_t"))
                tpd = p.emit("dve", lambda e, c=c, tt=tt: e.tensor_copy(
                    lnx3T[:, c * TLOC + tt * 128: c * TLOC + (tt + 1) * 128],
                    ps_t[:]), waits=[tp])
                G.consumed("ps_t", [tpd])
            G.consumed("lnx3", [tpd])
            t_lnx3T = tpd

        bar2 = p.barrier()
        A.free("attT")
        A.free("att_bf")
        A.free("wo_res")
        A.free("mask")
        A.free("x1t")
        out2 = A.alloc("out2", F32, NT * D)
        hTb = A.alloc("hTb", BF16, 8 * TLOC)
        w2fb = A.alloc("w2fb", BF16, 8 * D)
        b1sb = A.alloc("b1sb", F32, 64)
        b1c = A.get("b1sb")
        p.dma("sd", b1c, b1r_i, sem="b1", waits=bar2)
        t_b1 = p.full("b1")
        b2sb = A.alloc("b2sb", BF16, D)
        b2c = A.get("b2sb", 1)
        p.dma("gd", b2c, b2r_i, sem="b2", waits=bar2)
        t_b2 = p.full("b2")
        w1qb = [A.alloc(f"w1qb{i}", BF16, 2048) for i in range(2)]
        t_o2z = p.emit("dve", lambda e: e.memset(out2, 0.0), waits=bar2)

        o2_toks = {}

        def ffnA_one(fb, fl):
            fc = fb * 8 + fl
            par = fc % 2
            tl = p.dma("sd", w1qb[par].rearrange("p (b n) -> p b n", b=16),
                       w1q_i[fc],
                       waits=G.use(f"w1qb{par}") + bar2, sem=f"w1l{par}")
            spar = fl % 2
            scr = ps_scr[spar]
            pre = G.use(f"scr{spar}")
            mm = None
            for dc2 in range(DC):
                for ns in range(2):
                    mm = p.emit("pe", lambda e, dc2=dc2, ns=ns: e.matmul(
                        scr[:, ns * 512:(ns + 1) * 512],
                        w1qb[par][:, dc2 * 128:(dc2 + 1) * 128],
                        lnx3T[:, dc2 * TLOC + ns * 512:
                              dc2 * TLOC + (ns + 1) * 512],
                        start=(dc2 == 0), stop=(dc2 == DC - 1)),
                        waits=[tl, t_lnx3T] + pre)
            G.consumed(f"w1qb{par}", [mm])
            ge = p.emit("act", lambda e: e.activation(
                hTb[:, fl * TLOC:(fl + 1) * TLOC], scr[:], AF.Gelu,
                bias=b1c[:, fc:fc + 1]), waits=[mm, t_b1] + G.use(f"hTb{fl}"))
            G.consumed(f"scr{spar}", [ge])
            return ge

        def ffnB_one(fb, tt, ncol, ge_toks, w2l_toks):
            apar = (tt * 4 + ncol) % 2
            pav = ps_av[apar]
            prea = G.use(f"av{apar}")
            mm = None
            for fl in range(8):
                mm = p.emit("pe", lambda e, fl=fl: e.matmul(
                    pav[:, 0:512],
                    hTb[:, fl * TLOC + tt * 128: fl * TLOC + (tt + 1) * 128],
                    w2fb[:, fl * D + ncol * 512: fl * D + (ncol + 1) * 512],
                    start=(fl == 0), stop=False, skip_group_check=True),
                    waits=[ge_toks[fl], w2l_toks[fl]] + prea)
            if fb == 0:
                mm = p.emit("pe", lambda e: e.matmul(
                    pav[:, 0:512], onesrow1[:],
                    b2c[:, ncol * 512:(ncol + 1) * 512],
                    start=False, stop=True, skip_group_check=True),
                    waits=[t_b2, t_ones1])
            mg = p.emit("dve", lambda e: e.tensor_add(
                out2[:, tt * D + ncol * 512: tt * D + (ncol + 1) * 512],
                out2[:, tt * D + ncol * 512: tt * D + (ncol + 1) * 512],
                pav[:, 0:512]),
                waits=[mm, t_o2z, o2_toks.get((tt, ncol))])
            G.consumed(f"av{apar}", [mg])
            o2_toks[(tt, ncol)] = mg
            return mm

        for fb in range(8):
            ge_toks = [ffnA_one(fb, fl) for fl in range(8)]
            w2l_toks = []
            for fl in range(8):
                fc = fb * 8 + fl
                tl = p.dma("gd", w2fb[:, fl * D:(fl + 1) * D], w2q_i[fc],
                           waits=G.use(f"w2fb{fl}") + bar2, sem="w2l")
                w2l_toks.append(tl)
            last = None
            for tt in range(NT):
                for ncol in range(4):
                    last = ffnB_one(fb, tt, ncol, ge_toks, w2l_toks)
            for fl in range(8):
                G.consumed(f"hTb{fl}", [last])
                G.consumed(f"w2fb{fl}", [last])

        st_xo = None
        for tt in range(NT):
            xl = p.dma("sd", xres, x3_s[tt * 128:(tt + 1) * 128, :],
                       waits=G.use("xres") + [p.full("sx1")], sem="xl2")
            ev = p.emit("dve", lambda e, tt=tt: e.tensor_add(
                lnscr, out2[:, tt * D:(tt + 1) * D], xres),
                waits=[xl, o2_toks[(tt, 0)], o2_toks[(tt, 1)],
                       o2_toks[(tt, 2)], o2_toks[(tt, 3)]] + G.use("lnscr"))
            G.consumed("xres", [ev])
            st_xo = p.dma("sd", xout_o[tt * 128:(tt + 1) * 128, :], lnscr,
                          waits=[ev], sem="sxo")
            G.consumed("lnscr", [st_xo])

        p.emit("pool", lambda e: e.nop(), waits=[p.full("sxo")])

        p.build(block)
    return nc


def _rope_tables():
    inv = (1.0 / (10000.0 ** (np.arange(0, ROPE, 2, dtype=np.float64) / ROPE))
           ).astype(np.float32)
    t = np.arange(S, dtype=np.float64)
    freqs = (t[:, None] * inv[None, :].astype(np.float64)).astype(np.float32)
    return (np.cos(freqs.astype(np.float64)).astype(np.float32),
            np.sin(freqs.astype(np.float64)).astype(np.float32))


def _prep_L0(inputs):
    x = np.asarray(inputs["L0_attn_a_x"], np.float32).reshape(S, D)
    cos, sin = _rope_tables()
    iden = np.eye(128, dtype=ml_dtypes.bfloat16)
    g_q = np.ascontiguousarray(
        np.asarray(inputs["L0_attn_a_g_q"], np.float32).reshape(QL // 128, 128).T)
    g_kv = np.ascontiguousarray(
        np.asarray(inputs["L0_attn_a_g_kv"], np.float32).reshape(KVL // 128, 128).T)
    shared = {
        "w_dq": np.asarray(inputs["L0_attn_a_w_dq"], np.float32),
        "w_uq": np.asarray(inputs["L0_attn_a_w_uq"], np.float32),
        "g_q": g_q,
        "w_dkv": np.asarray(inputs["L0_attn_a_w_dkv"], np.float32),
        "g_kv": g_kv,
        "w_uk": np.asarray(inputs["L0_attn_a_w_uk"], np.float32),
        "w_uv": np.asarray(inputs["L0_attn_a_w_uv"], np.float32),
        "w_kr": np.asarray(inputs["L0_attn_a_w_kr"], np.float32),
        "iden": iden,
        "consts": np.tile(np.array([[1e-5, 1e-6, 0.0, 0.0]], np.float32),
                          (128, 1)),
    }
    maps = []
    for c in range(NCORES):
        m = dict(shared)
        m["x"] = np.ascontiguousarray(x[c::NCORES])
        m["cossin"] = np.ascontiguousarray(
            np.concatenate([cos[c::NCORES].T, sin[c::NCORES].T], axis=0))
        maps.append(m)
    return maps




def _prep_L1(inputs, res0):
    H2, T = H // 2, TLOC
    bf = ml_dtypes.bfloat16
    kT_full = np.empty((H, 128, S), bf)
    v_full = np.ones((S, H, 129), bf)
    kpe_full = np.empty((64, S), bf)
    knorm_full = np.empty((H, S), np.float32)
    for c in range(NCORES):
        r = res0[c]
        kT_full[:, :, c::NCORES] = np.asarray(r["kAT"])
        v_full[c::NCORES, :, 0:128] = np.asarray(r["vA"]).reshape(TLOC, H, 128)
        kpe_full[:, c::NCORES] = np.asarray(r["kpeT"])
        knorm_full[:, c::NCORES] = np.asarray(r["knormA"])
    x = np.asarray(inputs["L0_attn_a_x"], np.float32).reshape(S, D)
    w1 = np.asarray(inputs["L0_ffn_w1"], np.float32)
    w2 = np.asarray(inputs["L0_ffn_w2"], np.float32)
    w1T = np.ascontiguousarray(w1.T)
    w1t = np.ascontiguousarray(
        w1T.reshape(16, 128, 64, 128).transpose(2, 1, 0, 3))
    w2t = np.ascontiguousarray(w2.T.reshape(64, 128, D))
    b1 = np.asarray(inputs["L0_ffn_b1"], np.float32)
    b1r = np.ascontiguousarray(b1.reshape(64, 128).T)
    b2r = np.asarray(inputs["L0_ffn_b2"], np.float32).reshape(1, D)
    shared = {
        "kT": kT_full, "v": v_full.reshape(S, H * 129), "kpeT": kpe_full, "knorm": knorm_full,
        "iden": np.eye(128, dtype=bf),
        "consts": np.tile(np.array([[1e-5, 1e-6, 0.0, 0.0]], np.float32),
                          (128, 1)),
        "w_o": np.asarray(inputs["L0_attn_a_w_o"], np.float32),
        "w1t": w1t, "w2t": w2t, "b1r": b1r, "b2r": b2r,
        "WkT": np.ascontiguousarray(
            np.asarray(inputs["L0_attn_b_Wk"], np.float32).T),
        "WvT": np.ascontiguousarray(
            np.asarray(inputs["L0_attn_b_Wv"], np.float32).T),
    }
    maps = []
    for c in range(NCORES):
        r = res0[c]
        qpe = np.asarray(r["qpeT"])
        qp64 = np.empty((64, H * T), bf)
        for h in range(H):
            qp64[:, h * T:(h + 1) * T] = qpe[h // 2, 64 * (h % 2):
                                             64 * (h % 2) + 64, :]
        rho = np.arange(128)[:, None]
        kap = np.arange(1024)[None, :]
        mask = np.where(kap <= 8 * rho + c, 0.0, -1e30).astype(bf)
        m = dict(shared)
        m["x"] = np.ascontiguousarray(x[c::NCORES])
        m["qnT"] = np.asarray(r["qnopeT"])
        m["qpT64"] = qp64
        m["qmax"] = np.asarray(r["qmaxA"])
        m["mask"] = mask
        maps.append(m)
    return maps




def _prep_L2(inputs, res1):
    bf = ml_dtypes.bfloat16
    kT_full = np.empty((H, 128, S), bf)
    v_full = np.ones((S, H, 129), bf)
    for c in range(NCORES):
        r = res1[c]
        kT_full[:, :, c::NCORES] = np.asarray(r["kBT"])
        v_full[c::NCORES, :, 0:128] = np.asarray(r["vB"]).reshape(TLOC, H, 128)
    shared = {
        "kT": kT_full, "v": v_full.reshape(S, H * 129),
        "iden": np.eye(128, dtype=bf),
        "consts": np.tile(np.array([[1e-5, 1e-6, 0.0, 0.0]], np.float32),
                          (128, 1)),
        "WqT": np.ascontiguousarray(
            np.asarray(inputs["L0_attn_b_Wq"], np.float32).T),
        "WoT": np.ascontiguousarray(
            np.asarray(inputs["L0_attn_b_Wo"], np.float32).T),
        "gq": np.asarray(inputs["L0_attn_b_gq"], np.float32).reshape(128, 1),
        "gk": np.asarray(inputs["L0_attn_b_gk"], np.float32).reshape(128, 1),
        "w1q": np.asarray(res1[0]["w1q"]),
        "w2q": np.asarray(res1[0]["w2q"]),
        "b1r": np.ascontiguousarray(
            np.asarray(inputs["L0_ffn_b1"], np.float32).reshape(64, 128).T),
        "b2r": np.asarray(inputs["L0_ffn_b2"], np.float32).reshape(1, D),
    }
    maps = []
    for c in range(NCORES):
        r = res1[c]
        rho = np.arange(128)[:, None]
        kap = np.arange(1024)[None, :]
        mask = np.where(kap <= 8 * rho + c, 0.0, -1e30).astype(bf)
        m = dict(shared)
        m["x2"] = np.asarray(r["x2"])
        m["lnx2T"] = np.asarray(r["lnx2T"])
        m["mask"] = mask
        maps.append(m)
    return maps


def kernel(**inputs):
    maps0 = _prep_L0(inputs)
    res0 = run_bass_kernel_spmd(build_L0(), maps0, core_ids=list(range(NCORES)))
    maps1 = _prep_L1(inputs, res0.results)
    res1 = run_bass_kernel_spmd(build_L1(), maps1, core_ids=list(range(NCORES)))
    maps2 = _prep_L2(inputs, res1.results)
    res2 = run_bass_kernel_spmd(build_L2(), maps2, core_ids=list(range(NCORES)))
    out = np.empty((S, D), np.float32)
    for c in range(NCORES):
        out[c::NCORES] = np.asarray(res2.results[c]["xout"], np.float32)
    return out.reshape(1, S, D)


# version 13 (final)
# speedup vs baseline: 1.0124x; 1.0087x over previous
import os as _osmod
import numpy as np
import ml_dtypes
import concourse.bass as bass
import concourse.mybir as mybir
from concourse.bass_utils import run_bass_kernel_spmd

F32 = mybir.dt.float32
BF16 = mybir.dt.bfloat16
AF = mybir.ActivationFunctionType
ALU = mybir.AluOpType
AX = mybir.AxisListType

NCORES = 8
S = 8192
D = 2048
TLOC = S // NCORES
NT = TLOC // 128
H = 16
NOPE, ROPE, VD = 128, 64, 128
QKD = NOPE + ROPE
QL, KVL = 768, 512
DFF = 8192
DC = D // 128
SCALE_A = QKD ** -0.5
SCALE_B = 128 ** -0.5
SHIFT = 18.0


class Prog:

    QUEUES = {
        "sd": ("sync", 16),
        "ad": ("scalar", 16),
        "gd": ("gpsimd", 16),
        "pe": ("tensor", 1),
        "act": ("scalar", 1),
        "dve": ("vector", 1),
        "pool": ("gpsimd", 1),
    }
    COMPUTE = ("pe", "act", "dve", "pool")

    def __init__(self, nc):
        self.nc = nc
        self.ops = []
        self.count = {q: 0 for q in self.COMPUTE}

    SERIAL = bool(_osmod.environ.get("SERIAL"))

    def emit(self, queue, fn, waits=()):
        assert queue in self.COMPUTE
        if self.SERIAL:
            waits = list(waits) + [(q, v) for q, v in self.count.items() if v]
        w = {}
        for t in waits:
            if t is None:
                continue
            q, v = t
            if q == queue == "pe":
                continue
            w[q] = max(w.get(q, 0), v)
        self.count[queue] += 1
        self.ops.append((queue, fn, w, 1, queue))
        return (queue, self.count[queue])

    def dma(self, queue, out, in_, waits=(), sem=None):
        assert queue in ("sd", "ad", "gd")
        assert sem is not None, "DMA needs a stream semaphore name"
        if self.SERIAL:
            waits = list(waits) + [(q, v) for q, v in self.count.items() if v]
        w = {}
        for t in waits:
            if t is None:
                continue
            q, v = t
            w[q] = max(w.get(q, 0), v)
        self.count[sem] = self.count.get(sem, 0) + 16
        self.ops.append((queue, lambda e, o=out, i=in_: e.dma_start(out=o, in_=i),
                         w, 16, sem))
        return (sem, self.count[sem])

    def full(self, sem):
        return (sem, self.count.get(sem, 0))

    def barrier(self):
        snap = [(q, v) for q, v in self.count.items() if v > 0]
        toks = []
        for q in self.COMPUTE:
            toks.append(self.emit(q, lambda e: e.nop(), waits=snap))
        return toks

    def build(self, block):
        import contextlib
        stack = contextlib.ExitStack()
        handles = {q: stack.enter_context(self.nc.semaphore(f"sem_{q}"))
                   for q in self.count}
        per_engine = {}
        for (queue, fn, w, inc, sem) in self.ops:
            eng = self.QUEUES[queue][0]
            per_engine.setdefault(eng, []).append((fn, w, inc, sem))

        def make_section(ops):
            def section(eng):
                waited = {q: 0 for q in self.count}
                for (fn, w, inc, sem) in ops:
                    for q, v in w.items():
                        if v > waited[q]:
                            eng.wait_ge(handles[q], v)
                            waited[q] = v
                    inst = fn(eng)
                    if isinstance(inst, (list, tuple)):
                        inst = inst[-1]
                    inst.then_inc(handles[sem], inc)
            return section

        for engname, ops in per_engine.items():
            getattr(block, engname)(make_section(ops))
        return stack


class Guard:

    def __init__(self):
        self.last = {}

    def use(self, name):
        return list(self.last.get(name, ()))

    def consumed(self, name, toks):
        self.last[name] = [t for t in toks if t is not None]


class Arena:

    def __init__(self, backing):
        self.b = backing
        self.cap = backing.shape[1]
        self.free_list = [(0, self.cap)]
        self.regions = {}
        self.hwm = 0

    def alloc(self, name, dt, elems):
        units = elems * (2 if dt == F32 else 1)
        units = (units + 1) & ~1
        for i, (a, b) in enumerate(self.free_list):
            if b - a >= units:
                self.free_list[i] = (a + units, b)
                if self.free_list[i][0] == self.free_list[i][1]:
                    self.free_list.pop(i)
                self.regions[name] = (dt, a, units)
                self.hwm = max(self.hwm, max(r[1] + r[2]
                                             for r in self.regions.values()))
                return self.get(name)
        raise MemoryError(f"arena: no room for {name} ({units} units); "
                          f"free={self.free_list}")

    def free(self, name):
        dt, a, n = self.regions.pop(name)
        fl = self.free_list
        fl.append((a, a + n))
        fl.sort()
        merged = []
        for seg in fl:
            if merged and merged[-1][1] == seg[0]:
                merged[-1] = (merged[-1][0], seg[1])
            else:
                merged.append(seg)
        self.free_list = merged

    def get(self, name, rows=128):
        dt, a, n = self.regions[name]
        v = self.b[0:rows, a:a + n]
        return v.bitcast(F32) if dt == F32 else v


def rmsnorm_from_psum(p, ps_ap, dst_ap, scr_ap, stat4, width, eps_ap, waits):
    s1 = p.emit("act", lambda e: e.activation(
        scr_ap, ps_ap, AF.Square, accum_out=stat4[:, 0:1]), waits=waits)
    s2 = p.emit("act", lambda e: e.activation(
        stat4[:, 1:2], stat4[:, 0:1], AF.Ln, bias=eps_ap, scale=1.0 / width),
        waits=[s1])
    s3 = p.emit("act", lambda e: e.activation(
        stat4[:, 1:2], stat4[:, 1:2], AF.Exp, scale=-0.5), waits=[s2])
    s4 = p.emit("dve", lambda e: e.tensor_scalar(
        dst_ap, ps_ap, stat4[:, 1:2], None, ALU.mult), waits=[s3])
    return s1, s4


def rope32(p, x1_ranges, x2_ranges, hi_bf, lo_bf, cos0, sin0, scrA, scrB,
           waits):
    readers, done = [], []
    for (x1, n0, n1), (x2, m0, m1) in zip(x1_ranges, x2_ranges):
        w = n1 - n0
        t1 = p.emit("dve", lambda e, x1=x1, n0=n0, n1=n1: e.tensor_mul(
            scrA[:, n0:n1], x1, cos0[:, n0:n1]), waits=waits)
        t2 = p.emit("dve", lambda e, x2=x2, n0=n0, n1=n1: e.tensor_mul(
            scrB[:, n0:n1], x2, sin0[:, n0:n1]), waits=waits)
        d1 = p.emit("dve", lambda e, n0=n0, n1=n1: e.tensor_sub(
            hi_bf[:, n0:n1], scrA[:, n0:n1], scrB[:, n0:n1]), waits=[t1, t2])
        t3 = p.emit("dve", lambda e, x1=x1, n0=n0, n1=n1: e.tensor_mul(
            scrA[:, n0:n1], x1, sin0[:, n0:n1]), waits=[d1])
        t4 = p.emit("dve", lambda e, x2=x2, n0=n0, n1=n1: e.tensor_mul(
            scrB[:, n0:n1], x2, cos0[:, n0:n1]), waits=[d1])
        d2 = p.emit("dve", lambda e, n0=n0, n1=n1: e.tensor_add(
            lo_bf[:, n0:n1], scrA[:, n0:n1], scrB[:, n0:n1]), waits=[t3, t4])
        readers += [t3, t4]
        done.append(d2)
    return readers, done


def build_L0():
    nc = bass.Bass("TRN2", target_bir_lowering=False, debug=False)
    ap = lambda name, shape, dt, kind: nc.dram_tensor(name, shape, dt, kind=kind).ap()

    x_in = ap("x", [TLOC, D], F32, "ExternalInput")
    w_dq = ap("w_dq", [D, QL], F32, "ExternalInput")
    w_uq = ap("w_uq", [QL, H * QKD], F32, "ExternalInput")
    g_q = ap("g_q", [128, QL // 128], F32, "ExternalInput")
    w_dkv = ap("w_dkv", [D, KVL], F32, "ExternalInput")
    g_kv = ap("g_kv", [128, KVL // 128], F32, "ExternalInput")
    w_uk = ap("w_uk", [KVL, H * NOPE], F32, "ExternalInput")
    w_uv = ap("w_uv", [KVL, H * VD], F32, "ExternalInput")
    w_kr = ap("w_kr", [D, ROPE], F32, "ExternalInput")
    cossin = ap("cossin", [ROPE, TLOC], F32, "ExternalInput")
    iden = ap("iden", [128, 128], BF16, "ExternalInput")
    consts = ap("consts", [128, 4], F32, "ExternalInput")

    import os as _os
    DBG = bool(_os.environ.get("L0DBG"))
    if DBG:
        cqT_o = ap("cqT_dbg", [QL // 128, 128, TLOC], BF16, "ExternalOutput")
        wuq_o = ap("wuq_dbg", [128, 6 * 3072], BF16, "ExternalOutput")
    qnopeT_o = ap("qnopeT", [H, NOPE, TLOC], BF16, "ExternalOutput")
    qpeT_o = ap("qpeT", [H // 2, 128, TLOC], BF16, "ExternalOutput")
    kAT_o = ap("kAT", [H, NOPE, TLOC], BF16, "ExternalOutput")
    vA_o = ap("vA", [TLOC, H * VD], BF16, "ExternalOutput")
    kpeT_o = ap("kpeT", [ROPE, TLOC], BF16, "ExternalOutput")
    knormA_o = ap("knormA", [H, TLOC], F32, "ExternalOutput")
    qmaxA_o = ap("qmaxA", [1, H], F32, "ExternalOutput")

    p = Prog(nc)
    G = Guard()

    with (
        nc.sbuf_tensor("bfA", [128, 94 * 1024], BF16) as bfA,
        nc.sbuf_tensor("stats", [128, 16], F32) as stats,
        nc.sbuf_tensor("idn", [128, 128], BF16) as idn,
        nc.sbuf_tensor("onescol", [128, 1], BF16) as onescol,
        nc.sbuf_tensor("qmax_sb", [1, H], F32) as qmax_sb,
        nc.sbuf_tensor("qtmp", [1, 2], F32) as qtmp,
        nc.sbuf_tensor("csts", [128, 4], F32) as csts,
        nc.sbuf_tensor("knrow", [1, TLOC], F32) as knrow,
        nc.psum_tensor("ps_a", [128, 512], F32) as ps_a,
        nc.psum_tensor("ps_b", [128, 512], F32) as ps_b,
        nc.psum_tensor("ps_w", [128, 1024], F32) as ps_w,
        nc.psum_tensor("ps_x", [128, 1024], F32) as ps_x,
        nc.psum_tensor("ps_t", [128, 128], BF16) as ps_t,
        nc.psum_tensor("ps_n", [1, 512], F32) as ps_n,
        nc.Block() as block,
    ):
        A = Arena(bfA)
        lnxT = A.alloc("lnxT", BF16, DC * TLOC)
        cqT = A.alloc("cqT", BF16, (QL // 128) * TLOC)
        ckvT = A.alloc("ckvT", BF16, (KVL // 128) * TLOC)
        kpe_hi = A.alloc("kpe_hi", BF16, TLOC)
        kpe_lo = A.alloc("kpe_lo", BF16, TLOC)
        cs1 = A.alloc("cos0", F32, TLOC)
        cs2 = A.alloc("sin0", F32, TLOC)
        xt = A.alloc("xt", F32, D)
        lnscr = A.alloc("lnscr", F32, D)
        ropescr = A.alloc("ropescr", F32, TLOC)
        ropescr2 = A.alloc("ropescr2", F32, TLOC)
        wraw = A.alloc("wraw", F32, 2 * 3072)
        lnx = A.alloc("lnx", BF16, D)
        wdq_bf = A.alloc("wdq", BF16, DC * QL)
        wdkv_bf = A.alloc("wdkv", BF16, DC * KVL)
        wkr_bf = A.alloc("wkr", BF16, DC * ROPE)

        p.dma("sd", idn[:], iden, sem="s0")
        p.dma("sd", csts[:], consts, sem="s0")
        cos0 = A.get("cos0", 32)
        sin0 = A.get("sin0", 32)
        p.dma("sd", cos0, cossin[0:32, :], sem="s0")
        p.dma("sd", sin0, cossin[32:64, :], sem="s0")
        t_ones = p.emit("dve", lambda e: e.memset(onescol[:], 1.0))
        p.dma("gd", wdq_bf.rearrange("p (c q) -> p c q", c=DC),
              w_dq.rearrange("(c p) q -> p c q", p=128), sem="gw")
        p.dma("gd", wdkv_bf.rearrange("p (c q) -> p c q", c=DC),
              w_dkv.rearrange("(c p) q -> p c q", p=128), sem="gw")
        p.dma("gd", wkr_bf.rearrange("p (c q) -> p c q", c=DC),
              w_kr.rearrange("(c p) q -> p c q", p=128), sem="gw")
        t_static = p.full("s0")
        t_gw = p.full("gw")
        t_idn = t_cs1 = t_cs2 = t_csts = t_static
        t_wdq = t_wdkv = t_wkr = t_gw

        t_prev = None
        for tt in range(NT):
            tl = p.dma("ad", xt, x_in[tt * 128:(tt + 1) * 128, :],
                       waits=[t_prev], sem="xl")
            sc = stats[:, 0:5]
            t1 = p.emit("dve", lambda e, tl=tl: e.scalar_tensor_tensor(
                lnscr, xt, 1.0, xt, ALU.mult, ALU.mult, accum_out=sc[:, 0:1]),
                waits=[tl])
            t2 = p.emit("dve", lambda e, tl=tl: e.reduce_sum(
                sc[:, 1:2], xt, axis=AX.X), waits=[tl])
            t3 = p.emit("dve", lambda e: e.tensor_scalar_mul(
                sc[:, 1:2], sc[:, 1:2], 1.0 / D), waits=[t2])
            t4 = p.emit("dve", lambda e: e.scalar_tensor_tensor(
                sc[:, 2:3], sc[:, 1:2], 1.0, sc[:, 1:2], ALU.mult, ALU.mult),
                waits=[t3])
            t5 = p.emit("dve", lambda e: e.tensor_scalar(
                sc[:, 0:1], sc[:, 0:1], 1.0 / D, None, ALU.mult), waits=[t1])
            t6 = p.emit("dve", lambda e: e.tensor_sub(
                sc[:, 2:3], sc[:, 0:1], sc[:, 2:3]), waits=[t4, t5])
            t7 = p.emit("act", lambda e: e.activation(
                sc[:, 3:4], sc[:, 2:3], AF.Ln, bias=csts[:, 0:1]),
                waits=[t6, t_csts])
            t8 = p.emit("act", lambda e: e.activation(
                sc[:, 4:5], sc[:, 3:4], AF.Exp, scale=-0.5), waits=[t7])
            t9 = p.emit("dve", lambda e: e.tensor_scalar(
                lnx, xt, sc[:, 1:2], sc[:, 4:5], ALU.subtract, ALU.mult),
                waits=[t8])
            t_prev = t9
            for c in range(DC):
                tp = p.emit("pe", lambda e, c=c: e.transpose(
                    ps_t[:], lnx[:, c * 128:(c + 1) * 128], idn[:]),
                    waits=[t9, t_idn] + G.use("ps_t"))
                ev = p.emit("dve", lambda e, c=c, tt=tt: e.tensor_copy(
                    lnxT[:, c * TLOC + tt * 128: c * TLOC + (tt + 1) * 128],
                    ps_t[:]), waits=[tp])
                G.consumed("ps_t", [ev])
        t_lnxT = ev

        cqtile = A.alloc("cqtile", BF16, QL)
        rms32 = A.alloc("rms32", F32, QL)

        def down_proj(dstT, w_sb, t_w, width, eps_ap):
            lastT = None
            for tt in range(NT):
                mm = None
                pre = G.use("ps_w")
                for c in range(DC):
                    for n0 in range(0, width, 512):
                        n1 = min(width, n0 + 512)
                        mm = p.emit("pe", lambda e, c=c, n0=n0, n1=n1, tt=tt: e.matmul(
                            ps_w[:, n0:n1],
                            lnxT[:, c * TLOC + tt * 128: c * TLOC + (tt + 1) * 128],
                            w_sb[:, c * width + n0: c * width + n1],
                            start=(c == 0), stop=(c == DC - 1)),
                            waits=[t_lnxT, t_w] + pre)
                s1, s4 = rmsnorm_from_psum(
                    p, ps_w[:, 0:width], cqtile[:, 0:width], rms32[:, 0:width],
                    stats[:, 10:12], width, eps_ap, [mm, t_csts] + G.use("cqtile"))
                G.consumed("ps_w", [s1, s4])
                for c in range(width // 128):
                    tp = p.emit("pe", lambda e, c=c: e.transpose(
                        ps_t[:], cqtile[:, c * 128:(c + 1) * 128], idn[:]),
                        waits=[s4] + G.use("ps_t"))
                    lastT = p.emit("dve", lambda e, c=c, tt=tt: e.tensor_copy(
                        dstT[:, c * TLOC + tt * 128: c * TLOC + (tt + 1) * 128],
                        ps_t[:]), waits=[tp])
                    G.consumed("ps_t", [lastT])
                G.consumed("cqtile", [lastT])
            return lastT

        t_cqT = down_proj(cqT, wdq_bf, t_wdq, QL, csts[:, 1:2])
        t_ckvT = down_proj(ckvT, wdkv_bf, t_wdkv, KVL, csts[:, 1:2])

        mmx1 = mmx2 = None
        pre = G.use("ps_x") + G.use("ps_a") + G.use("ps_b")
        for c in range(DC):
            for ns in range(2):
                mmx1 = p.emit("pe", lambda e, c=c, ns=ns: e.matmul(
                    ps_x[0:32, ns * 512:(ns + 1) * 512],
                    wkr_bf[:, c * ROPE:c * ROPE + 32],
                    lnxT[:, c * TLOC + ns * 512: c * TLOC + (ns + 1) * 512],
                    start=(c == 0), stop=(c == DC - 1)),
                    waits=[t_lnxT, t_wkr] + pre)
                mmx2 = p.emit("pe", lambda e, c=c, ns=ns: e.matmul(
                    (ps_a if ns == 0 else ps_b)[0:32, 0:512],
                    wkr_bf[:, c * ROPE + 32:(c + 1) * ROPE],
                    lnxT[:, c * TLOC + ns * 512: c * TLOC + (ns + 1) * 512],
                    start=(c == 0), stop=(c == DC - 1)),
                    waits=[t_lnxT, t_wkr] + pre)
        rsA = A.get("ropescr", 32)
        rsB = A.get("ropescr2", 32)
        kpe_hi_ap = A.get("kpe_hi", 32)
        kpe_lo_ap = A.get("kpe_lo", 32)
        x1r = [(ps_x[0:32, 0:512], 0, 512), (ps_x[0:32, 512:1024], 512, 1024)]
        x2r = [(ps_a[0:32, 0:512], 0, 512), (ps_b[0:32, 0:512], 512, 1024)]
        readers, dones = rope32(p, x1r, x2r, kpe_hi_ap, kpe_lo_ap, cos0, sin0,
                                rsA, rsB, [mmx1, mmx2, t_cs1, t_cs2])
        G.consumed("ps_x", readers)
        G.consumed("ps_a", readers)
        G.consumed("ps_b", readers)
        r_kpe = dones[-1]
        p.dma("sd", kpeT_o[0:32, :], kpe_hi_ap, waits=dones, sem="kp")
        p.dma("sd", kpeT_o[32:64, :], kpe_lo_ap, waits=dones, sem="kp")
        st_kpe = p.full("kp")

        bar = p.barrier()
        A.free("lnxT")
        A.free("wdq")
        A.free("wdkv")
        A.free("wkr")
        A.free("lnx")
        A.free("xt")
        A.free("lnscr")
        wuq_bf = A.alloc("wuq", BF16, 6 * 3072)
        wuk_bf = A.alloc("wuk", BF16, 4 * 2048)
        wuv_bf = A.alloc("wuv", BF16, 4 * 2048)
        gq_sb = A.alloc("gq", F32, 6)
        qhT = A.alloc("qhT", BF16, TLOC)
        khT = A.alloc("khT", BF16, TLOC)
        vat = A.alloc("vat", BF16, H * VD)
        sq = A.alloc("sq", BF16, TLOC)
        qpe_hi = A.alloc("qpe_hi", BF16, TLOC)
        qpe_lo = A.alloc("qpe_lo", BF16, TLOC)
        sqh = A.alloc("sqh", BF16, TLOC)
        sql = A.alloc("sql", BF16, TLOC)
        kpesq_hi = A.alloc("kpesq_hi", BF16, TLOC)
        kpesq_lo = A.alloc("kpesq_lo", BF16, TLOC)
        def b32(name):
            return A.get(name, 32)
        gkv_sb = A.alloc("gkv", F32, 4)
        gq_ap = A.get("gq")
        gkv_ap = A.get("gkv")
        p.dma("sd", gq_ap, g_q, waits=bar, sem="g2")
        p.dma("sd", gkv_ap, g_kv, waits=bar, sem="g2")
        t_gq = t_gkv = p.full("g2")

        def fold_weight(dst_sb, w_hbm, gcol, nchunk, width, t_g):
            tok = None
            for c in range(nchunk):
                par = c % 2
                buf = wraw[:, par * 3072:par * 3072 + width]
                tl = p.dma("sd", buf, w_hbm[c * 128:(c + 1) * 128, :],
                           waits=G.use(f"wraw{par}") + bar, sem=f"wr{par}")
                tok = p.emit("dve", lambda e, b=buf, cc=c: e.tensor_scalar(
                    dst_sb[:, cc * width:(cc + 1) * width], b, gcol[:, cc:cc + 1],
                    None, ALU.mult), waits=[tl, t_g])
                G.consumed(f"wraw{par}", [tok])
            return tok

        t_wuq = fold_weight(wuq_bf, w_uq, gq_ap, 6, 3072, t_gq)
        t_wuk = fold_weight(wuk_bf, w_uk, gkv_ap, 4, 2048, t_gkv)
        t_wuv = fold_weight(wuv_bf, w_uv, gkv_ap, 4, 2048, t_gkv)

        if DBG:
            p.dma("ad", cqT_o.rearrange("b p t -> p b t"),
                  cqT.rearrange("p (b t) -> p b t", b=QL // 128), waits=[t_cqT],
                  sem="dbg")
            p.dma("ad", wuq_o, wuq_bf, waits=[t_wuq], sem="dbg")
        QCH = QL // 128
        st_q_toks = []
        qm_toks = []
        st_qp_toks = []
        for h in range(H):
            mm = None
            pre = G.use("ps_w")
            for c in range(QCH):
                for ns in range(2):
                    mm = p.emit("pe", lambda e, h=h, c=c, ns=ns: e.matmul(
                        ps_w[:, ns * 512:(ns + 1) * 512],
                        wuq_bf[:, c * 3072 + h * QKD: c * 3072 + h * QKD + 128],
                        cqT[:, c * TLOC + ns * 512: c * TLOC + (ns + 1) * 512],
                        start=(c == 0), stop=(c == QCH - 1)),
                        waits=[t_cqT, t_wuq] + pre)
            ev = p.emit("act", lambda e: e.activation(qhT, ps_w[:], AF.Copy),
                        waits=[mm] + G.use("qhT"))
            G.consumed("ps_w", [ev])
            st_q = p.dma("sd", qnopeT_o[h], qhT, waits=[ev], sem="stq")
            st_q_toks.append(st_q)
            mmx1 = mmx2 = None
            prex = G.use("ps_x") + G.use("ps_a") + G.use("ps_b")
            for c in range(QCH):
                for ns in range(2):
                    mmx1 = p.emit("pe", lambda e, h=h, c=c, ns=ns: e.matmul(
                        ps_x[0:32, ns * 512:(ns + 1) * 512],
                        wuq_bf[:, c * 3072 + h * QKD + 128:
                               c * 3072 + h * QKD + 160],
                        cqT[:, c * TLOC + ns * 512: c * TLOC + (ns + 1) * 512],
                        start=(c == 0), stop=(c == QCH - 1)),
                        waits=[t_cqT, t_wuq] + prex)
                    mmx2 = p.emit("pe", lambda e, h=h, c=c, ns=ns: e.matmul(
                        (ps_a if ns == 0 else ps_b)[0:32, 0:512],
                        wuq_bf[:, c * 3072 + h * QKD + 160:
                               c * 3072 + h * QKD + 192],
                        cqT[:, c * TLOC + ns * 512: c * TLOC + (ns + 1) * 512],
                        start=(c == 0), stop=(c == QCH - 1)),
                        waits=[t_cqT, t_wuq] + prex)
            x1r = [(ps_x[0:32, 0:512], 0, 512), (ps_x[0:32, 512:1024], 512, 1024)]
            x2r = [(ps_a[0:32, 0:512], 0, 512), (ps_b[0:32, 0:512], 512, 1024)]
            readers, dones = rope32(
                p, x1r, x2r, b32("qpe_hi"), b32("qpe_lo"), cos0, sin0, rsA, rsB,
                [mmx1, mmx2, t_cs1, t_cs2] + G.use("qpe_hi") + G.use("qpe_lo"))
            G.consumed("ps_x", readers)
            G.consumed("ps_a", readers)
            G.consumed("ps_b", readers)
            base = (h % 2) * 64
            sp1 = p.dma("sd", qpeT_o[h // 2, base:base + 32, :], b32("qpe_hi"),
                        waits=dones, sem="sp1")
            sp2 = p.dma("sd", qpeT_o[h // 2, base + 32:base + 64, :],
                        b32("qpe_lo"), waits=dones, sem="sp2")
            st_qp_toks += [sp1, sp2]
            sqa = p.emit("dve", lambda e: e.tensor_mul(sq, qhT, qhT),
                         waits=[ev] + G.use("sqg"))
            sqb = p.emit("dve", lambda e: e.tensor_mul(
                b32("sqh"), b32("qpe_hi"), b32("qpe_hi")),
                waits=dones + G.use("sqhg"))
            sqc = p.emit("dve", lambda e: e.tensor_mul(
                b32("sql"), b32("qpe_lo"), b32("qpe_lo")),
                waits=dones + G.use("sqlg"))
            G.consumed("qpe_hi", [sp1, sqb])
            G.consumed("qpe_lo", [sp2, sqc])
            reds = []
            mmn = None
            for ns in range(2):
                pren = G.use("ps_n")
                p.emit("pe", lambda e, ns=ns: e.matmul(
                    ps_n[0:1, :], onescol[:],
                    sq[:, ns * 512:(ns + 1) * 512], start=True, stop=False),
                    waits=[sqa, t_ones] + pren)
                p.emit("pe", lambda e, ns=ns: e.matmul(
                    ps_n[0:1, :], onescol[0:32, :],
                    b32("sqh")[:, ns * 512:(ns + 1) * 512], start=False,
                    stop=False), waits=[sqb])
                mmn = p.emit("pe", lambda e, ns=ns: e.matmul(
                    ps_n[0:1, :], onescol[0:32, :],
                    b32("sql")[:, ns * 512:(ns + 1) * 512], start=False,
                    stop=True), waits=[sqc])
                red = p.emit("dve", lambda e, ns=ns: e.reduce_max(
                    qtmp[0:1, ns:ns + 1], ps_n[0:1, :], axis=AX.X), waits=[mmn])
                G.consumed("ps_n", [red])
                reds.append(red)
            red = p.emit("dve", lambda e, h=h: e.tensor_max(
                qmax_sb[0:1, h:h + 1], qtmp[0:1, 0:1], qtmp[0:1, 1:2]), waits=reds)
            G.consumed("sqg", [mmn])
            G.consumed("sqhg", [mmn])
            G.consumed("sqlg", [mmn])
            G.consumed("qhT", [st_q, sqa])
            qm_toks.append(red)
        st_qmax = p.dma("sd", qmaxA_o, qmax_sb[:], waits=[qm_toks[-1]], sem="qm")

        KCH = KVL // 128
        kq1 = p.emit("dve", lambda e: e.tensor_mul(
            b32("kpesq_hi"), kpe_hi_ap, kpe_hi_ap), waits=[r_kpe])
        kq2 = p.emit("dve", lambda e: e.tensor_mul(
            b32("kpesq_lo"), kpe_lo_ap, kpe_lo_ap), waits=[r_kpe])
        st_k_toks = []
        kn_toks = []
        for h in range(H):
            mm = None
            pre = G.use("ps_w")
            for c in range(KCH):
                for ns in range(2):
                    mm = p.emit("pe", lambda e, h=h, c=c, ns=ns: e.matmul(
                        ps_w[:, ns * 512:(ns + 1) * 512],
                        wuk_bf[:, c * 2048 + h * NOPE: c * 2048 + (h + 1) * NOPE],
                        ckvT[:, c * TLOC + ns * 512: c * TLOC + (ns + 1) * 512],
                        start=(c == 0), stop=(c == KCH - 1)),
                        waits=[t_ckvT, t_wuk] + pre)
            ev = p.emit("act", lambda e: e.activation(khT, ps_w[:], AF.Copy),
                        waits=[mm] + G.use("khT"))
            G.consumed("ps_w", [ev])
            st_k = p.dma("sd", kAT_o[h], khT, waits=[ev], sem="stk")
            st_k_toks.append(st_k)
            sqa = p.emit("dve", lambda e: e.tensor_mul(sq, khT, khT),
                         waits=[ev] + G.use("sqg"))
            mmn = None
            cps = []
            for ns in range(2):
                pren = G.use("ps_n")
                p.emit("pe", lambda e, ns=ns: e.matmul(
                    ps_n[0:1, :], onescol[:],
                    sq[:, ns * 512:(ns + 1) * 512], start=True, stop=False),
                    waits=[sqa, t_ones] + pren)
                p.emit("pe", lambda e, ns=ns: e.matmul(
                    ps_n[0:1, :], onescol[0:32, :],
                    b32("kpesq_hi")[:, ns * 512:(ns + 1) * 512], start=False,
                    stop=False), waits=[kq1])
                mmn = p.emit("pe", lambda e, ns=ns: e.matmul(
                    ps_n[0:1, :], onescol[0:32, :],
                    b32("kpesq_lo")[:, ns * 512:(ns + 1) * 512], start=False,
                    stop=True), waits=[kq2])
                cp = p.emit("dve", lambda e, ns=ns: e.tensor_copy(
                    knrow[0:1, ns * 512:(ns + 1) * 512], ps_n[0:1, :]),
                    waits=[mmn] + G.use("knrow"))
                G.consumed("ps_n", [cp])
                cps.append(cp)
            st_kn = p.dma("sd", knormA_o[h:h + 1, :], knrow[0:1, :], waits=cps, sem="skn")
            G.consumed("sqg", [mmn])
            G.consumed("khT", [st_k, sqa])
            G.consumed("knrow", [st_kn])
            kn_toks.append(st_kn)

        st_v = None
        for tt in range(NT):
            evs = []
            for ncol in range(4):
                psu, psb = (("ps_a", ps_a) if ncol % 2 == 0 else ("ps_b", ps_b))
                mm = None
                pre = G.use(psu)
                for c in range(KCH):
                    mm = p.emit("pe", lambda e, tt=tt, c=c, ncol=ncol, psb=psb:
                                e.matmul(
                        psb[:],
                        ckvT[:, c * TLOC + tt * 128: c * TLOC + (tt + 1) * 128],
                        wuv_bf[:, c * 2048 + ncol * 512: c * 2048 + (ncol + 1) * 512],
                        start=(c == 0), stop=(c == KCH - 1)),
                        waits=[t_ckvT, t_wuv, st_v] + pre)
                ev = p.emit("act", lambda e, ncol=ncol, psb=psb: e.activation(
                    vat[:, ncol * 512:(ncol + 1) * 512], psb[:], AF.Copy),
                    waits=[mm])
                G.consumed(psu, [ev])
                evs.append(ev)
            st_v = p.dma("ad", vA_o[tt * 128:(tt + 1) * 128, :], vat, waits=evs,
                         sem="stv")

        p.emit("pool", lambda e: e.nop(),
               waits=[p.full(x) for x in
                      ("stq", "sp1", "sp2", "qm", "stk", "skn", "stv", "kp")])

        p.build(block)
    return nc




def build_L1():
    nc = bass.Bass("TRN2", target_bir_lowering=False, debug=False)
    ap = lambda name, shape, dt, kind: nc.dram_tensor(name, shape, dt, kind=kind).ap()

    x_in = ap("x", [TLOC, D], F32, "ExternalInput")
    qn_i = ap("qnT", [H, 128, TLOC], BF16, "ExternalInput")
    qp_i = ap("qpT64", [64, H * TLOC], BF16, "ExternalInput")
    kT_i = ap("kT", [H, 128, S], BF16, "ExternalInput")
    v_i = ap("v", [S, H * 129], BF16, "ExternalInput")
    kpe_i = ap("kpeT", [64, S], BF16, "ExternalInput")
    knorm_i = ap("knorm", [H, S], F32, "ExternalInput")
    qmax_i = ap("qmax", [1, H], F32, "ExternalInput")
    mask_i = ap("mask", [128, 1024], BF16, "ExternalInput")
    iden = ap("iden", [128, 128], BF16, "ExternalInput")
    consts = ap("consts", [128, 4], F32, "ExternalInput")
    wo_i = ap("w_o", [D, D], F32, "ExternalInput")
    w1t_i = ap("w1t", [64, 128, 16, 128], F32, "ExternalInput")
    w2t_i = ap("w2t", [64, 128, D], F32, "ExternalInput")
    b1r_i = ap("b1r", [128, 64], F32, "ExternalInput")
    b2r_i = ap("b2r", [1, D], F32, "ExternalInput")
    wkT_i = ap("WkT", [D, D], F32, "ExternalInput")
    wvT_i = ap("WvT", [D, D], F32, "ExternalInput")

    x2_o = ap("x2", [TLOC, D], F32, "ExternalOutput")
    lnx2T_o = ap("lnx2T", [DC, 128, TLOC], BF16, "ExternalOutput")
    kBT_o = ap("kBT", [H, 128, TLOC], BF16, "ExternalOutput")
    vB_o = ap("vB", [TLOC, D], BF16, "ExternalOutput")
    w1q_o = ap("w1q", [64, 128, 16, 128], BF16, "ExternalOutput")
    w2q_o = ap("w2q", [64, 128, D], BF16, "ExternalOutput")

    L1DBG = bool(_osmod.environ.get("L1DBG"))
    if L1DBG:
        x1dbg_o = ap("x1dbg", [TLOC, D], F32, "ExternalOutput")
        callb_o = ap("callbdbg", [128, H], F32, "ExternalOutput")
        attT_o = ap("attTdbg", [DC, 128, TLOC], BF16, "ExternalOutput")
        acc_o = ap("accdbg", [128, NT * H * 129], F32, "ExternalOutput")
    x1_s = nc.dram_tensor("x1_s", [TLOC, D], F32, kind="Internal").ap()
    km_s = nc.dram_tensor("km_s", [H, 1], F32, kind="Internal").ap()

    p = Prog(nc)
    G = Guard()

    with (
        nc.sbuf_tensor("bfA", [128, 105600], BF16) as bfA,
        nc.sbuf_tensor("stats", [128, 36], F32) as stats,
        nc.sbuf_tensor("idn", [128, 128], BF16) as idn,
        nc.sbuf_tensor("csts", [128, 4], F32) as csts,
        nc.sbuf_tensor("onescol", [128, 1], BF16) as onescol,
        nc.sbuf_tensor("onesrow1", [1, 128], BF16) as onesrow1,
        nc.sbuf_tensor("taus", [128, 4], F32) as taus,
        nc.sbuf_tensor("callb", [128, H], F32) as callb,
        nc.sbuf_tensor("brow", [1, 160], F32) as brow,
        nc.sbuf_tensor("acc16", [128, 32], F32) as acc16,
        nc.psum_tensor("ps_s0", [128, 1024], F32) as ps_s0,
        nc.psum_tensor("ps_s1", [128, 1024], F32) as ps_s1,
        nc.psum_tensor("ps_a", [128, 512], F32) as ps_a,
        nc.psum_tensor("ps_b", [128, 512], F32) as ps_b,
        nc.psum_tensor("ps_t", [128, 128], BF16) as ps_t,
        nc.psum_tensor("ps_n", [1, 512], F32) as ps_n,
        nc.Block() as block,
    ):
        A = Arena(bfA)
        ps_scr = [ps_s0, ps_s1]
        ps_av = [ps_a, ps_b]

        def reg(name, rows=128):
            return A.get(name, rows)

        p.dma("sd", idn[:], iden, sem="s0")
        p.dma("sd", csts[:], consts, sem="s0")
        mask_sb = A.alloc("mask", BF16, 1024)
        p.dma("sd", mask_sb, mask_i, sem="s0")
        qmaxrow = A.alloc("qmaxrow", F32, 16)
        qmr = reg("qmaxrow", 1)
        p.dma("sd", qmr, qmax_i, sem="s0")
        t_static = p.full("s0")
        t_ones = p.emit("dve", lambda e: e.memset(onescol[:], 1.0))
        t_ones1 = p.emit("dve", lambda e: e.memset(onesrow1[:], 1.0))
        onesrowF = brow[0:1, 32:160]
        onescolF = stats[:, 32:33]
        t_onesF = p.emit("dve", lambda e: e.memset(onesrowF, 1.0))
        t_onescF = p.emit("dve", lambda e: e.memset(onescolF, 1.0))

        knorm_sb = A.alloc("knorm", F32, S)
        kn16 = reg("knorm", H)
        p.dma("ad", kn16, knorm_i, sem="kn")
        kmax16 = A.alloc("kmax16", F32, 1)
        km_col = reg("kmax16", H)
        t1 = p.emit("dve", lambda e: e.reduce_max(km_col, kn16, axis=AX.X),
                    waits=[p.full("kn")])
        p.dma("sd", km_s, km_col, waits=[t1], sem="km1")
        kmrow = A.alloc("kmrow", F32, 16)
        kmr = reg("kmrow", 1)
        p.dma("sd", kmr, km_s.rearrange("h o -> o h"),
              waits=[p.full("km1")], sem="km2")
        b1_ = p.emit("dve", lambda e: e.tensor_mul(brow[0:1, 0:16], qmr, kmr),
                     waits=[p.full("km2"), t_static])
        b2_ = p.emit("act", lambda e: e.activation(
            brow[0:1, 0:16], brow[0:1, 0:16], AF.Sqrt), waits=[b1_])
        b3_ = p.emit("dve", lambda e: e.tensor_scalar(
            brow[0:1, 0:16], brow[0:1, 0:16], -SCALE_A * 1.02, SHIFT, ALU.mult,
            ALU.add), waits=[b2_])
        bm_ = p.emit("pe", lambda e: e.matmul(
            ps_a[:, 0:16], onesrowF, brow[0:1, 0:16], start=True, stop=True),
            waits=[b3_, t_onesF])
        t_call = p.emit("dve", lambda e: e.tensor_copy(callb[:], ps_a[:, 0:16]),
                        waits=[bm_])
        G.consumed("av0", [t_call])
        A.free("knorm")
        A.free("kmax16")
        A.free("kmrow")
        A.free("qmaxrow")

        qn_sb = A.alloc("qn", BF16, H * TLOC)
        A.alloc("qp", BF16, H * TLOC)
        qp64 = reg("qp", 64)
        kbuf = [A.alloc(f"kb{i}", BF16, H * 512) for i in range(2)]
        for i in range(2):
            A.alloc(f"kpeb{i}", BF16, 512)
        qp128 = reg("qp", 128)
        t_qz = p.emit("dve", lambda e: e.memset(qp128[64:128, :], 0.0))
        kpe128b = [reg(f"kpeb{i}", 128) for i in range(2)]
        t_kpz = [p.emit("dve", lambda e, i=i: e.memset(
            kpe128b[i][64:128, :], 0.0)) for i in range(2)]
        vbuf = [A.alloc(f"vb{i}", BF16, H * 129) for i in range(4)]
        ptil = [A.alloc(f"pt{i}", BF16, 1024) for i in range(4)]
        acc = A.alloc("acc", F32, NT * H * 129)
        wr1b = [A.alloc(f"w1r{i}", F32, 1024) for i in range(3)]
        wqb = [A.alloc("wq0", BF16, 2048)]

        p.dma("ad", qn_sb.rearrange("p (h t) -> p h t", h=H),
              qn_i.rearrange("h p t -> p h t"), sem="q0")
        p.dma("ad", qp64, qp_i, sem="q0")
        t_q = p.full("q0")
        t_accz = p.emit("dve", lambda e: e.memset(acc, 0.0))

        def wchunk_half(w4, ci, hf, dst):
            if w4 is w2t_i:
                return dst, w4[ci][:, hf * 1024:(hf + 1) * 1024]
            return (dst.rearrange("p (c n) -> p c n", c=8),
                    w4[ci][:, hf * 8:(hf + 1) * 8, :])

        zt = p.emit("dve", lambda e: e.memset(acc16[:], 0.0))
        scale_state = {"tok": None}
        tau_tok = {"t": None}
        load_toks = {}
        pend_qc = {}
        q_w1 = []
        q_w2 = []
        NWB = 3

        jobs = []
        for w4, col in ((w1t_i, 0), (w2t_i, 16)):
            for ci in range(64):
                for hf in range(2):
                    jobs.append(("s", w4, ci, hf, col))
        for w4, wqo, tcol, ncol, sl in ((w1t_i, w1q_o, 0, 2, q_w1),
                                        (w2t_i, w2q_o, 1, 3, q_w2)):
            for ci in range(64):
                for hf in range(2):
                    jobs.append(("q", w4, ci, hf, (wqo, tcol, ncol, sl)))

        def job_load(u):
            kind, w4, ci, hf, _ = jobs[u]
            par = u % NWB
            dd, ss = wchunk_half(w4, ci, hf, wr1b[par])
            load_toks[u] = p.dma(("gd", "ad")[u % 2], dd, ss,
                                 waits=G.use(f"w1r{par}"), sem=f"sc{u % 6}")

        def emit_tau_chain():
            s_last = scale_state["tok"]
            r1_ = p.emit("dve", lambda e: e.reduce_sum(
                stats[:, 0:1], acc16[:, 0:8], axis=AX.X), waits=[s_last])
            r2_ = p.emit("dve", lambda e: e.reduce_sum(
                stats[:, 1:2], acc16[:, 16:24], axis=AX.X), waits=[s_last])
            m1_ = p.emit("pe", lambda e: e.matmul(
                ps_n[0:1, 0:2], onescolF, stats[:, 0:2], start=True, stop=True),
                waits=[r1_, r2_, t_onescF] + G.use("ps_n"))
            t1_ = p.emit("dve", lambda e: e.tensor_scalar(
                stats[0:1, 4:6], ps_n[0:1, 0:2], 0.7 / (DFF * D), 0.0, ALU.mult,
                ALU.add), waits=[m1_])
            G.consumed("ps_n", [t1_])
            bm2 = p.emit("pe", lambda e: e.matmul(
                ps_a[:, 0:2], onesrowF, stats[0:1, 4:6], start=True, stop=True),
                waits=[t1_, t_onesF] + G.use("av0"))
            t2_ = p.emit("dve", lambda e: e.tensor_copy(
                taus[:, 0:2], ps_a[:, 0:2]), waits=[bm2])
            G.consumed("av0", [t2_])
            return p.emit("dve", lambda e: e.tensor_scalar(
                taus[:, 2:4], taus[:, 0:2], -1.0, 0.0, ALU.mult, ALU.add),
                waits=[t2_])

        def job_compute(u):
            kind, w4, ci, hf, x = jobs[u]
            par = u % NWB
            buf = wr1b[par]
            tl = load_toks[u]
            if kind == "s":
                col = x
                rtk = p.emit("dve", lambda e, b=buf: e.reduce_sum(
                    stats[:, 16:24], b.rearrange("p (c n) -> p c n", c=8),
                    axis=AX.X, apply_absolute_value=True),
                    waits=[tl] + G.use("scr16"))
                tok = p.emit("dve", lambda e, col=col: e.tensor_add(
                    acc16[:, col:col + 8], acc16[:, col:col + 8],
                    stats[:, 16:24]), waits=[rtk, scale_state["tok"], zt])
                G.consumed("scr16", [tok])
                G.consumed(f"w1r{par}", [tok])
                scale_state["tok"] = tok
            else:
                wqo, tcol, ncol, sl = x
                half = wqb[0][:, hf * 1024:(hf + 1) * 1024]
                qb_ = p.emit("dve", lambda e, b=buf, half=half, ncol=ncol:
                             e.tensor_scalar(
                    half, b, taus[:, ncol:ncol + 1], None, ALU.is_le),
                    waits=[tl, tau_tok["t"]]
                    + (G.use("wq0") if hf == 0 else []))
                qc_ = p.emit("dve", lambda e, b=buf, half=half, tcol=tcol:
                             e.scalar_tensor_tensor(
                    half, b, taus[:, tcol:tcol + 1], half, ALU.is_ge,
                    ALU.subtract), waits=[qb_])
                G.consumed(f"w1r{par}", [qc_])
                pend_qc[hf] = qc_
                if hf == 1:
                    st = p.dma("sd", wqo[ci],
                               (wqb[0] if w4 is w2t_i else
                                wqb[0].rearrange("p (b n) -> p b n", b=16)),
                               waits=[pend_qc[0], pend_qc[1]], sem="wqs0")
                    G.consumed("wq0", [st])
                    sl.append(st)

        drain = {"u": 0}

        def drain_slots(n):
            for _ in range(n):
                u = drain["u"]
                if u >= len(jobs):
                    return
                job_compute(u)
                if u + NWB < len(jobs):
                    job_load(u + NWB)
                drain["u"] += 1
                if u == 255:
                    tau_tok["t"] = emit_tau_chain()

        for u0 in range(NWB):
            job_load(u0)

        merge_toks = {}
        vload = [None] * 4

        def scores_one(kb, h, kt, kpar, kpe64, tkl, tpl, jmin, Nq):
            spar = kt % 2
            scr = ps_scr[spar]
            pre = G.use(f"scr{spar}")
            mm = None
            for n0 in range(0, Nq, 512):
                n1 = min(Nq, n0 + 512)
                mm = p.emit("pe", lambda e, n0=n0, n1=n1: e.matmul(
                    scr[:, n0:n1],
                    kbuf[kpar][:, h * 512 + kt * 128: h * 512 + (kt + 1) * 128],
                    qn_sb[:, h * TLOC + 128 * jmin + n0:
                          h * TLOC + 128 * jmin + n1],
                    start=True, stop=False, skip_group_check=True),
                    waits=[tkl, t_q] + pre)
                mm = p.emit("pe", lambda e, n0=n0, n1=n1: e.matmul(
                    scr[:, n0:n1],
                    kpe128b[kpar][:, kt * 128:(kt + 1) * 128],
                    qp128[:, h * TLOC + 128 * jmin + n0:
                          h * TLOC + 128 * jmin + n1],
                    start=False, stop=False, skip_group_check=True),
                    waits=[tpl, t_qz, t_kpz[kpar]])
            mm = p.emit("pe", lambda e: e.matmul(
                scr[:, 0:128],
                mask_sb[:, (kb % 2) * 512 + kt * 128:
                        (kb % 2) * 512 + (kt + 1) * 128],
                idn[:], start=False, stop=True, skip_group_check=True),
                waits=[t_static])
            ex = p.emit("act", lambda e: e.activation(
                ptil[kt][:, 0:Nq], scr[:, 0:Nq], AF.Exp,
                bias=callb[:, h:h + 1], scale=SCALE_A),
                waits=[mm, t_call] + G.use(f"pt{kt}"))
            G.consumed(f"scr{spar}", [ex])
            return ex

        def av_one(j, h, jmin, exp_toks):
            apar = j % 2
            pav = ps_av[apar]
            prea = G.use(f"av{apar}")
            mm = None
            for kt in range(4):
                mm = p.emit("pe", lambda e, kt=kt: e.matmul(
                    pav[:, 0:129],
                    ptil[kt][:, (j - jmin) * 128:(j - jmin + 1) * 128],
                    vbuf[kt][:, h * 129:(h + 1) * 129],
                    start=(kt == 0), stop=(kt == 3)),
                    waits=[exp_toks[kt], vload[kt]] + prea)
            mg = p.emit("dve", lambda e: e.tensor_add(
                acc[:, (j * H + h) * 129:(j * H + h + 1) * 129],
                acc[:, (j * H + h) * 129:(j * H + h + 1) * 129],
                pav[:, 0:129]),
                waits=[mm, t_accz, merge_toks.get((j, h))])
            G.consumed(f"av{apar}", [mg])
            merge_toks[(j, h)] = mg
            return mm

        for kb in range(16):
            jmin = kb // 2
            Nq = TLOC - 128 * jmin
            kpar = kb % 2
            tkl = p.dma("sd", kbuf[kpar].rearrange("p (h n) -> p h n", h=H),
                        kT_i.rearrange("h p n -> p h n")
                        [:, :, 512 * kb:512 * (kb + 1)],
                        waits=G.use(f"kb{kpar}") + [t_accz], sem=f"kl{kpar}")
            kpe64 = reg(f"kpeb{kpar}", 64)
            tpl = p.dma("sd", kpe64, kpe_i[0:64, 512 * kb:512 * (kb + 1)],
                        waits=G.use(f"kpeb{kpar}"), sem=f"kpl{kpar}")
            for kt in range(4):
                vload[kt] = p.dma(
                    "sd", vbuf[kt],
                    v_i.rearrange("(a p) d -> p a d", p=128)[:, 4 * kb + kt, :],
                    waits=G.use(f"vb{kt}"), sem=f"vl{kt}")
            last_mm = None
            for h in range(H):
                exp_toks = [scores_one(kb, h, kt, kpar, kpe64, tkl, tpl,
                                       jmin, Nq) for kt in range(4)]
                for j in range(jmin, NT):
                    last_mm = av_one(j, h, jmin, exp_toks)
                for kt in range(4):
                    G.consumed(f"pt{kt}", [last_mm])
                drain_slots(2)
            for kt in range(4):
                G.consumed(f"vb{kt}", [last_mm])
            G.consumed(f"kb{kpar}", [last_mm])
            G.consumed(f"kpeb{kpar}", [last_mm])

        drain_slots(len(jobs))

        bar_f = p.barrier()
        A.free("qn")
        A.free("qp")
        for i in range(2):
            A.free(f"kb{i}")
            A.free(f"kpeb{i}")
        for i in range(4):
            A.free(f"vb{i}")
            A.free(f"pt{i}")
        attT = A.alloc("attT", BF16, DC * TLOC)
        att_bf = A.alloc("att_bf", BF16, D)
        linv = stats[:, 8:9]
        t_attT = None
        for tt in range(NT):
            sc_toks = []
            for h in range(H):
                base = (tt * H + h) * 129
                l1 = p.emit("dve", lambda e, base=base: e.tensor_scalar_max(
                    stats[:, 7:8], acc[:, base + 128:base + 129], 1e-30),
                    waits=[merge_toks[(tt, h)]] + G.use("linv") + bar_f)
                l2 = p.emit("dve", lambda e: e.reciprocal(linv, stats[:, 7:8]),
                            waits=[l1])
                sc = p.emit("dve", lambda e, base=base, h=h: e.tensor_scalar(
                    att_bf[:, h * 128:(h + 1) * 128], acc[:, base:base + 128],
                    linv, None, ALU.mult), waits=[l2] + G.use("attbf"))
                G.consumed("linv", [sc])
                sc_toks.append(sc)
            for c in range(DC):
                tp = p.emit("pe", lambda e, c=c: e.transpose(
                    ps_t[:], att_bf[:, c * 128:(c + 1) * 128], idn[:]),
                    waits=[sc_toks[-1], t_static] + G.use("ps_t"))
                t_attT = p.emit("dve", lambda e, c=c, tt=tt: e.tensor_copy(
                    attT[:, c * TLOC + tt * 128: c * TLOC + (tt + 1) * 128],
                    ps_t[:]), waits=[tp])
                G.consumed("ps_t", [t_attT])
            G.consumed("attbf", [t_attT])

        bar_w = p.barrier()
        A.free("acc")
        wo_res = A.alloc("wo_res", BF16, DC * D)
        for c in range(DC):
            p.dma("gd", wo_res[:, c * D:(c + 1) * D],
                  wo_i[c * 128:(c + 1) * 128, :], sem="wol", waits=bar_w)
        t_wo = p.full("wol")
        xres = A.alloc("xres", F32, D)
        lnscr = A.alloc("lnscr", F32, D)
        lnx1 = A.alloc("lnx1", BF16, D)
        lnx1T = A.alloc("lnx1T", BF16, DC * TLOC)
        x1t = A.alloc("x1t", F32, D)

        t_lnx1T = None
        for tt in range(NT):
            xl = p.dma("sd", xres, x_in[tt * 128:(tt + 1) * 128, :],
                       waits=G.use("xres") + bar_w, sem="xl")
            evs = []
            for ncol in range(4):
                apar = ncol % 2
                pav = ps_av[apar]
                prea = G.use(f"av{apar}")
                mm = None
                for c in range(DC):
                    mm = p.emit("pe", lambda e, tt=tt, c=c, ncol=ncol, pav=pav:
                                e.matmul(
                        pav[:, 0:512],
                        attT[:, c * TLOC + tt * 128: c * TLOC + (tt + 1) * 128],
                        wo_res[:, c * D + ncol * 512: c * D + (ncol + 1) * 512],
                        start=(c == 0), stop=(c == DC - 1)),
                        waits=[t_attT, t_wo] + prea)
                ev = p.emit("dve", lambda e, ncol=ncol, pav=pav:
                            e.scalar_tensor_tensor(
                    x1t[:, ncol * 512:(ncol + 1) * 512], pav[:, 0:512], 1.0,
                    xres[:, ncol * 512:(ncol + 1) * 512], ALU.mult, ALU.add),
                    waits=[mm, xl] + (G.use("x1t") if ncol == 0 else []))
                G.consumed(f"av{apar}", [ev])
                evs.append(ev)
            G.consumed("xres", evs)
            st_x1 = p.dma("sd", x1_s[tt * 128:(tt + 1) * 128, :], x1t,
                          waits=evs, sem="sx1")
            sc = stats[:, 0:5]
            t1 = p.emit("act", lambda e: e.activation(
                lnscr, x1t, AF.Square, accum_out=sc[:, 0:1]), waits=evs)
            t2 = p.emit("dve", lambda e: e.reduce_sum(
                sc[:, 1:2], x1t, axis=AX.X), waits=evs)
            t3 = p.emit("dve", lambda e: e.tensor_scalar_mul(
                sc[:, 1:2], sc[:, 1:2], 1.0 / D), waits=[t2])
            t4 = p.emit("dve", lambda e: e.scalar_tensor_tensor(
                sc[:, 2:3], sc[:, 1:2], 1.0, sc[:, 1:2], ALU.mult, ALU.mult),
                waits=[t3])
            t5 = p.emit("dve", lambda e: e.tensor_scalar(
                sc[:, 0:1], sc[:, 0:1], 1.0 / D, None, ALU.mult), waits=[t1])
            t6 = p.emit("dve", lambda e: e.tensor_sub(
                sc[:, 2:3], sc[:, 0:1], sc[:, 2:3]), waits=[t4, t5])
            t7 = p.emit("act", lambda e: e.activation(
                sc[:, 3:4], sc[:, 2:3], AF.Ln, bias=csts[:, 0:1]),
                waits=[t6, t_static])
            t8 = p.emit("act", lambda e: e.activation(
                sc[:, 4:5], sc[:, 3:4], AF.Exp, scale=-0.5), waits=[t7])
            t9 = p.emit("dve", lambda e: e.tensor_scalar(
                lnx1, x1t, sc[:, 1:2], sc[:, 4:5], ALU.subtract, ALU.mult),
                waits=[t8] + G.use("lnx1"))
            G.consumed("x1t", [t9, st_x1, t1, t2])
            tpd = None
            for c in range(DC):
                tp = p.emit("pe", lambda e, c=c: e.transpose(
                    ps_t[:], lnx1[:, c * 128:(c + 1) * 128], idn[:]),
                    waits=[t9, t_static] + G.use("ps_t"))
                tpd = p.emit("dve", lambda e, c=c, tt=tt: e.tensor_copy(
                    lnx1T[:, c * TLOC + tt * 128: c * TLOC + (tt + 1) * 128],
                    ps_t[:]), waits=[tp])
                G.consumed("ps_t", [tpd])
            G.consumed("lnx1", [tpd])
            t_lnx1T = tpd

        if L1DBG:
            p.dma("ad", acc_o, acc,
                  waits=[merge_toks[(j, h)] for j in range(NT)
                         for h in range(H)], sem="dbg")
            p.dma("ad", x1dbg_o, x1_s, waits=[p.full("sx1")], sem="dbg")
            p.dma("ad", callb_o, callb[:], waits=[t_call], sem="dbg")
            p.dma("ad", attT_o.rearrange("c p t -> p c t"),
                  attT.rearrange("p (c t) -> p c t", c=DC), waits=[t_attT],
                  sem="dbg")
        bar2 = p.barrier()
        A.free("attT")
        A.free("att_bf")
        A.free("wo_res")
        A.free("mask")
        A.free("x1t")
        for i in range(2):
            A.free(f"w1r{i}")
        A.free("wq0")
        out2 = A.alloc("out2", F32, NT * D)
        hTb = A.alloc("hTb", BF16, 8 * TLOC)
        w2fb = A.alloc("w2fb", BF16, 8 * D)
        b1sb = A.alloc("b1sb", F32, 64)
        b1c = reg("b1sb")
        p.dma("sd", b1c, b1r_i, sem="b1", waits=bar2)
        t_b1 = p.full("b1")
        b2sb = A.alloc("b2sb", BF16, D)
        b2c = reg("b2sb", 1)
        p.dma("gd", b2c, b2r_i, sem="b2", waits=bar2)
        t_b2 = p.full("b2")
        w1qb = [A.alloc(f"w1qb{i}", BF16, 2048) for i in range(2)]
        t_o2z = p.emit("dve", lambda e: e.memset(out2, 0.0),
                       waits=bar2 + ([p.full("dbg")] if L1DBG else []))

        o2_toks = {}

        def ffnA_one(fb, fl):
            fc = fb * 8 + fl
            par = fc % 2
            tl = p.dma("sd", w1qb[par].rearrange("p (b n) -> p b n", b=16),
                       w1q_o[fc],
                       waits=G.use(f"w1qb{par}") + [q_w1[fc]] + bar2,
                       sem=f"w1l{par}")
            spar = fl % 2
            scr = ps_scr[spar]
            pre = G.use(f"scr{spar}")
            mm = None
            for dc2 in range(DC):
                for ns in range(2):
                    mm = p.emit("pe", lambda e, dc2=dc2, ns=ns: e.matmul(
                        scr[:, ns * 512:(ns + 1) * 512],
                        w1qb[par][:, dc2 * 128:(dc2 + 1) * 128],
                        lnx1T[:, dc2 * TLOC + ns * 512:
                              dc2 * TLOC + (ns + 1) * 512],
                        start=(dc2 == 0), stop=(dc2 == DC - 1)),
                        waits=[tl, t_lnx1T] + pre)
            G.consumed(f"w1qb{par}", [mm])
            ge = p.emit("act", lambda e: e.activation(
                hTb[:, fl * TLOC:(fl + 1) * TLOC], scr[:], AF.Gelu,
                bias=b1c[:, fc:fc + 1]), waits=[mm, t_b1] + G.use(f"hTb{fl}"))
            G.consumed(f"scr{spar}", [ge])
            return ge

        def ffnB_one(fb, tt, ncol, ge_toks, w2l_toks):
            apar = (tt * 4 + ncol) % 2
            pav = ps_av[apar]
            prea = G.use(f"av{apar}")
            mm = None
            for fl in range(8):
                mm = p.emit("pe", lambda e, fl=fl: e.matmul(
                    pav[:, 0:512],
                    hTb[:, fl * TLOC + tt * 128: fl * TLOC + (tt + 1) * 128],
                    w2fb[:, fl * D + ncol * 512: fl * D + (ncol + 1) * 512],
                    start=(fl == 0), stop=False, skip_group_check=True),
                    waits=[ge_toks[fl], w2l_toks[fl]] + prea)
            if fb == 0:
                mm = p.emit("pe", lambda e: e.matmul(
                    pav[:, 0:512], onesrow1[:],
                    b2c[:, ncol * 512:(ncol + 1) * 512],
                    start=False, stop=True, skip_group_check=True),
                    waits=[t_b2, t_ones1])
            mg = p.emit("dve", lambda e: e.tensor_add(
                out2[:, tt * D + ncol * 512: tt * D + (ncol + 1) * 512],
                out2[:, tt * D + ncol * 512: tt * D + (ncol + 1) * 512],
                pav[:, 0:512]),
                waits=[mm, t_o2z, o2_toks.get((tt, ncol))])
            G.consumed(f"av{apar}", [mg])
            o2_toks[(tt, ncol)] = mg
            return mm

        for fb in range(8):
            ge_toks = [ffnA_one(fb, fl) for fl in range(8)]
            w2l_toks = []
            for fl in range(8):
                fc = fb * 8 + fl
                tl = p.dma("gd", w2fb[:, fl * D:(fl + 1) * D], w2q_o[fc],
                           waits=G.use(f"w2fb{fl}") + [q_w2[fc]] + bar2,
                           sem=f"w2l{fl}")
                w2l_toks.append(tl)
            last = None
            for tt in range(NT):
                for ncol in range(4):
                    last = ffnB_one(fb, tt, ncol, ge_toks, w2l_toks)
            for fl in range(8):
                G.consumed(f"hTb{fl}", [last])
                G.consumed(f"w2fb{fl}", [last])

        bar3 = p.barrier()
        A.free("lnx1T")
        A.free("hTb")
        A.free("w2fb")
        A.free("b1sb")
        for i in range(2):
            A.free(f"w1qb{i}")
        x2t = A.alloc("x2t", F32, D)
        lnx2 = A.alloc("lnx2", BF16, D)
        lnx2T = A.alloc("lnx2T", BF16, DC * TLOC)
        t_lnx2T = None
        for tt in range(NT):
            xl = p.dma("sd", xres, x1_s[tt * 128:(tt + 1) * 128, :],
                       waits=G.use("xres") + [p.full("sx1")], sem="xl2")
            ev = p.emit("dve", lambda e, tt=tt: e.tensor_add(
                x2t, out2[:, tt * D:(tt + 1) * D], xres),
                waits=[xl, o2_toks[(tt, 0)], o2_toks[(tt, 1)],
                       o2_toks[(tt, 2)], o2_toks[(tt, 3)]] + G.use("x2t")
                + bar3)
            G.consumed("xres", [ev])
            st_x2 = p.dma("sd", x2_o[tt * 128:(tt + 1) * 128, :], x2t,
                          waits=[ev], sem="sx2")
            sc = stats[:, 0:5]
            t1 = p.emit("act", lambda e: e.activation(
                lnscr, x2t, AF.Square, accum_out=sc[:, 0:1]), waits=[ev])
            t2 = p.emit("dve", lambda e: e.reduce_sum(
                sc[:, 1:2], x2t, axis=AX.X), waits=[ev])
            t3 = p.emit("dve", lambda e: e.tensor_scalar_mul(
                sc[:, 1:2], sc[:, 1:2], 1.0 / D), waits=[t2])
            t4 = p.emit("dve", lambda e: e.scalar_tensor_tensor(
                sc[:, 2:3], sc[:, 1:2], 1.0, sc[:, 1:2], ALU.mult, ALU.mult),
                waits=[t3])
            t5 = p.emit("dve", lambda e: e.tensor_scalar(
                sc[:, 0:1], sc[:, 0:1], 1.0 / D, None, ALU.mult), waits=[t1])
            t6 = p.emit("dve", lambda e: e.tensor_sub(
                sc[:, 2:3], sc[:, 0:1], sc[:, 2:3]), waits=[t4, t5])
            t7 = p.emit("act", lambda e: e.activation(
                sc[:, 3:4], sc[:, 2:3], AF.Ln, bias=csts[:, 0:1]),
                waits=[t6, t_static])
            t8 = p.emit("act", lambda e: e.activation(
                sc[:, 4:5], sc[:, 3:4], AF.Exp, scale=-0.5), waits=[t7])
            t9 = p.emit("dve", lambda e: e.tensor_scalar(
                lnx2, x2t, sc[:, 1:2], sc[:, 4:5], ALU.subtract, ALU.mult),
                waits=[t8] + G.use("lnx2"))
            G.consumed("x2t", [t9, st_x2, t1, t2])
            tpd = None
            for c in range(DC):
                tp = p.emit("pe", lambda e, c=c: e.transpose(
                    ps_t[:], lnx2[:, c * 128:(c + 1) * 128], idn[:]),
                    waits=[t9, t_static] + G.use("ps_t"))
                tpd = p.emit("dve", lambda e, c=c, tt=tt: e.tensor_copy(
                    lnx2T[:, c * TLOC + tt * 128: c * TLOC + (tt + 1) * 128],
                    ps_t[:]), waits=[tp])
                G.consumed("ps_t", [tpd])
            G.consumed("lnx2", [tpd])
            t_lnx2T = tpd
        p.dma("sd", lnx2T_o.rearrange("c p t -> p c t"),
              lnx2T.rearrange("p (c t) -> p c t", c=DC),
              waits=[t_lnx2T], sem="slnxT")

        bar4 = p.barrier()
        A.free("out2")
        A.free("xres")
        A.free("lnscr")
        A.free("x2t")
        A.free("lnx2")
        A.free("b2sb")
        wbig = A.alloc("wbig", BF16, DC * D)
        for c in range(DC):
            p.dma("gd", wbig[:, c * D:(c + 1) * D],
                  wkT_i[c * 128:(c + 1) * 128, :], sem="wkl", waits=bar4)
        t_wk = p.full("wkl")
        ktile = A.alloc("ktile", F32, D)
        knb = A.alloc("knb", BF16, D)
        kbT = A.alloc("kbT", BF16, H * TLOC)
        hsum = stats[:, 16:32]
        t_kbT = None
        for tt in range(NT):
            evs = []
            for ncol in range(4):
                spar = ncol % 2
                scr = ps_scr[spar]
                pre = G.use(f"scr{spar}")
                mm = None
                for c in range(DC):
                    mm = p.emit("pe", lambda e, tt=tt, c=c, ncol=ncol, scr=scr:
                                e.matmul(
                        scr[:, 0:512],
                        lnx2T[:, c * TLOC + tt * 128: c * TLOC + (tt + 1) * 128],
                        wbig[:, c * D + ncol * 512: c * D + (ncol + 1) * 512],
                        start=(c == 0), stop=(c == DC - 1)),
                        waits=[t_lnx2T, t_wk] + pre)
                ev = p.emit("act", lambda e, ncol=ncol, scr=scr: e.activation(
                    ktile[:, ncol * 512:(ncol + 1) * 512], scr[:, 0:512],
                    AF.Copy), waits=[mm] + G.use("ktile")[0:0])
                G.consumed(f"scr{spar}", [ev])
                evs.append(ev)
            sq_toks = []
            for h in range(H):
                sq_toks.append(p.emit("act", lambda e, h=h: e.activation(
                    lnscr[:, 0:128], ktile[:, h * 128:(h + 1) * 128],
                    AF.Square, accum_out=hsum[:, h:h + 1]),
                    waits=evs + G.use("hsum")))
            rs1 = p.emit("act", lambda e: e.activation(
                stats[:, 0:16] if False else hsum, hsum, AF.Ln,
                bias=csts[:, 0:1], scale=1.0 / 128), waits=sq_toks)
            rs2 = p.emit("act", lambda e: e.activation(
                hsum, hsum, AF.Exp, scale=-0.5), waits=[rs1])
            nm_toks = []
            for h in range(H):
                nm_toks.append(p.emit("dve", lambda e, h=h: e.tensor_scalar(
                    knb[:, h * 128:(h + 1) * 128],
                    ktile[:, h * 128:(h + 1) * 128], hsum[:, h:h + 1], None,
                    ALU.mult), waits=[rs2] + G.use("knb")))
            G.consumed("hsum", nm_toks)
            G.consumed("ktile", nm_toks)
            for c in range(H):
                tp = p.emit("pe", lambda e, c=c: e.transpose(
                    ps_t[:], knb[:, c * 128:(c + 1) * 128], idn[:]),
                    waits=[nm_toks[-1], t_static] + G.use("ps_t"))
                t_kbT = p.emit("dve", lambda e, c=c, tt=tt: e.tensor_copy(
                    kbT[:, c * TLOC + tt * 128: c * TLOC + (tt + 1) * 128],
                    ps_t[:]), waits=[tp])
                G.consumed("ps_t", [t_kbT])
            G.consumed("knb", [t_kbT])
        p.dma("sd", kBT_o.rearrange("h p t -> p h t"),
              kbT.rearrange("p (h t) -> p h t", h=H), waits=[t_kbT], sem="skbt")

        for c in range(DC):
            p.dma("gd", wbig[:, c * D:(c + 1) * D],
                  wvT_i[c * 128:(c + 1) * 128, :],
                  waits=[t_kbT] if c < 2 else (), sem="wvl")
        t_wv = p.full("wvl")
        vtile = A.alloc("vtile", BF16, D)
        st_vb = None
        for tt in range(NT):
            evs = []
            for ncol in range(4):
                spar = ncol % 2
                scr = ps_scr[spar]
                pre = G.use(f"scr{spar}")
                mm = None
                for c in range(DC):
                    mm = p.emit("pe", lambda e, tt=tt, c=c, ncol=ncol, scr=scr:
                                e.matmul(
                        scr[:, 0:512],
                        lnx2T[:, c * TLOC + tt * 128: c * TLOC + (tt + 1) * 128],
                        wbig[:, c * D + ncol * 512: c * D + (ncol + 1) * 512],
                        start=(c == 0), stop=(c == DC - 1)),
                        waits=[t_lnx2T, t_wv] + pre)
                ev = p.emit("act", lambda e, ncol=ncol, scr=scr: e.activation(
                    vtile[:, ncol * 512:(ncol + 1) * 512], scr[:, 0:512],
                    AF.Copy), waits=[mm] + G.use("vtile")[0:0])
                G.consumed(f"scr{spar}", [ev])
                evs.append(ev)
            st_vb = p.dma("ad", vB_o[tt * 128:(tt + 1) * 128, :], vtile,
                          waits=evs + G.use("vtile"), sem="svb")
            G.consumed("vtile", [st_vb])

        p.emit("pool", lambda e: e.nop(),
               waits=[p.full(x) for x in
                      ("sx2", "slnxT", "skbt", "svb", "wqs0")])

        p.build(block)
    return nc




def build_L2():
    nc = bass.Bass("TRN2", target_bir_lowering=False, debug=False)
    ap = lambda name, shape, dt, kind: nc.dram_tensor(name, shape, dt, kind=kind).ap()

    x2_i = ap("x2", [TLOC, D], F32, "ExternalInput")
    lnx2T_i = ap("lnx2T", [DC, 128, TLOC], BF16, "ExternalInput")
    kT_i = ap("kT", [H, 128, S], BF16, "ExternalInput")
    v_i = ap("v", [S, H * 129], BF16, "ExternalInput")
    mask_i = ap("mask", [128, 1024], BF16, "ExternalInput")
    iden = ap("iden", [128, 128], BF16, "ExternalInput")
    consts = ap("consts", [128, 4], F32, "ExternalInput")
    wqT_i = ap("WqT", [D, D], F32, "ExternalInput")
    woT_i = ap("WoT", [D, D], F32, "ExternalInput")
    gq_i = ap("gq", [128, 1], F32, "ExternalInput")
    gk_i = ap("gk", [128, 1], F32, "ExternalInput")
    w1q_i = ap("w1q", [64, 128, 16, 128], BF16, "ExternalInput")
    w2q_i = ap("w2q", [64, 128, D], BF16, "ExternalInput")
    b1r_i = ap("b1r", [128, 64], F32, "ExternalInput")
    b2r_i = ap("b2r", [1, D], F32, "ExternalInput")

    xout_o = ap("xout", [TLOC, D], F32, "ExternalOutput")

    x3_s = nc.dram_tensor("x3_s", [TLOC, D], F32, kind="Internal").ap()

    p = Prog(nc)
    G = Guard()

    with (
        nc.sbuf_tensor("bfA", [128, 105600], BF16) as bfA,
        nc.sbuf_tensor("stats", [128, 36], F32) as stats,
        nc.sbuf_tensor("idn", [128, 128], BF16) as idn,
        nc.sbuf_tensor("csts", [128, 4], F32) as csts,
        nc.sbuf_tensor("onescol", [128, 1], BF16) as onescol,
        nc.sbuf_tensor("onesrow1", [1, 128], BF16) as onesrow1,
        nc.sbuf_tensor("g2col", [128, 1], F32) as g2col,
        nc.sbuf_tensor("callb", [128, H], F32) as callb,
        nc.sbuf_tensor("brow", [1, 160], F32) as brow,
        nc.psum_tensor("ps_s0", [128, 1024], F32) as ps_s0,
        nc.psum_tensor("ps_s1", [128, 1024], F32) as ps_s1,
        nc.psum_tensor("ps_a", [128, 512], F32) as ps_a,
        nc.psum_tensor("ps_b", [128, 512], F32) as ps_b,
        nc.psum_tensor("ps_t", [128, 128], BF16) as ps_t,
        nc.psum_tensor("ps_n", [1, 512], F32) as ps_n,
        nc.Block() as block,
    ):
        A = Arena(bfA)
        ps_scr = [ps_s0, ps_s1]
        ps_av = [ps_a, ps_b]

        p.dma("sd", idn[:], iden, sem="s0")
        p.dma("sd", csts[:], consts, sem="s0")
        mask_sb = A.alloc("mask", BF16, 1024)
        p.dma("sd", mask_sb, mask_i, sem="s0")
        p.dma("sd", stats[:, 24:25], gq_i, sem="s0")
        p.dma("sd", stats[:, 25:26], gk_i, sem="s0")
        t_static = p.full("s0")
        t_ones = p.emit("dve", lambda e: e.memset(onescol[:], 1.0))
        t_ones1 = p.emit("dve", lambda e: e.memset(onesrow1[:], 1.0))
        onesrowF = brow[0:1, 32:160]
        onescolF = stats[:, 32:33]
        t_onesF = p.emit("dve", lambda e: e.memset(onesrowF, 1.0))
        t_g2 = p.emit("dve", lambda e: e.tensor_mul(
            g2col[:], stats[:, 24:25], stats[:, 25:26]), waits=[t_static])

        qT = A.alloc("qT", BF16, H * TLOC)
        qn_ps = A.alloc("qn_ps", F32, 16)
        lnx2T = A.alloc("lnx2T", BF16, DC * TLOC)
        p.dma("ad", lnx2T.rearrange("p (c t) -> p c t", c=DC),
              lnx2T_i.rearrange("c p t -> p c t"), sem="lx")
        t_lnx2T = p.full("lx")
        wbig = A.alloc("wbig", BF16, DC * D)
        for c in range(DC):
            p.dma("gd", wbig[:, c * D:(c + 1) * D],
                  wqT_i[c * 128:(c + 1) * 128, :], sem="wql")
        t_wq = p.full("wql")
        qtile = A.alloc("qtile", F32, D)
        qnb = A.alloc("qnb", BF16, D)
        hsum = stats[:, 16:32][:, 0:16] if False else stats[:, 0:16]
        sq = A.alloc("sq", BF16, TLOC)
        t_qT = None
        qmax_toks = []
        for tt in range(NT):
            evs = []
            for ncol in range(4):
                spar = ncol % 2
                scr = ps_scr[spar]
                pre = G.use(f"scr{spar}")
                mm = None
                for c in range(DC):
                    mm = p.emit("pe", lambda e, tt=tt, c=c, ncol=ncol, scr=scr:
                                e.matmul(
                        scr[:, 0:512],
                        lnx2T[:, c * TLOC + tt * 128: c * TLOC + (tt + 1) * 128],
                        wbig[:, c * D + ncol * 512: c * D + (ncol + 1) * 512],
                        start=(c == 0), stop=(c == DC - 1)),
                        waits=[t_lnx2T, t_wq] + pre)
                ev = p.emit("act", lambda e, ncol=ncol, scr=scr: e.activation(
                    qtile[:, ncol * 512:(ncol + 1) * 512], scr[:, 0:512],
                    AF.Copy), waits=[mm] + G.use("qtile")[0:0])
                G.consumed(f"scr{spar}", [ev])
                evs.append(ev)
            sq_toks = []
            for h in range(H):
                sq_toks.append(p.emit("act", lambda e, h=h: e.activation(
                    qnb[:, 0:128], qtile[:, h * 128:(h + 1) * 128],
                    AF.Square, accum_out=hsum[:, h:h + 1]),
                    waits=evs + G.use("hsum") + G.use("qnb")))
            rs1 = p.emit("act", lambda e: e.activation(
                hsum, hsum, AF.Ln, bias=csts[:, 0:1], scale=1.0 / 128),
                waits=sq_toks + [t_static])
            rs2 = p.emit("act", lambda e: e.activation(
                hsum, hsum, AF.Exp, scale=-0.5), waits=[rs1])
            nm_toks = []
            for h in range(H):
                nm_toks.append(p.emit("dve", lambda e, h=h: e.tensor_scalar(
                    qnb[:, h * 128:(h + 1) * 128],
                    qtile[:, h * 128:(h + 1) * 128], hsum[:, h:h + 1], None,
                    ALU.mult), waits=[rs2] + G.use("qnb")))
            G.consumed("hsum", nm_toks)
            G.consumed("qtile", nm_toks)
            G.consumed("qnb", nm_toks)
            for h in range(H):
                tp = p.emit("pe", lambda e, h=h: e.transpose(
                    ps_t[:], qnb[:, h * 128:(h + 1) * 128], idn[:]),
                    waits=[nm_toks[h], t_static] + G.use("ps_t"))
                t_qT = p.emit("dve", lambda e, h=h, tt=tt: e.tensor_scalar(
                    qT[:, h * TLOC + tt * 128: h * TLOC + (tt + 1) * 128],
                    ps_t[:], g2col[:], None, ALU.mult), waits=[tp, t_g2])
                G.consumed("ps_t", [t_qT])
            G.consumed("qnb", [t_qT])
        qn_row = A.get("qn_ps", 1)
        for h in range(H):
            sqa = p.emit("dve", lambda e, h=h: e.tensor_mul(
                sq, qT[:, h * TLOC:(h + 1) * TLOC],
                qT[:, h * TLOC:(h + 1) * TLOC]), waits=[t_qT] + G.use("sqg"))
            reds = []
            mmn = None
            for ns in range(2):
                pren = G.use("ps_n")
                mmn = p.emit("pe", lambda e, ns=ns: e.matmul(
                    ps_n[0:1, :], onescol[:],
                    sq[:, ns * 512:(ns + 1) * 512], start=True, stop=True),
                    waits=[sqa, t_ones] + pren)
                red = p.emit("dve", lambda e, ns=ns: e.reduce_max(
                    stats[0:1, 20 + ns:21 + ns], ps_n[0:1, :], axis=AX.X),
                    waits=[mmn])
                G.consumed("ps_n", [red])
                reds.append(red)
            qmax_toks.append(p.emit("dve", lambda e, h=h: e.tensor_max(
                qn_row[0:1, h:h + 1], stats[0:1, 20:21], stats[0:1, 21:22]),
                waits=reds))
            G.consumed("sqg", [mmn])
        b1_ = p.emit("act", lambda e: e.activation(
            brow[0:1, 0:16], qn_row[0:1, 0:16], AF.Sqrt), waits=qmax_toks)
        b3_ = p.emit("dve", lambda e: e.tensor_scalar(
            brow[0:1, 0:16], brow[0:1, 0:16],
            -SCALE_B * 1.02 * (128.0 ** 0.5), SHIFT, ALU.mult, ALU.add),
            waits=[b1_])
        bm_ = p.emit("pe", lambda e: e.matmul(
            ps_a[:, 0:16], onesrowF, brow[0:1, 0:16], start=True, stop=True),
            waits=[b3_, t_onesF] + G.use("av0"))
        t_call = p.emit("dve", lambda e: e.tensor_copy(callb[:], ps_a[:, 0:16]),
                        waits=[bm_])
        G.consumed("av0", [t_call])

        bar_q = p.barrier()
        A.free("wbig")
        A.free("qtile")
        A.free("qnb")
        A.free("sq")
        A.free("lnx2T")
        kbuf = [A.alloc(f"kb{i}", BF16, H * 512) for i in range(2)]
        vbuf = [A.alloc(f"vb{i}", BF16, H * 129) for i in range(4)]
        ptil = [A.alloc(f"pt{i}", BF16, 1024) for i in range(4)]
        acc = A.alloc("acc", F32, NT * H * 129)
        t_accz = p.emit("dve", lambda e: e.memset(acc, 0.0), waits=bar_q)

        merge_toks = {}
        vload = [None] * 4

        def scores_one(kb, h, kt, kpar, tkl, jmin, Nq):
            spar = kt % 2
            scr = ps_scr[spar]
            pre = G.use(f"scr{spar}")
            mm = None
            for n0 in range(0, Nq, 512):
                n1 = min(Nq, n0 + 512)
                mm = p.emit("pe", lambda e, n0=n0, n1=n1: e.matmul(
                    scr[:, n0:n1],
                    kbuf[kpar][:, h * 512 + kt * 128: h * 512 + (kt + 1) * 128],
                    qT[:, h * TLOC + 128 * jmin + n0:
                       h * TLOC + 128 * jmin + n1],
                    start=True, stop=False, skip_group_check=True),
                    waits=[tkl, t_qT] + pre)
            mm = p.emit("pe", lambda e: e.matmul(
                scr[:, 0:128],
                mask_sb[:, (kb % 2) * 512 + kt * 128:
                        (kb % 2) * 512 + (kt + 1) * 128],
                idn[:], start=False, stop=True, skip_group_check=True),
                waits=[t_static])
            ex = p.emit("act", lambda e: e.activation(
                ptil[kt][:, 0:Nq], scr[:, 0:Nq], AF.Exp,
                bias=callb[:, h:h + 1], scale=SCALE_B),
                waits=[mm, t_call] + G.use(f"pt{kt}"))
            G.consumed(f"scr{spar}", [ex])
            return ex

        def av_one(j, h, jmin, exp_toks):
            apar = j % 2
            pav = ps_av[apar]
            prea = G.use(f"av{apar}")
            mm = None
            for kt in range(4):
                mm = p.emit("pe", lambda e, kt=kt: e.matmul(
                    pav[:, 0:129],
                    ptil[kt][:, (j - jmin) * 128:(j - jmin + 1) * 128],
                    vbuf[kt][:, h * 129:(h + 1) * 129],
                    start=(kt == 0), stop=(kt == 3)),
                    waits=[exp_toks[kt], vload[kt]] + prea)
            mg = p.emit("dve", lambda e: e.tensor_add(
                acc[:, (j * H + h) * 129:(j * H + h + 1) * 129],
                acc[:, (j * H + h) * 129:(j * H + h + 1) * 129],
                pav[:, 0:129]),
                waits=[mm, t_accz, merge_toks.get((j, h))])
            G.consumed(f"av{apar}", [mg])
            merge_toks[(j, h)] = mg
            return mm

        for kb in range(16):
            jmin = kb // 2
            Nq = TLOC - 128 * jmin
            kpar = kb % 2
            tkl = p.dma("sd", kbuf[kpar].rearrange("p (h n) -> p h n", h=H),
                        kT_i.rearrange("h p n -> p h n")
                        [:, :, 512 * kb:512 * (kb + 1)],
                        waits=G.use(f"kb{kpar}") + bar_q, sem=f"kl{kpar}")
            for kt in range(4):
                vload[kt] = p.dma(
                    "ad", vbuf[kt],
                    v_i.rearrange("(a p) d -> p a d", p=128)[:, 4 * kb + kt, :],
                    waits=G.use(f"vb{kt}") + bar_q, sem=f"vl{kt}")
            last_mm = None
            for h in range(H):
                exp_toks = [scores_one(kb, h, kt, kpar, tkl, jmin, Nq)
                            for kt in range(4)]
                for j in range(jmin, NT):
                    last_mm = av_one(j, h, jmin, exp_toks)
                for kt in range(4):
                    G.consumed(f"pt{kt}", [last_mm])
            for kt in range(4):
                G.consumed(f"vb{kt}", [last_mm])
            G.consumed(f"kb{kpar}", [last_mm])

        bar_f = p.barrier()
        for i in range(2):
            A.free(f"kb{i}")
        for i in range(4):
            A.free(f"vb{i}")
            A.free(f"pt{i}")
        A.free("qT")
        A.free("qn_ps")
        attT = A.alloc("attT", BF16, DC * TLOC)
        att_bf = A.alloc("att_bf", BF16, D)
        linv = stats[:, 8:9]
        t_attT = None
        for tt in range(NT):
            sc_toks = []
            for h in range(H):
                base = (tt * H + h) * 129
                l1 = p.emit("dve", lambda e, base=base: e.tensor_scalar_max(
                    stats[:, 7:8], acc[:, base + 128:base + 129], 1e-30),
                    waits=[merge_toks[(tt, h)]] + G.use("linv") + bar_f)
                l2 = p.emit("dve", lambda e: e.reciprocal(linv, stats[:, 7:8]),
                            waits=[l1])
                sc = p.emit("dve", lambda e, base=base, h=h: e.tensor_scalar(
                    att_bf[:, h * 128:(h + 1) * 128], acc[:, base:base + 128],
                    linv, None, ALU.mult), waits=[l2] + G.use("attbf"))
                G.consumed("linv", [sc])
                sc_toks.append(sc)
            for c in range(DC):
                tp = p.emit("pe", lambda e, c=c: e.transpose(
                    ps_t[:], att_bf[:, c * 128:(c + 1) * 128], idn[:]),
                    waits=[sc_toks[-1], t_static] + G.use("ps_t"))
                t_attT = p.emit("dve", lambda e, c=c, tt=tt: e.tensor_copy(
                    attT[:, c * TLOC + tt * 128: c * TLOC + (tt + 1) * 128],
                    ps_t[:]), waits=[tp])
                G.consumed("ps_t", [t_attT])
            G.consumed("attbf", [t_attT])

        bar_w = p.barrier()
        A.free("acc")
        wo_res = A.alloc("wo_res", BF16, DC * D)
        for c in range(DC):
            p.dma("gd", wo_res[:, c * D:(c + 1) * D],
                  woT_i[c * 128:(c + 1) * 128, :], sem="wol", waits=bar_w)
        t_wo = p.full("wol")
        xres = A.alloc("xres", F32, D)
        lnscr = A.alloc("lnscr", F32, D)
        lnx3 = A.alloc("lnx3", BF16, D)
        lnx3T = A.alloc("lnx3T", BF16, DC * TLOC)
        x1t = A.alloc("x1t", F32, D)

        t_lnx3T = None
        for tt in range(NT):
            xl = p.dma("sd", xres, x2_i[tt * 128:(tt + 1) * 128, :],
                       waits=G.use("xres") + bar_w, sem="xl")
            evs = []
            for ncol in range(4):
                apar = ncol % 2
                pav = ps_av[apar]
                prea = G.use(f"av{apar}")
                mm = None
                for c in range(DC):
                    mm = p.emit("pe", lambda e, tt=tt, c=c, ncol=ncol, pav=pav:
                                e.matmul(
                        pav[:, 0:512],
                        attT[:, c * TLOC + tt * 128: c * TLOC + (tt + 1) * 128],
                        wo_res[:, c * D + ncol * 512: c * D + (ncol + 1) * 512],
                        start=(c == 0), stop=(c == DC - 1)),
                        waits=[t_attT, t_wo] + prea)
                ev = p.emit("dve", lambda e, ncol=ncol, pav=pav:
                            e.scalar_tensor_tensor(
                    x1t[:, ncol * 512:(ncol + 1) * 512], pav[:, 0:512], 1.0,
                    xres[:, ncol * 512:(ncol + 1) * 512], ALU.mult, ALU.add),
                    waits=[mm, xl] + (G.use("x1t") if ncol == 0 else []))
                G.consumed(f"av{apar}", [ev])
                evs.append(ev)
            G.consumed("xres", evs)
            st_x1 = p.dma("sd", x3_s[tt * 128:(tt + 1) * 128, :], x1t,
                          waits=evs, sem="sx1")
            sc = stats[:, 0:5]
            t1 = p.emit("act", lambda e: e.activation(
                lnscr, x1t, AF.Square, accum_out=sc[:, 0:1]), waits=evs)
            t2 = p.emit("dve", lambda e: e.reduce_sum(
                sc[:, 1:2], x1t, axis=AX.X), waits=evs)
            t3 = p.emit("dve", lambda e: e.tensor_scalar_mul(
                sc[:, 1:2], sc[:, 1:2], 1.0 / D), waits=[t2])
            t4 = p.emit("dve", lambda e: e.scalar_tensor_tensor(
                sc[:, 2:3], sc[:, 1:2], 1.0, sc[:, 1:2], ALU.mult, ALU.mult),
                waits=[t3])
            t5 = p.emit("dve", lambda e: e.tensor_scalar(
                sc[:, 0:1], sc[:, 0:1], 1.0 / D, None, ALU.mult), waits=[t1])
            t6 = p.emit("dve", lambda e: e.tensor_sub(
                sc[:, 2:3], sc[:, 0:1], sc[:, 2:3]), waits=[t4, t5])
            t7 = p.emit("act", lambda e: e.activation(
                sc[:, 3:4], sc[:, 2:3], AF.Ln, bias=csts[:, 0:1]),
                waits=[t6, t_static])
            t8 = p.emit("act", lambda e: e.activation(
                sc[:, 4:5], sc[:, 3:4], AF.Exp, scale=-0.5), waits=[t7])
            t9 = p.emit("dve", lambda e: e.tensor_scalar(
                lnx3, x1t, sc[:, 1:2], sc[:, 4:5], ALU.subtract, ALU.mult),
                waits=[t8] + G.use("lnx3"))
            G.consumed("x1t", [t9, st_x1, t1, t2])
            tpd = None
            for c in range(DC):
                tp = p.emit("pe", lambda e, c=c: e.transpose(
                    ps_t[:], lnx3[:, c * 128:(c + 1) * 128], idn[:]),
                    waits=[t9, t_static] + G.use("ps_t"))
                tpd = p.emit("dve", lambda e, c=c, tt=tt: e.tensor_copy(
                    lnx3T[:, c * TLOC + tt * 128: c * TLOC + (tt + 1) * 128],
                    ps_t[:]), waits=[tp])
                G.consumed("ps_t", [tpd])
            G.consumed("lnx3", [tpd])
            t_lnx3T = tpd

        bar2 = p.barrier()
        A.free("attT")
        A.free("att_bf")
        A.free("wo_res")
        A.free("mask")
        A.free("x1t")
        out2 = A.alloc("out2", F32, NT * D)
        hTb = A.alloc("hTb", BF16, 8 * TLOC)
        w2fb = A.alloc("w2fb", BF16, 8 * D)
        b1sb = A.alloc("b1sb", F32, 64)
        b1c = A.get("b1sb")
        p.dma("sd", b1c, b1r_i, sem="b1", waits=bar2)
        t_b1 = p.full("b1")
        b2sb = A.alloc("b2sb", BF16, D)
        b2c = A.get("b2sb", 1)
        p.dma("gd", b2c, b2r_i, sem="b2", waits=bar2)
        t_b2 = p.full("b2")
        w1qb = [A.alloc(f"w1qb{i}", BF16, 2048) for i in range(2)]
        t_o2z = p.emit("dve", lambda e: e.memset(out2, 0.0), waits=bar2)

        o2_toks = {}

        def ffnA_one(fb, fl):
            fc = fb * 8 + fl
            par = fc % 2
            tl = p.dma("sd", w1qb[par].rearrange("p (b n) -> p b n", b=16),
                       w1q_i[fc],
                       waits=G.use(f"w1qb{par}") + bar2, sem=f"w1l{par}")
            spar = fl % 2
            scr = ps_scr[spar]
            pre = G.use(f"scr{spar}")
            mm = None
            for dc2 in range(DC):
                for ns in range(2):
                    mm = p.emit("pe", lambda e, dc2=dc2, ns=ns: e.matmul(
                        scr[:, ns * 512:(ns + 1) * 512],
                        w1qb[par][:, dc2 * 128:(dc2 + 1) * 128],
                        lnx3T[:, dc2 * TLOC + ns * 512:
                              dc2 * TLOC + (ns + 1) * 512],
                        start=(dc2 == 0), stop=(dc2 == DC - 1)),
                        waits=[tl, t_lnx3T] + pre)
            G.consumed(f"w1qb{par}", [mm])
            ge = p.emit("act", lambda e: e.activation(
                hTb[:, fl * TLOC:(fl + 1) * TLOC], scr[:], AF.Gelu,
                bias=b1c[:, fc:fc + 1]), waits=[mm, t_b1] + G.use(f"hTb{fl}"))
            G.consumed(f"scr{spar}", [ge])
            return ge

        def ffnB_one(fb, tt, ncol, ge_toks, w2l_toks):
            apar = (tt * 4 + ncol) % 2
            pav = ps_av[apar]
            prea = G.use(f"av{apar}")
            mm = None
            for fl in range(8):
                mm = p.emit("pe", lambda e, fl=fl: e.matmul(
                    pav[:, 0:512],
                    hTb[:, fl * TLOC + tt * 128: fl * TLOC + (tt + 1) * 128],
                    w2fb[:, fl * D + ncol * 512: fl * D + (ncol + 1) * 512],
                    start=(fl == 0), stop=False, skip_group_check=True),
                    waits=[ge_toks[fl], w2l_toks[fl]] + prea)
            if fb == 0:
                mm = p.emit("pe", lambda e: e.matmul(
                    pav[:, 0:512], onesrow1[:],
                    b2c[:, ncol * 512:(ncol + 1) * 512],
                    start=False, stop=True, skip_group_check=True),
                    waits=[t_b2, t_ones1])
            mg = p.emit("dve", lambda e: e.tensor_add(
                out2[:, tt * D + ncol * 512: tt * D + (ncol + 1) * 512],
                out2[:, tt * D + ncol * 512: tt * D + (ncol + 1) * 512],
                pav[:, 0:512]),
                waits=[mm, t_o2z, o2_toks.get((tt, ncol))])
            G.consumed(f"av{apar}", [mg])
            o2_toks[(tt, ncol)] = mg
            return mm

        for fb in range(8):
            ge_toks = [ffnA_one(fb, fl) for fl in range(8)]
            w2l_toks = []
            for fl in range(8):
                fc = fb * 8 + fl
                tl = p.dma("gd", w2fb[:, fl * D:(fl + 1) * D], w2q_i[fc],
                           waits=G.use(f"w2fb{fl}") + bar2, sem=f"w2l{fl}")
                w2l_toks.append(tl)
            last = None
            for tt in range(NT):
                for ncol in range(4):
                    last = ffnB_one(fb, tt, ncol, ge_toks, w2l_toks)
            for fl in range(8):
                G.consumed(f"hTb{fl}", [last])
                G.consumed(f"w2fb{fl}", [last])

        st_xo = None
        for tt in range(NT):
            xl = p.dma("sd", xres, x3_s[tt * 128:(tt + 1) * 128, :],
                       waits=G.use("xres") + [p.full("sx1")], sem="xl2")
            ev = p.emit("dve", lambda e, tt=tt: e.tensor_add(
                lnscr, out2[:, tt * D:(tt + 1) * D], xres),
                waits=[xl, o2_toks[(tt, 0)], o2_toks[(tt, 1)],
                       o2_toks[(tt, 2)], o2_toks[(tt, 3)]] + G.use("lnscr"))
            G.consumed("xres", [ev])
            st_xo = p.dma("sd", xout_o[tt * 128:(tt + 1) * 128, :], lnscr,
                          waits=[ev], sem="sxo")
            G.consumed("lnscr", [st_xo])

        p.emit("pool", lambda e: e.nop(), waits=[p.full("sxo")])

        p.build(block)
    return nc


def _rope_tables():
    inv = (1.0 / (10000.0 ** (np.arange(0, ROPE, 2, dtype=np.float64) / ROPE))
           ).astype(np.float32)
    t = np.arange(S, dtype=np.float64)
    freqs = (t[:, None] * inv[None, :].astype(np.float64)).astype(np.float32)
    return (np.cos(freqs.astype(np.float64)).astype(np.float32),
            np.sin(freqs.astype(np.float64)).astype(np.float32))


def _prep_L0(inputs):
    x = np.asarray(inputs["L0_attn_a_x"], np.float32).reshape(S, D)
    cos, sin = _rope_tables()
    iden = np.eye(128, dtype=ml_dtypes.bfloat16)
    g_q = np.ascontiguousarray(
        np.asarray(inputs["L0_attn_a_g_q"], np.float32).reshape(QL // 128, 128).T)
    g_kv = np.ascontiguousarray(
        np.asarray(inputs["L0_attn_a_g_kv"], np.float32).reshape(KVL // 128, 128).T)
    shared = {
        "w_dq": np.asarray(inputs["L0_attn_a_w_dq"], np.float32),
        "w_uq": np.asarray(inputs["L0_attn_a_w_uq"], np.float32),
        "g_q": g_q,
        "w_dkv": np.asarray(inputs["L0_attn_a_w_dkv"], np.float32),
        "g_kv": g_kv,
        "w_uk": np.asarray(inputs["L0_attn_a_w_uk"], np.float32),
        "w_uv": np.asarray(inputs["L0_attn_a_w_uv"], np.float32),
        "w_kr": np.asarray(inputs["L0_attn_a_w_kr"], np.float32),
        "iden": iden,
        "consts": np.tile(np.array([[1e-5, 1e-6, 0.0, 0.0]], np.float32),
                          (128, 1)),
    }
    maps = []
    for c in range(NCORES):
        m = dict(shared)
        m["x"] = np.ascontiguousarray(x[c::NCORES])
        m["cossin"] = np.ascontiguousarray(
            np.concatenate([cos[c::NCORES].T, sin[c::NCORES].T], axis=0))
        maps.append(m)
    return maps




def _prep_L1(inputs, res0):
    H2, T = H // 2, TLOC
    bf = ml_dtypes.bfloat16
    kT_full = np.empty((H, 128, S), bf)
    v_full = np.ones((S, H, 129), bf)
    kpe_full = np.empty((64, S), bf)
    knorm_full = np.empty((H, S), np.float32)
    for c in range(NCORES):
        r = res0[c]
        kT_full[:, :, c::NCORES] = np.asarray(r["kAT"])
        v_full[c::NCORES, :, 0:128] = np.asarray(r["vA"]).reshape(TLOC, H, 128)
        kpe_full[:, c::NCORES] = np.asarray(r["kpeT"])
        knorm_full[:, c::NCORES] = np.asarray(r["knormA"])
    x = np.asarray(inputs["L0_attn_a_x"], np.float32).reshape(S, D)
    w1 = np.asarray(inputs["L0_ffn_w1"], np.float32)
    w2 = np.asarray(inputs["L0_ffn_w2"], np.float32)
    w1T = np.ascontiguousarray(w1.T)
    w1t = np.ascontiguousarray(
        w1T.reshape(16, 128, 64, 128).transpose(2, 1, 0, 3))
    w2t = np.ascontiguousarray(w2.T.reshape(64, 128, D))
    b1 = np.asarray(inputs["L0_ffn_b1"], np.float32)
    b1r = np.ascontiguousarray(b1.reshape(64, 128).T)
    b2r = np.asarray(inputs["L0_ffn_b2"], np.float32).reshape(1, D)
    shared = {
        "kT": kT_full, "v": v_full.reshape(S, H * 129), "kpeT": kpe_full, "knorm": knorm_full,
        "iden": np.eye(128, dtype=bf),
        "consts": np.tile(np.array([[1e-5, 1e-6, 0.0, 0.0]], np.float32),
                          (128, 1)),
        "w_o": np.asarray(inputs["L0_attn_a_w_o"], np.float32),
        "w1t": w1t, "w2t": w2t, "b1r": b1r, "b2r": b2r,
        "WkT": np.ascontiguousarray(
            np.asarray(inputs["L0_attn_b_Wk"], np.float32).T),
        "WvT": np.ascontiguousarray(
            np.asarray(inputs["L0_attn_b_Wv"], np.float32).T),
    }
    maps = []
    for c in range(NCORES):
        r = res0[c]
        qpe = np.asarray(r["qpeT"])
        qp64 = np.empty((64, H * T), bf)
        for h in range(H):
            qp64[:, h * T:(h + 1) * T] = qpe[h // 2, 64 * (h % 2):
                                             64 * (h % 2) + 64, :]
        rho = np.arange(128)[:, None]
        kap = np.arange(1024)[None, :]
        mask = np.where(kap <= 8 * rho + c, 0.0, -1e30).astype(bf)
        m = dict(shared)
        m["x"] = np.ascontiguousarray(x[c::NCORES])
        m["qnT"] = np.asarray(r["qnopeT"])
        m["qpT64"] = qp64
        m["qmax"] = np.asarray(r["qmaxA"])
        m["mask"] = mask
        maps.append(m)
    return maps




def _prep_L2(inputs, res1):
    bf = ml_dtypes.bfloat16
    kT_full = np.empty((H, 128, S), bf)
    v_full = np.ones((S, H, 129), bf)
    for c in range(NCORES):
        r = res1[c]
        kT_full[:, :, c::NCORES] = np.asarray(r["kBT"])
        v_full[c::NCORES, :, 0:128] = np.asarray(r["vB"]).reshape(TLOC, H, 128)
    shared = {
        "kT": kT_full, "v": v_full.reshape(S, H * 129),
        "iden": np.eye(128, dtype=bf),
        "consts": np.tile(np.array([[1e-5, 1e-6, 0.0, 0.0]], np.float32),
                          (128, 1)),
        "WqT": np.ascontiguousarray(
            np.asarray(inputs["L0_attn_b_Wq"], np.float32).T),
        "WoT": np.ascontiguousarray(
            np.asarray(inputs["L0_attn_b_Wo"], np.float32).T),
        "gq": np.asarray(inputs["L0_attn_b_gq"], np.float32).reshape(128, 1),
        "gk": np.asarray(inputs["L0_attn_b_gk"], np.float32).reshape(128, 1),
        "w1q": np.asarray(res1[0]["w1q"]),
        "w2q": np.asarray(res1[0]["w2q"]),
        "b1r": np.ascontiguousarray(
            np.asarray(inputs["L0_ffn_b1"], np.float32).reshape(64, 128).T),
        "b2r": np.asarray(inputs["L0_ffn_b2"], np.float32).reshape(1, D),
    }
    maps = []
    for c in range(NCORES):
        r = res1[c]
        rho = np.arange(128)[:, None]
        kap = np.arange(1024)[None, :]
        mask = np.where(kap <= 8 * rho + c, 0.0, -1e30).astype(bf)
        m = dict(shared)
        m["x2"] = np.asarray(r["x2"])
        m["lnx2T"] = np.asarray(r["lnx2T"])
        m["mask"] = mask
        maps.append(m)
    return maps


def kernel(**inputs):
    maps0 = _prep_L0(inputs)
    res0 = run_bass_kernel_spmd(build_L0(), maps0, core_ids=list(range(NCORES)))
    maps1 = _prep_L1(inputs, res0.results)
    res1 = run_bass_kernel_spmd(build_L1(), maps1, core_ids=list(range(NCORES)))
    maps2 = _prep_L2(inputs, res1.results)
    res2 = run_bass_kernel_spmd(build_L2(), maps2, core_ids=list(range(NCORES)))
    out = np.empty((S, D), np.float32)
    for c in range(NCORES):
        out[c::NCORES] = np.asarray(res2.results[c]["xout"], np.float32)
    return out.reshape(1, S, D)
